# version 21
# baseline (speedup 1.0000x reference)
import numpy as np
from contextlib import ExitStack
import concourse.bass as bass
import concourse.mybir as mybir
from concourse.bass_utils import run_bass_kernel_spmd

F32 = mybir.dt.float32
BF16 = mybir.dt.bfloat16
AF = mybir.ActivationFunctionType
ALU = mybir.AluOpType
AX = mybir.AxisListType

NCORES = 8
P = 128
S = 8192
D = 2048
TOK = S // NCORES
LAT = 512
H, HD, PD = 16, 128, 64
NKT = D // P
NMT = LAT // P
NKEY = S // P
NST = S // 512
NQB = TOK // P
NQC = TOK // 512
E_ = 8
IM = 1408
IMT = IM // P
ISH = 5632
ATT_SCALE = float(1.0 / np.sqrt(HD + PD))
EPS = 1e-5

SBUF_LO = 16512
SBUF_HI = 229344
KB = 1024

PH = {"p1": 0, "at": 1, "p3": 2, "p4": 3}


def _dtsize(dt):
    return {F32: 4, BF16: 2}[dt]


class Lay:
    def __init__(self, nc):
        self.nc = nc
        self.recs = []

    def at(self, name, shape, dtype, off, ph0, ph1=None):
        ph1 = ph0 if ph1 is None else ph1
        a, b = PH[ph0], PH[ph1]
        size = int(np.prod(shape[1:])) * _dtsize(dtype)
        off = int(off)
        assert off % 64 == 0, f"{name} misaligned {off}"
        assert off >= SBUF_LO and off + size <= SBUF_HI, f"{name}: {off}..{off + size} out of range"
        for (n2, o2, s2, a2, b2) in self.recs:
            if off < o2 + s2 and o2 < off + size and a <= b2 and a2 <= b:
                raise AssertionError(f"SBUF overlap: {name}[{ph0}-{ph1}] vs {n2}")
        self.recs.append((name, off, size, a, b))
        return self.nc.alloc_sbuf_tensor_at(name, shape, dtype, offset=off)


class Ctr:
    def __init__(self, handle):
        self.h = handle
        self.n = 0

    def inc(self, k=1):
        self.n += k
        return self.n


class K:
    DBG_H = 3

    def __init__(self, debug=0):
        self.debug = debug
        self.nc = bass.Bass("TRN2", target_bir_lowering=False)
        self.ops = {e: [] for e in ("tensor", "vector", "scalar", "gpsimd", "sync")}
        self.bankfree = {}
        self.last_dne = [0] * NQB
        self._dw_u = 0
        self._xnb_pe = [None, None]
        self._krc_dv = [None, None]
        self._pend_epilogue = None

    def t(self, fn): self.ops["tensor"].append(fn)
    def v(self, fn): self.ops["vector"].append(fn)
    def s(self, fn): self.ops["scalar"].append(fn)
    def g(self, fn): self.ops["gpsimd"].append(fn)
    def y(self, fn): self.ops["sync"].append(fn)

    def ctr(self, eng):
        return {"tensor": self.pe, "vector": self.dv, "scalar": self.ac,
                "gpsimd": self.gp}[eng]

    def dma(self, engq, sem, out_ap, in_ap, waits=()):
        pre = sem.n
        t = sem.inc(16)
        def th(e, pre=pre, out_ap=out_ap, in_ap=in_ap, waits=tuple(waits), sem=sem):
            for eng, val in waits:
                e.wait_ge(self.ctr(eng).h, val)
            e.wait_ge(sem.h, pre)
            e.dma_start(out_ap, in_ap).then_inc(sem.h, 16)
        self.ops[engq].append(th)
        return t

    def wait_bf(self, e, f):
        if f is not None:
            e.wait_ge(self.ctr(f[0]).h, f[1])

    def build(self):
        nc = self.nc
        self.declare_io()
        self.alloc()
        with ExitStack() as stack:
            for nm in ("pe", "dv", "ac", "gp", "swa", "swo"):
                setattr(self, nm, Ctr(stack.enter_context(nc.semaphore(nm))))
            for nm in ("sckc", "scks", "suk", "suv", "suq", "sqr", "sgw", "suw"):
                setattr(self, nm, [Ctr(stack.enter_context(nc.semaphore(f"{nm}{i}"))) for i in range(2)])
            self.sx = [Ctr(stack.enter_context(nc.semaphore(f"sx{i}"))) for i in range(5)]
            self.sdw = [Ctr(stack.enter_context(nc.semaphore(f"sdw{i}"))) for i in range(4)]
            self.so = [Ctr(stack.enter_context(nc.semaphore(f"so{i}"))) for i in range(4)]
            block = stack.enter_context(nc.Block())
            self.emit_all()
            for eng in ("tensor", "vector", "scalar", "gpsimd", "sync"):
                lst = self.ops[eng]
                getattr(block, eng)(lambda e, _l=lst: [f(e) for f in _l])
        return nc

    def declare_io(self):
        nc = self.nc
        di = {}
        def inp(name, shape, dtype=F32):
            di[name] = nc.dram_tensor(name, shape, dtype, kind="ExternalInput").ap()
        inp("x_full", [S, D]); inp("x_own", [TOK, D])
        inp("wdkv", [D, LAT]); inp("wdq", [D, LAT]); inp("wkr", [D, PD])
        inp("wuk", [LAT, D]); inp("wuv", [LAT, D]); inp("wuq", [LAT, D])
        inp("wqr", [LAT, H * PD]); inp("wo", [D, D])
        inp("cosk", [PD, S]); inp("sink", [PD, S])
        inp("cosq", [PD, TOK]); inp("sinq", [PD, TOK])
        inp("rt", [PD, PD]); inp("ident", [P, P]); inp("rwsg", [P, NKT * 9])
        for e in range(E_):
            inp(f"gupt{e}", [D, 2 * IM])
            inp(f"downt{e}", [IM, D])
        inp("shgt", [D, ISH]); inp("shupt", [D, ISH]); inp("shdt", [ISH, D])
        self.io = di
        self.out = nc.dram_tensor("out", [TOK, D], F32, kind="ExternalOutput").ap()
        self.dbg = {}
        if self.debug:
            def dout(name, shape, dtype=F32):
                self.dbg[name] = nc.dram_tensor(name, shape, dtype, kind="ExternalOutput").ap()
            dout("d_ckvt", [P, NMT * S], BF16)
            dout("d_krt", [PD + 1, S], BF16)
            dout("d_cqt", [P, NMT * TOK], BF16)
            dout("d_kct", [P, S], BF16)
            dout("d_qct", [P, TOK], BF16)
            dout("d_qrt", [PD + 1, TOK], BF16)
            dout("d_vt", [P, 32 * 130], BF16)
            dout("d_ot", [P, H * TOK], BF16)
            dout("d_accp3", [P, NQB * D])
            dout("d_accp4s", [P, NQB * D])
            dout("d_comb", [P, NQB * 8])
            dout("d_sig", [P, NQB])
            dout("d_xnt", [P, NKT * TOK], BF16)
            dout("d_shact", [P, IMT * TOK], BF16)
            dout("d_bnd", [1, 2])

    def emit_dbg(self, pairs, waits):
        if not self.debug:
            return
        sem = self.so[3]
        for name, src in pairs:
            pre = sem.n
            sem.inc(16)
            def th(sy, name=name, src=src, pre=pre, waits=tuple(waits), sem=sem):
                for eng, val in waits:
                    sy.wait_ge(self.ctr(eng).h, val)
                sy.wait_ge(sem.h, pre)
                sy.dma_start(self.dbg[name], src).then_inc(sem.h, 16)
            self.y(th)
        tso = sem.n
        pre_dv = self.dv.n
        tdv = self.dv.inc()
        self.v(lambda v, w=tso, p=pre_dv: (v.wait_ge(sem.h, w), v.wait_ge(self.dv.h, p),
                                           v.memset(self.dbg_v[:, :], 0.0).then_inc(self.dv.h, 1)))
        tpe = self.pe.inc()
        self.t(lambda t, w=tso: (t.wait_ge(sem.h, w), t.nop().then_inc(self.pe.h, 1)))
        pre_ac = self.ac.n
        tac = self.ac.inc()
        self.s(lambda s, w=tso, p=pre_ac: (s.wait_ge(sem.h, w), s.wait_ge(self.ac.h, p),
                                           s.activation(self.dbg_s[:, :], self.eps[:, :], AF.Copy).then_inc(self.ac.h, 1)))

    def alloc(self):
        nc = self.nc
        lay = Lay(nc)
        self.lay = lay
        cur = [SBUF_LO]
        def seq(name, shape, dtype):
            cur[0] = (cur[0] + 63) // 64 * 64
            t = lay.at(name, shape, dtype, cur[0], "p1", "p4")
            cur[0] += int(np.prod(shape[1:])) * _dtsize(dtype)
            return t
        self.eps = seq("eps", [P, 1], F32)
        self.ones128f = seq("ones128f", [P, 1], F32)
        self.onesrow = seq("onesrow", [1, 1024], BF16)
        self.ident_bf = seq("ident_bf", [P, P], BF16)
        self.rt_bf = seq("rt_bf", [PD, PD], BF16)
        self.bn_a = seq("bn_a", [P, 96], F32)
        self.mv2_a = seq("mv2_a", [P, 8], F32)
        self.mean_a = seq("mean_a", [P, 4], F32)
        self.sq_a = seq("sq_a", [P, 4], F32)
        self.rstd_a4 = seq("rstd_a4", [P, 4], F32)
        self.negm_a4 = seq("negm_a4", [P, 4], F32)
        self.kabs = seq("kabs", [P, 1], F32)
        self.qabs = seq("qabs", [P, 1], F32)
        self.qabs2 = seq("qabs2", [P, 1], F32)
        self.kr2s = seq("kr2s", [1, 1], F32)
        self.bsum = seq("bsum", [1, 2], F32)
        self.negb = seq("negb", [1, 1], F32)
        self.rd = seq("rd", [P, 1], F32)
        self.combine = seq("combine", [P, NQB * 8], F32)
        self.dbg_v = seq("dbg_v", [P, 1], F32)
        self.dbg_s = seq("dbg_s", [P, 1], F32)
        self.sig = seq("sig", [P, NQB], F32)
        self.negm_sv = seq("negm_sv", [P, NQB], F32)
        self.rstd_sv = seq("rstd_sv", [P, NQB], F32)
        A = (cur[0] + 1023) // 1024 * 1024
        self.wdkv_b = lay.at("wdkv_b", [P, NKT * LAT], BF16, A, "p1")
        self.wkr_b = lay.at("wkr_b", [P, NKT * PD], BF16, A + 16 * KB, "p1")
        self.coskb = [lay.at(f"coskb{i}", [PD, 512], F32, A + 18 * KB + 2 * KB * i, "p1") for i in range(2)]
        self.sinkb = [lay.at(f"sinkb{i}", [PD, 512], F32, A + 22 * KB + 2 * KB * i, "p1") for i in range(2)]
        self.ropet1 = lay.at("ropet1", [PD, 512], F32, A + 26 * KB, "p1", "at")
        self.ropet2 = lay.at("ropet2", [PD, 512], F32, A + 28 * KB, "p1", "at")
        self.kr_bf = lay.at("kr_bf", [PD, 512], BF16, A + 30 * KB, "p1")
        self.wdq_b = lay.at("wdq_b", [P, NKT * LAT], BF16, A + 31 * KB, "p1")
        self.xn_bf = [lay.at(f"xn_bf{i}", [P, D], BF16, A + 47 * KB + 4 * KB * i, "p1", "p4") for i in range(2)]
        AEND = A + 55 * KB
        self.ukb = [lay.at(f"ukb{i}", [P, NMT * P], BF16, A + KB * i, "at") for i in range(2)]
        self.uvb = [lay.at(f"uvb{i}", [P, NMT * P], BF16, A + 2 * KB + KB * i, "at") for i in range(2)]
        self.uqb = [lay.at(f"uqb{i}", [P, NMT * P], BF16, A + 4 * KB + KB * i, "at") for i in range(2)]
        self.qrb = [lay.at(f"qrb{i}", [P, NMT * PD], BF16, A + 6 * KB + 512 * i, "at") for i in range(2)]
        self.e_bf = ([lay.at(f"e_bf{i}", [P, 512], BF16, A + 7 * KB + KB * i, "at") for i in range(2)]
                     + [lay.at(f"e_bf{i+2}", [P, 512], BF16, A + 24 * KB + KB * i, "at") for i in range(2)])
        self.o_bf = lay.at("o_bf", [P, P], BF16, A + 9 * KB, "at")
        self.qct = lay.at("qct", [P, TOK], BF16, A + 10 * KB, "at")
        self.qrt = lay.at("qrt", [P, TOK], BF16, A + 12 * KB, "at")
        self.qr_bf = lay.at("qr_bf", [PD, 512], BF16, A + 14 * KB, "at")
        self.cosq = lay.at("cosq", [PD, TOK], F32, A + 16 * KB, "at")
        self.sinq = lay.at("sinq", [PD, TOK], F32, A + 20 * KB, "at")
        self.oacc = lay.at("oacc", [P, 8 * 132], F32, A + 31 * KB, "at")
        self.xwo = [lay.at(f"xwo{i}", [P, TOK], F32, A + 4 * KB * i, "p3") for i in range(4)]
        self.gwb = [lay.at(f"gwb{i}", [P, NKT * P], BF16, A + 4 * KB * i, "p4") for i in range(2)]
        self.uwb = [lay.at(f"uwb{i}", [P, NKT * P], BF16, A + 8 * KB + 4 * KB * i, "p4") for i in range(2)]
        self.dwb = ([lay.at(f"dwb{i}", [P, 2048], BF16, A + 16 * KB + 4 * KB * i, "p4") for i in range(2)]
                    + [lay.at(f"dwb{i+2}", [P, 2048], BF16, A + 32 * KB + 4 * KB * i, "p4") for i in range(2)])
        self.rwsg_b = lay.at("rwsg_b", [P, NKT * 9], BF16, A + 24 * KB, "p4")
        self.probs_all = lay.at("probs_all", [P, NQB * 8], F32, A + 25 * KB, "p4").ap()
        self.mask_all = lay.at("mask_all", [P, NQB * 8], F32, A + 25 * KB + 256, "p4").ap()
        self.keep_all = lay.at("keep_all", [P, NQB * 8], F32, A + 25 * KB + 512, "p4").ap()
        self.m1s = lay.at("m1s", [P, NQB], F32, A + 25 * KB + 768, "p4").ap()
        self.m2s = lay.at("m2s", [P, NQB], F32, A + 25 * KB + 832, "p4").ap()
        self.d8 = lay.at("d8", [P, NQB], F32, A + 25 * KB + 896, "p4").ap()
        self.r8 = lay.at("r8", [P, NQB], F32, A + 25 * KB + 960, "p4").ap()
        self.silu_t = [lay.at(f"silu_t{i}", [P, 512], F32, A + 28 * KB + 2 * KB * i, "p4") for i in range(2)]
        R = AEND
        self.xbuf = lay.at("xbuf", [P, 5 * D], F32, R, "p1")
        self.oT = lay.at("oT", [P, H * TOK], BF16, R, "at", "p3")
        self.xnt = lay.at("xnt", [P, NKT * TOK], BF16, R, "p4")
        self.xnt_scr = lay.at("xnt_scr", [P, NKT * 512], BF16, R + 40 * KB, "p1")
        self.kct = lay.at("kct", [P, S], BF16, R + 32 * KB, "at")
        self.wo_b = lay.at("wo_b", [P, NKT * TOK], BF16, R + 32 * KB, "p3")
        self.shact = lay.at("shact", [P, IMT * TOK], BF16, R + 32 * KB, "p4")
        self.vbuf = lay.at("vbuf", [P, 32 * 130], BF16, R + 48 * KB, "at")
        self.actt = lay.at("actt", [P, IMT * TOK], BF16, R + 54 * KB, "p4")
        CK = R + 57 * KB
        self.ckvt = lay.at("ckvt", [P, NMT * S], BF16, CK, "p1", "at")
        self.krt = lay.at("krt", [P, S], BF16, CK + 64 * KB, "p1", "at")
        self.cqt = lay.at("cqt", [P, NMT * TOK], BF16, CK + 80 * KB, "p1", "at")
        assert CK + 88 * KB <= SBUF_HI
        OA = (SBUF_HI - 64 * KB) // 64 * 64
        self.out_acc = lay.at("out_acc", [P, NQB * D], F32, OA, "p3", "p4")
        assert OA >= R + 76 * KB, "out_acc overlaps p4 actt"
        assert OA >= R + 64 * KB, "out_acc overlaps p3 wo_b"
        self.pb = [nc.alloc_psum_tensor(f"pb{i}", [P, 512], F32) for i in range(8)]
        self.pb_bf = [b.ap().bitcast(BF16) for b in self.pb]

    def emit_all(self):
        self.emit_init()
        self.emit_stage_a()
        self.emit_cq()
        if self.debug < 9:
            self.emit_attention()
        if self.debug < 8:
            self.emit_wo()
            self.emit_ffn_pre()
            self.emit_shared()
            self.emit_moe()
        self.emit_store()

    def emit_init(self):
        io = self.io
        def ms(v):
            v.memset(self.eps[:, :], EPS)
            v.memset(self.ones128f[:, :], 1.0)
            v.memset(self.onesrow[:, :], 1.0)
            v.memset(self.krt.ap()[64:65, :], 1.0).then_inc(self.dv.h, 1)
        self.init_dv = self.dv.inc()
        self.v(ms)
        self.dma("gpsimd", self.swa, self.ident_bf.ap()[:, :], io["ident"][:, :])
        self.rt_ld = self.dma("gpsimd", self.swa, self.rt_bf.ap()[:, :], io["rt"][:, :])
        self.dma("gpsimd", self.swa,
                 self.wdkv_b.ap().rearrange("p (kt n) -> p kt n", n=LAT),
                 io["wdkv"].rearrange("(kt p) n -> p kt n", p=P))
        self.dma("gpsimd", self.swa,
                 self.wkr_b.ap().rearrange("p (kt n) -> p kt n", n=PD),
                 io["wkr"].rearrange("(kt p) n -> p kt n", p=P))
        self.wdq_ld = self.dma("gpsimd", self.swa,
                               self.wdq_b.ap().rearrange("p (kt n) -> p kt n", n=LAT),
                               io["wdq"].rearrange("(kt p) n -> p kt n", p=P))
        self.wa_ld = self.wdq_ld

    def emit_ln_bn(self, xap, slot, xwait=None):
        dv = self.dv
        j = slot % 4
        bn = self.bn_a[:, 24 * j:24 * (j + 1)]
        pre = dv.n
        def st_(v, xap=xap, bn=bn, pre=pre, xw=xwait):
            v.wait_ge(self.dv.h, pre)
            if xw is not None:
                v.wait_ge(xw[0].h, xw[1])
            v.bn_stats(bn[:, 0:6], xap[:, 0:512])
            v.bn_stats(bn[:, 6:12], xap[:, 512:1024])
            v.bn_stats(bn[:, 12:18], xap[:, 1024:1536])
            v.bn_stats(bn[:, 18:24], xap[:, 1536:2048]).then_inc(self.dv.h, 1)
        t = dv.inc()
        self.v(st_)
        return t

    def emit_ln_mv4(self, t_bns, save_st=None):
        dv, ac = self.dv, self.ac
        if save_st is None:
            rstd_a, negm_a = self.rstd_a4[:, 0:4], self.negm_a4[:, 0:4]
        else:
            rstd_a = self.rstd_sv[:, 4 * save_st:4 * save_st + 4]
            negm_a = self.negm_sv[:, 4 * save_st:4 * save_st + 4]
        negm_cols = [negm_a[:, tt:tt + 1] for tt in range(4)]
        pre = dv.n
        def agg(v, w=t_bns, pre=pre):
            v.wait_ge(self.dv.h, max(w, pre))
            for tt in range(3):
                v.bn_aggr(self.mv2_a[:, 2 * tt:2 * tt + 2], self.bn_a[:, 24 * tt:24 * tt + 24])
            v.bn_aggr(self.mv2_a[:, 6:8], self.bn_a[:, 72:96]).then_inc(self.dv.h, 1)
        tagg = dv.inc()
        self.v(agg)
        pre_ac = ac.n
        tac = ac.inc()
        def sq4(s, w=tagg, pre=pre_ac):
            s.wait_ge(self.dv.h, w)
            s.wait_ge(self.ac.h, pre)
            for tt in range(3):
                s.activation(self.sq_a[:, tt:tt + 1], self.mv2_a[:, 2 * tt + 1:2 * tt + 2],
                             AF.Sqrt, bias=self.eps[:, 0:1])
            s.activation(self.sq_a[:, 3:4], self.mv2_a[:, 7:8],
                         AF.Sqrt, bias=self.eps[:, 0:1]).then_inc(self.ac.h, 1)
        self.s(sq4)
        def fin(v, w=tac, rstd_a=rstd_a, negm_cols=tuple(negm_cols)):
            v.wait_ge(self.ac.h, w)
            v.reciprocal(rstd_a, self.sq_a[:, 0:4])
            for tt in range(3):
                v.tensor_scalar(negm_cols[tt], self.mv2_a[:, 2 * tt:2 * tt + 1], -1.0, None, ALU.mult)
            v.tensor_scalar(negm_cols[3], self.mv2_a[:, 6:7], -1.0, None, ALU.mult).then_inc(self.dv.h, 1)
        t_fin = dv.inc()
        self.v(fin)
        return rstd_a, negm_a, t_fin

    def emit_ln_apply(self, xap, xn_out, rstd, negm, t_fin):
        dv = self.dv
        def apply(v, xap=xap, xn_out=xn_out, w=t_fin, rstd=rstd, negm=negm):
            v.wait_ge(self.dv.h, w)
            v.tensor_scalar(xn_out[:, :], xap[:, :], negm, rstd,
                            ALU.add, ALU.mult).then_inc(self.dv.h, 1)
        t2 = dv.inc()
        self.v(apply)
        return t2

    def emit_transpose16(self, xn_ap, dst_cols, wait_dv, evac="vector", banks=(2, 3)):
        pe = self.pe
        for half in range(2):
            ev = evac if evac != "mixed" else ("vector" if half == 0 else "scalar")
            bank = banks[half]
            def dotp(t, xn_ap=xn_ap, half=half, bank=bank, wdv=wait_dv, fb=self.bankfree.get(bank)):
                t.wait_ge(self.dv.h, wdv)
                t.wait_ge(self.swa.h, 16)
                self.wait_bf(t, fb)
                for i in range(7):
                    kt = half * 8 + i
                    t.transpose(self.pb_bf[bank][:, P * i:P * (i + 1)],
                                xn_ap[:, P * kt:P * (kt + 1)], self.ident_bf.ap()[:, :])
                t.transpose(self.pb_bf[bank][:, P * 7:P * 8],
                            xn_ap[:, P * (half * 8 + 7):P * (half * 8 + 8)],
                            self.ident_bf.ap()[:, :]).then_inc(self.pe.h, 1)
            tpe = pe.inc()
            self.t(dotp)
            if ev == "vector":
                def doev(v, half=half, bank=bank, dst_cols=dst_cols, w=tpe):
                    v.wait_ge(self.pe.h, w)
                    for i in range(7):
                        v.tensor_copy(dst_cols[half * 8 + i], self.pb_bf[bank][:, P * i:P * (i + 1)])
                    v.tensor_copy(dst_cols[half * 8 + 7], self.pb_bf[bank][:, P * 7:P * 8]).then_inc(self.dv.h, 1)
                tdv = self.dv.inc()
                self.v(doev)
                self.bankfree[bank] = ("vector", tdv)
            else:
                def doev(s, half=half, bank=bank, dst_cols=dst_cols, w=tpe, pre=self.ac.n):
                    s.wait_ge(self.pe.h, w)
                    s.wait_ge(self.ac.h, pre)
                    for i in range(7):
                        s.copy(dst_cols[half * 8 + i], self.pb_bf[bank][:, P * i:P * (i + 1)])
                    s.copy(dst_cols[half * 8 + 7], self.pb_bf[bank][:, P * 7:P * 8]).then_inc(self.ac.h, 1)
                tac = self.ac.inc()
                self.s(doev)
                self.bankfree[bank] = ("scalar", tac)
        return pe.n, (self.dv.n if evac == "vector" else None)

    def emit_stage_a(self):
        io = self.io
        xsrc = io["x_full"].rearrange("(g p) d -> g p d", p=P)
        self._stage_tokens(xsrc, 2 * NST, True)

    def _stage_tokens(self, xsrc, nhalves, is_full):
        nst = nhalves // 2
        xfree = [self.dv.n] * 5
        stats_p = {}
        mvd = {}
        def emit_ls(st):
            for tt in range(4):
                gi = st * 4 + tt
                slot = gi % 5
                tsx = self.dma("sync", self.sx[slot],
                               self.xbuf.ap().rearrange("p (f d) -> p f d", d=D)[:, slot, :],
                               xsrc[gi], [("vector", xfree[slot])])
                xap = self.xbuf.ap().rearrange("p (f d) -> p f d", d=D)[:, slot, :]
                stats_p[gi] = [tt, xap, tsx, slot]
        def emit_bn(gi):
            tt, xap, tsx, slot = stats_p[gi]
            return self.emit_ln_bn(xap, tt, xwait=(self.sx[slot], tsx))
        def emit_mv4(st, t_bns):
            mvd[st] = self.emit_ln_mv4(t_bns, save_st=(None if is_full else st))
        emit_ls(0)
        for tt in range(3):
            emit_bn(tt)
        emit_mv4(0, emit_bn(3))
        for st in range(nst):
            rstd_a, negm_a, t_fin = mvd.pop(st)
            for tt in range(4):
                gi = st * 4 + tt
                ptt, pxap, _, pslot = stats_p.pop(gi)
                self._flush_ln(ptt, pxap, rstd_a[:, tt:tt + 1], negm_a[:, tt:tt + 1], t_fin)
                xfree[pslot] = self.dv.n
            xnt_ready = ("vector", self.dv.n, "scalar", self.ac.n)
            tcb = None
            if st + 1 < nst:
                emit_ls(st + 1)
                def tile_cb(mt, st=st):
                    if mt < 2:
                        emit_bn((st + 1) * 4 + mt)
                    else:
                        emit_bn((st + 1) * 4 + 2)
                        emit_mv4(st + 1, emit_bn((st + 1) * 4 + 3))
                tcb = tile_cb
            if is_full:
                self._stagea_proj(st, xnt_ready, tile_cb=tcb)
            else:
                self._cq_proj(st, xnt_ready)
                if tcb is not None:
                    for mt in range(3):
                        tcb(mt)

    def _flush_ln(self, tt, xap, rstd, negm, t_fin):
        xnb = self.xn_bf[tt % 2]
        prev_pe = self._xnb_pe[tt % 2]
        if prev_pe is not None:
            self.v(lambda v, w=prev_pe: v.wait_ge(self.pe.h, w))
        wdv = self.emit_ln_apply(xap, xnb.ap(), rstd, negm, t_fin)
        dst = [self.xnt_scr.ap()[:, kt * 512 + tt * P: kt * 512 + tt * P + P] for kt in range(NKT)]
        pec, dvc = self.emit_transpose16(xnb.ap(), dst, wdv, evac="scalar",
                                         banks=((2, 3) if tt % 2 == 0 else (4, 5)))
        self._xnb_pe[tt % 2] = pec

    def _stagea_proj(self, st, xnt_ready, tile_cb=None):
        io, pe, dv = self.io, self.pe, self.dv
        slot = st % 2
        kw = [] if self._krc_dv[slot] is None else [("vector", self._krc_dv[slot])]
        t_c = self.dma("gpsimd", self.sckc[slot], self.coskb[slot].ap()[:, :],
                       io["cosk"][:, 512 * st:512 * (st + 1)], kw)
        t_s = self.dma("gpsimd", self.scks[slot], self.sinkb[slot].ap()[:, :],
                       io["sink"][:, 512 * st:512 * (st + 1)], kw)
        def krmm(t, st=st, w=xnt_ready, wld=self.wa_ld, fb=self.bankfree.get(4)):
            t.wait_ge(self.dv.h, w[1])
            t.wait_ge(self.ac.h, w[3])
            self.wait_bf(t, fb)
            t.wait_ge(self.swa.h, wld)
            for kt in range(NKT - 1):
                t.matmul(self.pb[4][0:PD, :],
                         self.wkr_b.ap()[:, kt * PD:(kt + 1) * PD],
                         self.xnt_scr.ap()[:, kt * 512:(kt + 1) * 512],
                         start=(kt == 0), stop=False)
            kt = NKT - 1
            t.matmul(self.pb[4][0:PD, :],
                     self.wkr_b.ap()[:, kt * PD:(kt + 1) * PD],
                     self.xnt_scr.ap()[:, kt * 512:(kt + 1) * 512],
                     start=False, stop=True).then_inc(self.pe.h, 1)
        tpe_kr = pe.inc()
        self.t(krmm)
        def krev(v, w=tpe_kr):
            v.wait_ge(self.pe.h, w)
            v.tensor_copy(self.kr_bf.ap()[:, :], self.pb[4][0:PD, :]).then_inc(self.dv.h, 1)
        tdv_krbf = dv.inc()
        self.v(krev)
        def krrot(t, w=tdv_krbf, wrt=self.rt_ld, fb=self.bankfree.get(5)):
            t.wait_ge(self.dv.h, w)
            self.wait_bf(t, fb)
            t.wait_ge(self.swa.h, wrt)
            t.matmul(self.pb[5][0:PD, :], self.rt_bf.ap()[:, :], self.kr_bf.ap()[:, :],
                     start=True, stop=True).then_inc(self.pe.h, 1)
        tpe_rot = pe.inc()
        self.t(krrot)
        pre_kr = dv.n
        def krcomb(v, st=st, slot=slot, w=tpe_rot, wc=t_c, ws=t_s, pre=pre_kr):
            v.wait_ge(self.pe.h, w)
            v.wait_ge(self.sckc[slot].h, wc)
            v.wait_ge(self.scks[slot].h, ws)
            v.wait_ge(self.dv.h, pre)
            v.tensor_tensor(self.ropet1.ap()[:, :], self.pb[4][0:PD, :], self.coskb[slot].ap()[:, :], ALU.mult)
            v.tensor_tensor(self.ropet2.ap()[:, :], self.pb[5][0:PD, :], self.sinkb[slot].ap()[:, :], ALU.mult).then_inc(self.dv.h, 1)
            v.wait_ge(self.dv.h, pre + 1)
            v.tensor_tensor(self.krt.ap()[0:PD, 512 * st:512 * (st + 1)],
                            self.ropet1.ap()[:, :], self.ropet2.ap()[:, :], ALU.add).then_inc(self.dv.h, 1)
        dv.inc()
        tdv_kr = dv.inc()
        self.v(krcomb)
        self._krc_dv[slot] = tdv_kr
        self.bankfree[4] = ("vector", tdv_kr)
        self.bankfree[5] = ("vector", tdv_kr)
        for mt in range(NMT):
            bank = mt % 2
            def ckv(t, st=st, mt=mt, bank=bank, w=xnt_ready, wld=self.wa_ld, fb=self.bankfree.get(bank)):
                t.wait_ge(self.dv.h, w[1])
                t.wait_ge(self.ac.h, w[3])
                self.wait_bf(t, fb)
                t.wait_ge(self.swa.h, wld)
                for kt in range(NKT - 1):
                    t.matmul(self.pb[bank][:, :],
                             self.wdkv_b.ap()[:, kt * LAT + mt * P: kt * LAT + (mt + 1) * P],
                             self.xnt_scr.ap()[:, kt * 512:(kt + 1) * 512],
                             start=(kt == 0), stop=False)
                kt = NKT - 1
                t.matmul(self.pb[bank][:, :],
                         self.wdkv_b.ap()[:, kt * LAT + mt * P: kt * LAT + (mt + 1) * P],
                         self.xnt_scr.ap()[:, kt * 512:(kt + 1) * 512],
                         start=False, stop=True).then_inc(self.pe.h, 1)
            tpe = pe.inc()
            self.t(ckv)
            def ckve(v, st=st, mt=mt, bank=bank, w=tpe):
                v.wait_ge(self.pe.h, w)
                v.tensor_copy(self.ckvt.ap()[:, mt * S + st * 512: mt * S + st * 512 + 512],
                              self.pb[bank][:, :]).then_inc(self.dv.h, 1)
            tdv = dv.inc()
            self.v(ckve)
            self.bankfree[bank] = ("vector", tdv)
            if tile_cb is not None and mt < 3:
                tile_cb(mt)

    def _cq_proj(self, j, xnt_ready):
        pe, dv = self.pe, self.dv
        for mt in range(NMT):
            bank = mt % 2
            def cq(t, j=j, mt=mt, bank=bank, w=xnt_ready, wld=self.wdq_ld, fb=self.bankfree.get(bank)):
                t.wait_ge(self.dv.h, w[1])
                t.wait_ge(self.ac.h, w[3])
                self.wait_bf(t, fb)
                t.wait_ge(self.swa.h, wld)
                for kt in range(NKT - 1):
                    t.matmul(self.pb[bank][:, :],
                             self.wdq_b.ap()[:, kt * LAT + mt * P: kt * LAT + (mt + 1) * P],
                             self.xnt_scr.ap()[:, kt * 512:(kt + 1) * 512],
                             start=(kt == 0), stop=False)
                kt = NKT - 1
                t.matmul(self.pb[bank][:, :],
                         self.wdq_b.ap()[:, kt * LAT + mt * P: kt * LAT + (mt + 1) * P],
                         self.xnt_scr.ap()[:, kt * 512:(kt + 1) * 512],
                         start=False, stop=True).then_inc(self.pe.h, 1)
            tpe = pe.inc()
            self.t(cq)
            def cqe(v, j=j, mt=mt, bank=bank, w=tpe):
                v.wait_ge(self.pe.h, w)
                v.tensor_copy(self.cqt.ap()[:, mt * TOK + j * 512: mt * TOK + j * 512 + 512],
                              self.pb[bank][:, :]).then_inc(self.dv.h, 1)
            tdv = dv.inc()
            self.v(cqe)
            self.bankfree[bank] = ("vector", tdv)

    def emit_cq(self):
        io = self.io
        xsrc = io["x_own"].rearrange("(g p) d -> g p d", p=P)
        self._stage_tokens(xsrc, 4, False)
        wkr = self.dv.n
        def krb(v, w=wkr):
            v.wait_ge(self.dv.h, w)
            v.reduce_max(self.kabs[0:PD, :], self.krt.ap()[0:PD, :], AX.X,
                         apply_absolute_value=True).then_inc(self.dv.h, 1)
        t1 = self.dv.inc()
        self.v(krb)
        t2 = self.ac.inc()
        self.s(lambda s, w=t1: (s.wait_ge(self.dv.h, w),
                                s.activation(self.kabs[0:PD, :], self.kabs[0:PD, :], AF.Square).then_inc(self.ac.h, 1)))
        def krmm(t, w=t2, fb=self.bankfree.get(0)):
            t.wait_ge(self.ac.h, w)
            self.wait_bf(t, fb)
            t.matmul(self.pb[0][0:1, 0:1], self.ones128f[0:PD, :], self.kabs[0:PD, :],
                     start=True, stop=True).then_inc(self.pe.h, 1)
        t3 = self.pe.inc()
        self.t(krmm)
        def kre(v, w=t3):
            v.wait_ge(self.pe.h, w)
            v.tensor_copy(self.kr2s[0:1, :], self.pb[0][0:1, 0:1]).then_inc(self.dv.h, 1)
        self.kr2s_dv = self.dv.inc()
        self.v(kre)
        self.bankfree[0] = ("vector", self.kr2s_dv)
        self.emit_dbg([("d_ckvt", self.ckvt.ap()[:, :]),
                       ("d_krt", self.krt.ap()[0:PD + 1, :]),
                       ("d_cqt", self.cqt.ap()[:, :])],
                      [("vector", self.dv.n)])

    def emit_attention(self):
        io = self.io
        p1_pe, p1_dv = self.pe.n, self.dv.n
        pre_v = self.dv.n
        def msv(v, wp=p1_pe, pre=pre_v):
            v.wait_ge(self.pe.h, wp)
            v.wait_ge(self.dv.h, pre)
            v.memset(self.vbuf.ap()[:, :], 0.0).then_inc(self.dv.h, 1)
            v.wait_ge(self.dv.h, pre + 1)
            ones_cols = self.vbuf.ap().rearrange("p (k c) -> p k c", c=130)[:, :, 128:129]
            v.memset(ones_cols, 1.0).then_inc(self.dv.h, 1)
        self.dv.inc(2)
        self.v(msv)
        self.cosq_ld = self.dma("gpsimd", self.sckc[0], self.cosq.ap()[:, :], io["cosq"][:, :],
                                [("tensor", p1_pe), ("vector", p1_dv)])
        self.sinq_ld = self.dma("gpsimd", self.scks[0], self.sinq.ap()[:, :], io["sinq"][:, :],
                                [("tensor", p1_pe), ("vector", p1_dv)])
        uk_src = io["wuk"].rearrange("(mt l) n -> l mt n", l=P)
        uv_src = io["wuv"].rearrange("(mt l) n -> l mt n", l=P)
        uq_src = io["wuq"].rearrange("(mt l) n -> l mt n", l=P)
        qr_src = io["wqr"].rearrange("(mt l) n -> l mt n", l=P)
        hwfree = [None, None]
        for h in range(H):
            par = h % 2
            if h == 0:
                waits = [("tensor", p1_pe), ("vector", p1_dv)]
            elif hwfree[par] is None:
                waits = []
            else:
                waits = [("tensor", hwfree[par])]
            wld = {}
            wld["uk"] = self.dma("gpsimd", self.suk[par],
                                 self.ukb[par].ap().rearrange("p (mt n) -> p mt n", n=P),
                                 uk_src[:, :, h * P:(h + 1) * P], waits)
            wld["uv"] = self.dma("gpsimd", self.suv[par],
                                 self.uvb[par].ap().rearrange("p (mt n) -> p mt n", n=P),
                                 uv_src[:, :, h * P:(h + 1) * P], waits)
            wld["uq"] = self.dma("gpsimd", self.suq[par],
                                 self.uqb[par].ap().rearrange("p (mt n) -> p mt n", n=P),
                                 uq_src[:, :, h * P:(h + 1) * P], waits)
            wld["qr"] = self.dma("gpsimd", self.sqr[par],
                                 self.qrb[par].ap().rearrange("p (mt n) -> p mt n", n=PD),
                                 qr_src[:, :, h * PD:(h + 1) * PD], waits)
            self.emit_head(h, wld, par)
            hwfree[par] = self._head_w_done
        if self._pend_epilogue is not None:
            self._pend_epilogue()
            self._pend_epilogue = None
        self.emit_dbg([("d_ot", self.oT.ap()[:, :])], [("vector", self.dv.n)])

    def emit_head(self, h, wld, par):
        pe, dv, ac = self.pe, self.dv, self.ac
        dbg_me = self.debug and h == self.DBG_H
        O_BANKS = (2, 3, 6, 7)
        for ch in range(16):
            bank = ch % 2
            def kup(t, h=h, ch=ch, bank=bank, par=par, w=wld["uk"], fb=self.bankfree.get(bank)):
                t.wait_ge(self.suk[par].h, w)
                self.wait_bf(t, fb)
                for mt in range(NMT - 1):
                    t.matmul(self.pb[bank][:, :],
                             self.ukb[par].ap()[:, mt * P:(mt + 1) * P],
                             self.ckvt.ap()[:, mt * S + ch * 512: mt * S + ch * 512 + 512],
                             start=(mt == 0), stop=False)
                mt = NMT - 1
                t.matmul(self.pb[bank][:, :],
                         self.ukb[par].ap()[:, mt * P:(mt + 1) * P],
                         self.ckvt.ap()[:, mt * S + ch * 512: mt * S + ch * 512 + 512],
                         start=False, stop=True).then_inc(self.pe.h, 1)
            tpe = pe.inc()
            self.t(kup)
            if ch % 2 == 0:
                def kupe(v, ch=ch, bank=bank, w=tpe):
                    v.wait_ge(self.pe.h, w)
                    v.tensor_copy(self.kct.ap()[:, ch * 512:(ch + 1) * 512],
                                  self.pb[bank][:, :]).then_inc(self.dv.h, 1)
                tdv = dv.inc()
                self.v(kupe)
                self.bankfree[bank] = ("vector", tdv)
            else:
                def kupe_s(s, ch=ch, bank=bank, w=tpe, pre=self.ac.n):
                    s.wait_ge(self.pe.h, w)
                    s.wait_ge(self.ac.h, pre)
                    s.copy(self.kct.ap()[:, ch * 512:(ch + 1) * 512],
                           self.pb[bank][:, :]).then_inc(self.ac.h, 1)
                tac = ac.inc()
                self.s(kupe_s)
                self.bankfree[bank] = ("scalar", tac)
        kct_dv, kct_ac = self.dv.n, self.ac.n
        if self._pend_epilogue is not None:
            self._pend_epilogue()
            self._pend_epilogue = None
        for ch in range(2):
            bank = ch % 2
            def qc(t, ch=ch, bank=bank, par=par, w=wld["uq"], fb=self.bankfree.get(bank)):
                t.wait_ge(self.suq[par].h, w)
                self.wait_bf(t, fb)
                for mt in range(NMT - 1):
                    t.matmul(self.pb[bank][:, :],
                             self.uqb[par].ap()[:, mt * P:(mt + 1) * P],
                             self.cqt.ap()[:, mt * TOK + ch * 512: mt * TOK + ch * 512 + 512],
                             start=(mt == 0), stop=False)
                mt = NMT - 1
                t.matmul(self.pb[bank][:, :],
                         self.uqb[par].ap()[:, mt * P:(mt + 1) * P],
                         self.cqt.ap()[:, mt * TOK + ch * 512: mt * TOK + ch * 512 + 512],
                         start=False, stop=True).then_inc(self.pe.h, 1)
            tpe = pe.inc()
            self.t(qc)
            def qce(v, ch=ch, bank=bank, w=tpe):
                v.wait_ge(self.pe.h, w)
                v.tensor_copy(self.qct.ap()[:, ch * 512:(ch + 1) * 512], self.pb[bank][:, :]).then_inc(self.dv.h, 1)
            tdv = dv.inc()
            self.v(qce)
            self.bankfree[bank] = ("vector", tdv)
        def qr_stage_a(ch):
            def qr(t, ch=ch, par=par, w=wld["qr"], fb=self.bankfree.get(4)):
                t.wait_ge(self.sqr[par].h, w)
                self.wait_bf(t, fb)
                for mt in range(NMT - 1):
                    t.matmul(self.pb[4][0:PD, :],
                             self.qrb[par].ap()[:, mt * PD:(mt + 1) * PD],
                             self.cqt.ap()[:, mt * TOK + ch * 512: mt * TOK + ch * 512 + 512],
                             start=(mt == 0), stop=False)
                mt = NMT - 1
                t.matmul(self.pb[4][0:PD, :],
                         self.qrb[par].ap()[:, mt * PD:(mt + 1) * PD],
                         self.cqt.ap()[:, mt * TOK + ch * 512: mt * TOK + ch * 512 + 512],
                         start=False, stop=True).then_inc(self.pe.h, 1)
            tpe_qr = pe.inc()
            self.t(qr)
            def qrev(v, w=tpe_qr):
                v.wait_ge(self.pe.h, w)
                v.tensor_copy(self.qr_bf.ap()[:, :], self.pb[4][0:PD, :]).then_inc(self.dv.h, 1)
            tdv_qrbf = dv.inc()
            self.v(qrev)
            return tdv_qrbf
        def qr_stage_b(ch, tdv_qrbf):
            def qrrot(t, w=tdv_qrbf, fb=self.bankfree.get(5)):
                t.wait_ge(self.dv.h, w)
                self.wait_bf(t, fb)
                t.matmul(self.pb[5][0:PD, :], self.rt_bf.ap()[:, :], self.qr_bf.ap()[:, :],
                         start=True, stop=True).then_inc(self.pe.h, 1)
            tpe_rot = pe.inc()
            self.t(qrrot)
            pre_qr = dv.n
            def qrcomb(v, ch=ch, w=tpe_rot, wc=self.cosq_ld, ws=self.sinq_ld, pre=pre_qr):
                v.wait_ge(self.pe.h, w)
                v.wait_ge(self.sckc[0].h, wc)
                v.wait_ge(self.scks[0].h, ws)
                v.wait_ge(self.dv.h, pre)
                v.tensor_tensor(self.ropet1.ap()[:, :], self.pb[4][0:PD, :],
                                self.cosq.ap()[:, ch * 512:(ch + 1) * 512], ALU.mult)
                v.tensor_tensor(self.ropet2.ap()[:, :], self.pb[5][0:PD, :],
                                self.sinq.ap()[:, ch * 512:(ch + 1) * 512], ALU.mult).then_inc(self.dv.h, 1)
                v.wait_ge(self.dv.h, pre + 1)
                v.tensor_tensor(self.qrt.ap()[0:PD, 512 * ch:512 * (ch + 1)],
                                self.ropet1.ap()[:, :], self.ropet2.ap()[:, :], ALU.add).then_inc(self.dv.h, 1)
            dv.inc()
            tdv_qr = dv.inc()
            self.v(qrcomb)
            self.bankfree[4] = ("vector", tdv_qr)
            self.bankfree[5] = ("vector", tdv_qr)
        def emit_qb():
            qdone = self.dv.n
            def qb_v(v, w=qdone):
                v.wait_ge(self.dv.h, w)
                v.reduce_max(self.qabs[:, :], self.qct.ap()[:, :], AX.X, apply_absolute_value=True)
                v.reduce_max(self.qabs2[0:PD, :], self.qrt.ap()[0:PD, :], AX.X,
                             apply_absolute_value=True).then_inc(self.dv.h, 1)
            t1q = dv.inc()
            self.v(qb_v)
            t2q = ac.inc()
            def qb_s(s, w=t1q):
                s.wait_ge(self.dv.h, w)
                s.activation(self.qabs[:, :], self.qabs[:, :], AF.Square)
                s.activation(self.qabs2[0:PD, :], self.qabs2[0:PD, :], AF.Square).then_inc(self.ac.h, 1)
            self.s(qb_s)
            return t2q
        def kb_chunk(j):
            def kb(v, j=j, wd=kct_dv, wa=kct_ac):
                v.wait_ge(self.dv.h, wd)
                v.wait_ge(self.ac.h, wa)
                v.reduce_max(self.mean_a[:, j:j + 1], self.kct.ap()[:, j * 2048:(j + 1) * 2048],
                             AX.X, apply_absolute_value=True).then_inc(self.dv.h, 1)
            dv.inc()
            self.v(kb)
        def kb_fin():
            pre = dv.n
            def kf(v, pre=pre):
                v.wait_ge(self.dv.h, pre)
                v.reduce_max(self.kabs[:, :], self.mean_a[:, 0:4], AX.X,
                             apply_absolute_value=True).then_inc(self.dv.h, 1)
            t1k = dv.inc()
            self.v(kf)
            t2k = ac.inc()
            self.s(lambda s, w=t1k: (s.wait_ge(self.dv.h, w),
                                     s.activation(self.kabs[:, :], self.kabs[:, :], AF.Square).then_inc(self.ac.h, 1)))
            return t2k

        def emit_bound_mms(t2k, t2q):
            def kb_mm(t, w=t2k, fb=self.bankfree.get(0)):
                t.wait_ge(self.ac.h, w)
                self.wait_bf(t, fb)
                t.matmul(self.pb[0][0:1, 0:1], self.ones128f[:, :], self.kabs[:, :],
                         start=True, stop=True).then_inc(self.pe.h, 1)
            t3 = pe.inc()
            self.t(kb_mm)
            pre_kb = dv.n
            def kb_e(v, w=t3, wkr=self.kr2s_dv, pre=pre_kb):
                v.wait_ge(self.pe.h, w)
                v.wait_ge(self.dv.h, max(wkr, pre))
                v.tensor_copy(self.bsum[0:1, 0:1], self.pb[0][0:1, 0:1]).then_inc(self.dv.h, 1)
                v.wait_ge(self.dv.h, pre + 1)
                v.tensor_tensor(self.bsum[0:1, 0:1], self.bsum[0:1, 0:1], self.kr2s[0:1, 0:1],
                                ALU.add).then_inc(self.dv.h, 1)
            dv.inc()
            kb_dv = dv.inc()
            self.v(kb_e)
            self.bankfree[0] = ("vector", kb_dv)
            def qb_mm(t, w=t2q, fb=self.bankfree.get(1)):
                t.wait_ge(self.ac.h, w)
                self.wait_bf(t, fb)
                t.matmul(self.pb[1][0:1, 1:2], self.ones128f[:, :], self.qabs[:, :], start=True, stop=False)
                t.matmul(self.pb[1][0:1, 1:2], self.ones128f[0:PD, :], self.qabs2[0:PD, :],
                         start=False, stop=True).then_inc(self.pe.h, 1)
            t3q = pe.inc()
            self.t(qb_mm)
            pre_qb = dv.n
            def qb_e(v, w=t3q, pre=pre_qb):
                v.wait_ge(self.pe.h, w)
                v.wait_ge(self.dv.h, pre)
                v.tensor_copy(self.bsum[0:1, 1:2], self.pb[1][0:1, 1:2]).then_inc(self.dv.h, 1)
                v.wait_ge(self.dv.h, pre + 1)
                v.tensor_mul(self.negb[0:1, :], self.bsum[0:1, 0:1], self.bsum[0:1, 1:2]).then_inc(self.dv.h, 1)
            dv.inc()
            t5 = dv.inc()
            self.v(qb_e)
            self.bankfree[1] = ("vector", t5)
            t6 = ac.inc()
            self.s(lambda s, w=t5: (s.wait_ge(self.dv.h, w),
                                    s.activation(self.negb[0:1, :], self.negb[0:1, :], AF.Sqrt).then_inc(self.ac.h, 1)))
            pre_r64 = dv.n
            def row64(v, w=t6, pre=pre_r64):
                v.wait_ge(self.ac.h, w)
                v.wait_ge(self.dv.h, pre)
                v.tensor_scalar(self.negb[0:1, :], self.negb[0:1, :], -1.0, None, ALU.mult).then_inc(self.dv.h, 1)
                v.wait_ge(self.dv.h, pre + 1)
                v.tensor_scalar(self.qrt.ap()[PD:PD + 1, :], self.onesrow.ap()[0:1, :],
                                self.negb[0:1, 0:1], None, ALU.mult).then_inc(self.dv.h, 1)
            dv.inc()
            b_dv = dv.inc()
            self.v(row64)
            return b_dv
        bound_dv = None
        t2k = t2q = None
        t_qrev = [None, None]
        for vh in range(2):
            for ktl in range(32):
                kt = vh * 32 + ktl
                bank = O_BANKS[ktl % 4]
                def vup(t, kt=kt, ktl=ktl, bank=bank, par=par, w=wld["uv"], fb=self.bankfree.get(bank)):
                    t.wait_ge(self.suv[par].h, w)
                    self.wait_bf(t, fb)
                    for mt in range(NMT - 1):
                        t.matmul(self.pb[bank][:, 0:P],
                                 self.ckvt.ap()[:, mt * S + kt * P: mt * S + (kt + 1) * P],
                                 self.uvb[par].ap()[:, mt * P:(mt + 1) * P],
                                 start=(mt == 0), stop=False)
                    mt = NMT - 1
                    t.matmul(self.pb[bank][:, 0:P],
                             self.ckvt.ap()[:, mt * S + kt * P: mt * S + (kt + 1) * P],
                             self.uvb[par].ap()[:, mt * P:(mt + 1) * P],
                             start=False, stop=True).then_inc(self.pe.h, 1)
                tpe = pe.inc()
                self.t(vup)
                if ktl % 2 == 0:
                    def vupe(v, ktl=ktl, bank=bank, w=tpe):
                        v.wait_ge(self.pe.h, w)
                        v.tensor_copy(self.vbuf.ap()[:, ktl * 130: ktl * 130 + P],
                                      self.pb[bank][:, 0:P]).then_inc(self.dv.h, 1)
                    tdv = dv.inc()
                    self.v(vupe)
                    self.bankfree[bank] = ("vector", tdv)
                else:
                    def vupe_s(s, ktl=ktl, bank=bank, w=tpe, pre=self.ac.n):
                        s.wait_ge(self.pe.h, w)
                        s.wait_ge(self.ac.h, pre)
                        s.copy(self.vbuf.ap()[:, ktl * 130: ktl * 130 + P],
                               self.pb[bank][:, 0:P]).then_inc(self.ac.h, 1)
                    tac = ac.inc()
                    self.s(vupe_s)
                    self.bankfree[bank] = ("scalar", tac)
                if vh == 0:
                    if ktl == 1:
                        t_qrev[0] = qr_stage_a(0)
                    elif ktl == 3:
                        qr_stage_b(0, t_qrev[0])
                    elif ktl == 5:
                        t_qrev[1] = qr_stage_a(1)
                    elif ktl == 7:
                        qr_stage_b(1, t_qrev[1])
                    elif ktl == 9:
                        t2q = emit_qb()
                    elif ktl in (11, 14, 17, 20):
                        kb_chunk((ktl - 11) // 3)
                    elif ktl == 22:
                        t2k = kb_fin()
            if vh == 0:
                bound_dv = emit_bound_mms(t2k, t2q)
                if dbg_me:
                    self.emit_dbg([("d_kct", self.kct.ap()[:, :]),
                                   ("d_qct", self.qct.ap()[:, :]),
                                   ("d_qrt", self.qrt.ap()[0:PD + 1, :]),
                                   ("d_bnd", self.bsum[0:1, 0:2])],
                                  [("vector", bound_dv)])
            vdone = self.dv.n
            vdone_ac = self.ac.n
            if dbg_me and vh == 1:
                self.emit_dbg([("d_vt", self.vbuf.ap()[:, :])], [("vector", vdone), ("scalar", vdone_ac)])
            for qc in range(NQC):
                SC_BANKS = (0, 1, 4, 5)
                tac_list = [None] * 32
                def emit_sc(ktl, qc=qc, vh=vh, vdone=vdone, vdone_ac=vdone_ac):
                    kt = vh * 32 + ktl
                    bank = SC_BANKS[ktl % 4]
                    def sc(t, qc=qc, kt=kt, ktl=ktl, bank=bank, wb=bound_dv, wv=vdone, wva=vdone_ac,
                           fb=self.bankfree.get(bank)):
                        if ktl == 0 and qc == 0:
                            t.wait_ge(self.dv.h, max(wb, wv))
                            t.wait_ge(self.ac.h, wva)
                        self.wait_bf(t, fb)
                        t.matmul(self.pb[bank][:, :],
                                 self.kct.ap()[:, kt * P:(kt + 1) * P],
                                 self.qct.ap()[:, qc * 512:(qc + 1) * 512],
                                 start=True, stop=False)
                        t.matmul(self.pb[bank][:, :],
                                 self.krt.ap()[0:PD + 1, kt * P:(kt + 1) * P],
                                 self.qrt.ap()[0:PD + 1, qc * 512:(qc + 1) * 512],
                                 start=False, stop=True).then_inc(self.pe.h, 1)
                    tpe_sc = self.pe.inc()
                    self.t(sc)
                    def ex(s, bank=bank, w=tpe_sc):
                        s.wait_ge(self.pe.h, w)
                        s.activation(self.e_bf[bank if bank < 2 else bank - 2].ap()[:, :],
                                     self.pb[bank][:, :], AF.Exp,
                                     scale=ATT_SCALE).then_inc(self.ac.h, 1)
                    tac = self.ac.inc()
                    self.s(ex)
                    tac_list[ktl] = tac
                    self.bankfree[bank] = ("scalar", tac)
                emit_sc(0)
                emit_sc(1)
                emit_sc(2)
                for ktl in range(32):
                    if ktl + 3 < 32:
                        emit_sc(ktl + 3)
                    sb = SC_BANKS[ktl % 4]
                    ebuf = self.e_bf[sb if sb < 2 else sb - 2]
                    def om(t, ktl=ktl, ebuf=ebuf, w=tac_list[ktl],
                           fbs=tuple(self.bankfree.get(b) for b in O_BANKS) if ktl == 0 else ()):
                        t.wait_ge(self.ac.h, w)
                        for f in fbs:
                            self.wait_bf(t, f)
                        for j in range(4):
                            ins = t.matmul(self.pb[O_BANKS[j]][:, 0:129],
                                           ebuf.ap()[:, j * P:(j + 1) * P],
                                           self.vbuf.ap()[:, ktl * 130: ktl * 130 + 129],
                                           start=(ktl == 0), stop=(ktl == 31))
                        ins.then_inc(self.pe.h, 1)
                    tpe_o = self.pe.inc()
                    self.t(om)
                pre_oe = dv.n
                def oev(v, qc=qc, vh=vh, w=tpe_o, pre=pre_oe):
                    v.wait_ge(self.pe.h, w)
                    v.wait_ge(self.dv.h, pre)
                    for j in range(4):
                        dst = self.oacc.ap()[:, (qc * 4 + j) * 132: (qc * 4 + j) * 132 + 129]
                        if vh == 0:
                            ins = v.tensor_copy(dst, self.pb[O_BANKS[j]][:, 0:129])
                        else:
                            ins = v.tensor_tensor(dst, dst, self.pb[O_BANKS[j]][:, 0:129], ALU.add)
                    ins.then_inc(self.dv.h, 1)
                t_oe = dv.inc()
                self.v(oev)
                for b in O_BANKS:
                    self.bankfree[b] = ("vector", t_oe)
        oacc_dv = self.dv.n
        def emit_epilogue(h=h, oacc_dv=oacc_dv):
            dv, pe = self.dv, self.pe
            for qb in range(NQB):
                off = qb * 132
                pre_ep = dv.n
                prev_tp = getattr(self, "_prev_obf_pe", None)
                def ep12(v, off=off, w=oacc_dv, pre=pre_ep, ptp=prev_tp):
                    if ptp is not None:
                        v.wait_ge(self.pe.h, ptp)
                    v.wait_ge(self.dv.h, max(w, pre))
                    v.reciprocal(self.rd[:, :], self.oacc.ap()[:, off + 128:off + 129]).then_inc(self.dv.h, 1)
                    v.wait_ge(self.dv.h, pre + 1)
                    v.tensor_scalar(self.o_bf.ap()[:, :], self.oacc.ap()[:, off:off + P],
                                    self.rd[:, 0:1], None, ALU.mult).then_inc(self.dv.h, 1)
                dv.inc()
                t2 = dv.inc()
                self.v(ep12)
                def ep3(t, w=t2, fb=self.bankfree.get(1)):
                    t.wait_ge(self.dv.h, w)
                    self.wait_bf(t, fb)
                    t.transpose(self.pb_bf[1][:, 0:P], self.o_bf.ap()[:, :],
                                self.ident_bf.ap()[:, :]).then_inc(self.pe.h, 1)
                t3 = pe.inc()
                self._prev_obf_pe = t3
                self.t(ep3)
                def ep4(v, h=h, qb=qb, w=t3):
                    v.wait_ge(self.pe.h, w)
                    v.tensor_copy(self.oT.ap()[:, h * TOK + qb * P: h * TOK + (qb + 1) * P],
                                  self.pb_bf[1][:, 0:P]).then_inc(self.dv.h, 1)
                t4 = dv.inc()
                self.v(ep4)
                self.bankfree[1] = ("vector", t4)
        self._pend_epilogue = emit_epilogue
        self._head_w_done = self.pe.n

    def emit_wo(self):
        io, pe, dv = self.io, self.pe, self.dv
        attn_pe = self.pe.n
        attn_dv = self.dv.n
        wo_src = io["wo"].rearrange("(kt p) n -> p kt n", p=P)
        xsrc = io["x_own"].rearrange("(qb p) d -> qb p d", p=P)
        half_pe = [None, None]
        prev_hf_dv = attn_dv
        wo_sems = [self.swo, self.sckc[0]]
        for hf in range(2):
            wwaits = [("tensor", attn_pe)] if hf == 0 else [("tensor", half_pe[0])]
            wo_lds = []
            for wc in range(8):
                wo_lds.append(self.dma(
                    "gpsimd", wo_sems[wc % 2],
                    self.wo_b.ap().rearrange("p (kt n) -> p kt n", n=TOK)[:, 2 * wc:2 * (wc + 1), :],
                    wo_src[:, 2 * wc:2 * (wc + 1), hf * TOK:(hf + 1) * TOK],
                    wwaits if wc < 2 else []))
            xld = [None] * NQB
            for qb in range(NQB):
                xwaits = [("vector", prev_hf_dv)] if qb < 2 else [("vector", xld[qb - 2])]
                tsx = self.dma("sync", self.sx[qb % 2], self.xwo[qb % 2].ap()[:, :],
                               xsrc[qb][:, hf * TOK:(hf + 1) * TOK], xwaits)
                for ch in range(2):
                    bank = (qb * 2 + ch) % 4
                    def wo(t, hf=hf, qb=qb, ch=ch, bank=bank, wlds=tuple(wo_lds), fb=self.bankfree.get(bank)):
                        self.wait_bf(t, fb)
                        for kt in range(NKT):
                            if kt % 2 == 0:
                                t.wait_ge(wo_sems[(kt // 2) % 2].h, wlds[kt // 2])
                            ins = t.matmul(self.pb[bank][:, :],
                                           self.oT.ap()[:, kt * TOK + qb * P: kt * TOK + (qb + 1) * P],
                                           self.wo_b.ap()[:, kt * TOK + ch * 512: kt * TOK + ch * 512 + 512],
                                           start=(kt == 0), stop=(kt == NKT - 1))
                        ins.then_inc(self.pe.h, 1)
                    tpe = pe.inc()
                    self.t(wo)
                    def woe(v, hf=hf, qb=qb, ch=ch, bank=bank, w=tpe, wx=tsx):
                        v.wait_ge(self.pe.h, w)
                        v.wait_ge(self.sx[qb % 2].h, wx)
                        off = qb * D + hf * TOK + ch * 512
                        v.tensor_tensor(self.out_acc.ap()[:, off:off + 512], self.pb[bank][:, :],
                                        self.xwo[qb % 2].ap()[:, ch * 512:(ch + 1) * 512],
                                        ALU.add).then_inc(self.dv.h, 1)
                    tdv = dv.inc()
                    self.v(woe)
                    self.bankfree[bank] = ("vector", tdv)
                xld[qb] = dv.n
            half_pe[hf] = self.pe.n
            prev_hf_dv = self.dv.n
        self.p3_dv = self.dv.n
        self.emit_dbg([("d_accp3", self.out_acc.ap()[:, :])], [("vector", self.dv.n)])

    def emit_ffn_pre(self):
        io, pe, dv, ac = self.io, self.pe, self.dv, self.ac
        rw_ld = self.dma("gpsimd", self.swa, self.rwsg_b.ap()[:, :], io["rwsg"][:, :],
                         [("vector", self.p3_dv)])
        xsrc = io["x_own"].rearrange("(qb p) d -> qb p d", p=P)
        xld = [None] * NQB
        base_dv = self.p3_dv
        tpe_r = [None] * NQB
        for qb in range(NQB):
            p_ = qb % 2
            b0, b1 = self.xwo[2 * p_], self.xwo[2 * p_ + 1]
            if qb < 2:
                xwaits = [("vector", self.p3_dv)]
            else:
                xwaits = [("vector", xld[qb - 2])]
            tx0 = self.dma("sync", self.sx[0], b0.ap()[:, :], xsrc[qb][:, 0:TOK], xwaits)
            tx1 = self.dma("sync", self.sx[1], b1.ap()[:, :], xsrc[qb][:, TOK:D], xwaits)
            xnb = self.xn_bf[p_]
            prev_pe = self._xnb_pe[p_]
            negm = self.negm_sv[:, qb:qb + 1]
            rstd = self.rstd_sv[:, qb:qb + 1]
            def apply(v, xnb=xnb, a=tx0, b=tx1, b0=b0, b1=b1, negm=negm, rstd=rstd,
                      w=base_dv, wp=prev_pe):
                v.wait_ge(self.sx[0].h, a)
                v.wait_ge(self.sx[1].h, b)
                v.wait_ge(self.dv.h, w)
                if wp is not None:
                    v.wait_ge(self.pe.h, wp)
                v.tensor_scalar(xnb.ap()[:, 0:TOK], b0.ap()[:, :], negm, rstd, ALU.add, ALU.mult)
                v.tensor_scalar(xnb.ap()[:, TOK:D], b1.ap()[:, :], negm, rstd,
                                ALU.add, ALU.mult).then_inc(self.dv.h, 1)
            t2 = dv.inc()
            self.v(apply)
            dst = [self.xnt.ap()[:, kt * TOK + qb * P: kt * TOK + (qb + 1) * P] for kt in range(NKT)]
            pec, dvc = self.emit_transpose16(xnb.ap(), dst, t2, evac="mixed",
                                             banks=((2, 3) if qb % 2 == 0 else (4, 5)))
            self._xnb_pe[p_] = pec
            xld[qb] = t2
            xnt_dv, xnt_ac = self.dv.n, self.ac.n
            def rmm(t, qb=qb, w=xnt_dv, wa=xnt_ac, wld=rw_ld,
                    fb=(self.bankfree.get(7) if qb == 0 else None)):
                t.wait_ge(self.dv.h, w)
                t.wait_ge(self.ac.h, wa)
                t.wait_ge(self.swa.h, wld)
                self.wait_bf(t, fb)
                for kt in range(NKT - 1):
                    t.matmul(self.pb[7][:, qb * 9: qb * 9 + 9],
                             self.xnt.ap()[:, kt * TOK + qb * P: kt * TOK + (qb + 1) * P],
                             self.rwsg_b.ap()[:, kt * 9: kt * 9 + 9],
                             start=(qb == 0 and kt == 0), stop=False)
                kt = NKT - 1
                t.matmul(self.pb[7][:, qb * 9: qb * 9 + 9],
                         self.xnt.ap()[:, kt * TOK + qb * P: kt * TOK + (qb + 1) * P],
                         self.rwsg_b.ap()[:, kt * 9: kt * 9 + 9],
                         start=False, stop=(qb == NQB - 1)).then_inc(self.pe.h, 1)
            tpe_r[qb] = pe.inc()
            self.t(rmm)
        self.p4a_apply_dv = self.dv.n
        pre_ac2 = ac.n
        def sigs(s, w=tpe_r[NQB - 1], pre=pre_ac2):
            s.wait_ge(self.pe.h, w)
            s.wait_ge(self.ac.h, pre)
            for qb in range(NQB):
                ins = s.activation(self.sig[:, qb:qb + 1],
                                   self.pb[7][:, qb * 9 + 8: qb * 9 + 9], AF.Sigmoid)
            ins.then_inc(self.ac.h, 1)
        t_sig = ac.inc()
        self.s(sigs)
        pre_dv2 = dv.n
        def rmx(v, w=tpe_r[NQB - 1], pre=pre_dv2):
            v.wait_ge(self.pe.h, w)
            v.wait_ge(self.dv.h, pre)
            for qb in range(NQB):
                ins = v.reduce_max(self.m1s[:, qb:qb + 1],
                                   self.pb[7][:, qb * 9: qb * 9 + 8], AX.X, negate=True)
            ins.then_inc(self.dv.h, 1)
        t_rmx = dv.inc()
        self.v(rmx)
        def exps(s, w=t_rmx, pre=t_sig):
            s.wait_ge(self.dv.h, w)
            s.wait_ge(self.ac.h, pre)
            for qb in range(NQB):
                ins = s.activation(self.probs_all[:, qb * 8:(qb + 1) * 8],
                                   self.pb[7][:, qb * 9: qb * 9 + 8],
                                   AF.Exp, bias=self.m1s[:, qb:qb + 1])
            ins.then_inc(self.ac.h, 1)
        t_exp = ac.inc()
        self.s(exps)
        self.bankfree[7] = ("scalar", t_exp)
        b2 = dv.n
        def top2b(v, w=t_exp, b=b2):
            v.wait_ge(self.ac.h, w)
            v.wait_ge(self.dv.h, b)
            for qb in range(NQB):
                ins = v.reduce_max(self.m1s[:, qb:qb + 1],
                                   self.probs_all[:, qb * 8:(qb + 1) * 8], AX.X)
            ins.then_inc(self.dv.h, 1)
            v.wait_ge(self.dv.h, b + 1)
            for qb in range(NQB):
                ins = v.tensor_scalar(self.mask_all[:, qb * 8:(qb + 1) * 8],
                                      self.probs_all[:, qb * 8:(qb + 1) * 8],
                                      self.m1s[:, qb:qb + 1], None, ALU.is_ge)
            ins.then_inc(self.dv.h, 1)
            v.wait_ge(self.dv.h, b + 2)
            v.scalar_tensor_tensor(self.mask_all[:, :], self.mask_all[:, :], -2.0,
                                   self.probs_all[:, :], ALU.mult, ALU.add).then_inc(self.dv.h, 1)
            v.wait_ge(self.dv.h, b + 3)
            for qb in range(NQB):
                ins = v.reduce_max(self.m2s[:, qb:qb + 1],
                                   self.mask_all[:, qb * 8:(qb + 1) * 8], AX.X)
            ins.then_inc(self.dv.h, 1)
            v.wait_ge(self.dv.h, b + 4)
            for qb in range(NQB):
                v.tensor_scalar(self.keep_all[:, qb * 8:(qb + 1) * 8],
                                self.probs_all[:, qb * 8:(qb + 1) * 8],
                                self.m2s[:, qb:qb + 1], None, ALU.is_ge)
            v.tensor_tensor(self.d8[:, :], self.m1s[:, :], self.m2s[:, :],
                            ALU.add).then_inc(self.dv.h, 1)
            v.wait_ge(self.dv.h, b + 5)
            v.reciprocal(self.r8[:, :], self.d8[:, :]).then_inc(self.dv.h, 1)
            v.wait_ge(self.dv.h, b + 6)
            for qb in range(NQB):
                ins = v.scalar_tensor_tensor(self.combine[:, qb * 8:(qb + 1) * 8],
                                             self.probs_all[:, qb * 8:(qb + 1) * 8],
                                             self.r8[:, qb:qb + 1],
                                             self.keep_all[:, qb * 8:(qb + 1) * 8],
                                             ALU.mult, ALU.mult)
            ins.then_inc(self.dv.h, 1)
        dv.inc(6)
        t_comb = dv.inc()
        self.v(top2b)
        self.ffnpre_dv = self.dv.n
        self.emit_dbg([("d_xnt", self.xnt.ap()[:, :]),
                       ("d_comb", self.combine[:, :]),
                       ("d_sig", self.sig[:, :])],
                      [("vector", self.dv.n), ("scalar", self.ac.n)])

    def emit_gu_pair(self, gbuf, ubuf, gsem, usem, gld, uld, dst_cols):
        pe, dv, ac = self.pe, self.dv, self.ac
        for ch in range(2):
            bga, bgb = 2 * ch, 2 * ch + 1
            def gu(t, ch=ch, bga=bga, bgb=bgb, gbuf=gbuf, ubuf=ubuf, gsem=gsem, usem=usem, gld=gld, uld=uld,
                   fa=self.bankfree.get(bga), fbb=self.bankfree.get(bgb)):
                t.wait_ge(gsem.h, gld)
                t.wait_ge(usem.h, uld)
                self.wait_bf(t, fa)
                self.wait_bf(t, fbb)
                for kt in range(NKT):
                    t.matmul(self.pb[bga][:, :], gbuf.ap()[:, kt * P:(kt + 1) * P],
                             self.xnt.ap()[:, kt * TOK + ch * 512: kt * TOK + ch * 512 + 512],
                             start=(kt == 0), stop=(kt == NKT - 1))
                for kt in range(NKT - 1):
                    t.matmul(self.pb[bgb][:, :], ubuf.ap()[:, kt * P:(kt + 1) * P],
                             self.xnt.ap()[:, kt * TOK + ch * 512: kt * TOK + ch * 512 + 512],
                             start=(kt == 0), stop=False)
                kt = NKT - 1
                t.matmul(self.pb[bgb][:, :], ubuf.ap()[:, kt * P:(kt + 1) * P],
                         self.xnt.ap()[:, kt * TOK + ch * 512: kt * TOK + ch * 512 + 512],
                         start=False, stop=True).then_inc(self.pe.h, 1)
            tpe = pe.inc()
            self.t(gu)
            def sil(s, ch=ch, bga=bga, w=tpe):
                s.wait_ge(self.pe.h, w)
                s.activation(self.silu_t[ch].ap()[:, :], self.pb[bga][:, :], AF.Silu).then_inc(self.ac.h, 1)
            tac = ac.inc()
            self.s(sil)
            self.bankfree[bga] = ("scalar", tac)
            def mul(v, ch=ch, bgb=bgb, dst=dst_cols[ch], w=tac):
                v.wait_ge(self.ac.h, w)
                v.scalar_tensor_tensor(dst, self.pb[bgb][:, :], 1.0, self.silu_t[ch].ap()[:, :],
                                       ALU.mult, ALU.mult).then_inc(self.dv.h, 1)
            tdv = dv.inc()
            self.v(mul)
            self.bankfree[bgb] = ("vector", tdv)

    def emit_down(self, act_buf, dsrc4_fn, scal_fn):
        pe, dv = self.pe, self.dv
        act_ready = self.dv.n
        NG = (IMT + 3) // 4
        units = [(dc, g) for dc in range(4) for g in range(NG)]
        base_u = self._dw_u
        lds = {}
        def emit_ld(u):
            dc, g = units[u]
            par = (base_u + u) % 4
            free = self.dwfree.get(par)
            waits = [] if free is None else [("tensor", free)]
            nk = min(4, IMT - 4 * g)
            lds[u] = self.dma("gpsimd", self.sdw[par],
                              self.dwb[par].ap().rearrange("p (k n) -> p k n", n=512)[:, 0:nk, :],
                              dsrc4_fn(g, dc), waits)
        DW_DEPTH = 3
        for u in range(DW_DEPTH):
            emit_ld(u)
        for dc in range(4):
            for half in range(2):
                qbs = range(4) if half == 0 else range(4, NQB)
                for g in range(NG):
                    u = dc * NG + g
                    par = (base_u + u) % 4
                    nk = min(4, IMT - 4 * g)
                    for ki in range(nk):
                        kt = 4 * g + ki
                        def dn(t, kt=kt, ki=ki, dc=dc, par=par, wld=lds[u], act_buf=act_buf,
                               qbs=tuple(qbs),
                               fbs=tuple(self.bankfree.get(b) for b in qbs) if kt == 0 else (),
                               war=act_ready if (kt == 0 and dc == 0 and half == 0) else None):
                            t.wait_ge(self.sdw[par].h, wld)
                            if war is not None:
                                t.wait_ge(self.dv.h, war)
                            for f in fbs:
                                self.wait_bf(t, f)
                            for qb in qbs:
                                ins = t.matmul(self.pb[qb][:, :],
                                               act_buf.ap()[:, kt * TOK + qb * P: kt * TOK + (qb + 1) * P],
                                               self.dwb[par].ap()[:, ki * 512:(ki + 1) * 512],
                                               start=(kt == 0), stop=(kt == IMT - 1))
                            ins.then_inc(self.pe.h, 1)
                        tpe = pe.inc()
                        self.t(dn)
                    if half == 1:
                        self.dwfree[par] = tpe
                        if u + DW_DEPTH < len(units):
                            emit_ld(u + DW_DEPTH)
                last_pe = self.pe.n
                for qb in qbs:
                    def dne(v, qb=qb, dc=dc, w=last_pe, sc=scal_fn(qb)):
                        v.wait_ge(self.pe.h, w)
                        off = qb * D + dc * 512
                        v.scalar_tensor_tensor(self.out_acc.ap()[:, off:off + 512], self.pb[qb][:, :],
                                               sc, self.out_acc.ap()[:, off:off + 512],
                                               ALU.mult, ALU.add).then_inc(self.dv.h, 1)
                    tdv = dv.inc()
                    self.v(dne)
                    self.bankfree[qb] = ("vector", tdv)
                    self.last_dne[qb] = tdv
        self._dw_u += len(units)

    def emit_shared(self):
        io = self.io
        self.dwfree = {}
        self.gwfree = {}
        gsrc = io["shgt"].rearrange("(kt p) n -> p kt n", p=P)
        usrc = io["shupt"].rearrange("(kt p) n -> p kt n", p=P)
        pre = self.p4a_apply_dv
        for q in range(4):
            for mt in range(IMT):
                i_col = q * IM + mt * P
                par = mt % 2
                free = self.gwfree.get(par)
                waits = [("vector", pre)] if free is None else [("tensor", free)]
                gld = self.dma("gpsimd", self.sgw[par],
                               self.gwb[par].ap().rearrange("p (kt n) -> p kt n", n=P),
                               gsrc[:, :, i_col:i_col + P], waits)
                uld = self.dma("gpsimd", self.suw[par],
                               self.uwb[par].ap().rearrange("p (kt n) -> p kt n", n=P),
                               usrc[:, :, i_col:i_col + P], waits)
                dst = [self.shact.ap()[:, mt * TOK + ch * 512: mt * TOK + ch * 512 + 512] for ch in range(2)]
                self.emit_gu_pair(self.gwb[par], self.uwb[par], self.sgw[par], self.suw[par], gld, uld, dst)
                self.gwfree[par] = self.pe.n
            def dsrc4(g, dc, q=q):
                nk = min(4, IMT - 4 * g)
                return io["shdt"][q * IM + 4 * g * P: q * IM + (4 * g + nk) * P,
                                  dc * 512:(dc + 1) * 512].rearrange("(k p) n -> p k n", p=P)
            self.emit_down(self.shact, dsrc4, lambda qb: self.sig[:, qb:qb + 1])
        self.emit_dbg([("d_shact", self.shact.ap()[:, :]),
                       ("d_accp4s", self.out_acc.ap()[:, :])], [("vector", self.dv.n)])

    def emit_moe(self):
        io = self.io
        for e in range(E_):
            gsrc = io[f"gupt{e}"].rearrange("(kt p) n -> p kt n", p=P)
            for mt in range(IMT):
                par = mt % 2
                free = self.gwfree.get(par)
                waits = [] if free is None else [("tensor", free)]
                gld = self.dma("gpsimd", self.sgw[par],
                               self.gwb[par].ap().rearrange("p (kt n) -> p kt n", n=P),
                               gsrc[:, :, mt * P:(mt + 1) * P], waits)
                uld = self.dma("gpsimd", self.suw[par],
                               self.uwb[par].ap().rearrange("p (kt n) -> p kt n", n=P),
                               gsrc[:, :, IM + mt * P: IM + (mt + 1) * P], waits)
                dst = [self.actt.ap()[:, mt * TOK + ch * 512: mt * TOK + ch * 512 + 512] for ch in range(2)]
                self.emit_gu_pair(self.gwb[par], self.uwb[par], self.sgw[par], self.suw[par], gld, uld, dst)
                self.gwfree[par] = self.pe.n
            def dsrc4(g, dc, e=e):
                nk = min(4, IMT - 4 * g)
                return io[f"downt{e}"][4 * g * P:(4 * g + nk) * P,
                                       dc * 512:(dc + 1) * 512].rearrange("(k p) n -> p k n", p=P)
            def scal(qb, e=e):
                return self.combine[:, qb * 8 + e: qb * 8 + e + 1]
            self.emit_down(self.actt, dsrc4, scal)

    def emit_store(self):
        if self.debug >= 8:
            def fin(sy, tv=self.so[3].n):
                sy.wait_ge(self.so[3].h, tv)
            self.y(fin)
            return
        osrc = self.out.rearrange("(qb p) d -> qb p d", p=P)
        for qb in range(NQB):
            sem = self.so[qb % 4]
            pre = sem.n
            sem.inc(16)
            def th(sy, qb=qb, pre=pre, sem=sem, w=self.last_dne[qb]):
                sy.wait_ge(self.dv.h, w)
                sy.wait_ge(sem.h, pre)
                sy.dma_start(osrc[qb], self.out_acc.ap()[:, qb * D:(qb + 1) * D]).then_inc(sem.h, 16)
            self.y(th)
        def fin(sy, targets=tuple((i, self.so[i].n) for i in range(4))):
            for i, tv in targets:
                sy.wait_ge(self.so[i].h, tv)
        self.y(fin)


def rope_host_tables():
    inv_freq = 1.0 / 10000.0 ** (np.arange(0, PD, 2, dtype=np.float32) / PD)
    t = np.arange(S, dtype=np.float32)
    freqs = t[:, None] * inv_freq[None, :]
    emb = np.concatenate([freqs, freqs], axis=-1)
    return (np.ascontiguousarray(np.cos(emb).T.astype(np.float32)),
            np.ascontiguousarray(np.sin(emb).T.astype(np.float32)))


def _pack_rwsg(inputs):
    rw = np.concatenate([np.asarray(inputs["L0_ffn_router_weight"], dtype=np.float32).T,
                         np.asarray(inputs["L0_ffn_shared_gate_w"], dtype=np.float32).T], axis=1)
    return np.ascontiguousarray(rw.reshape(NKT, P, 9).transpose(1, 0, 2).reshape(P, NKT * 9))


def make_in_maps(inputs):
    x = np.ascontiguousarray(np.asarray(inputs["L0_attn_hidden_states"], dtype=np.float32).reshape(S, D))
    cosk, sink = rope_host_tables()
    rtm = np.zeros((PD, PD), dtype=np.float32)
    for i in range(32):
        rtm[i + 32, i] = -1.0
        rtm[i, i + 32] = 1.0
    common = {
        "x_full": x,
        "wdkv": np.ascontiguousarray(inputs["L0_attn_W_DKV"], dtype=np.float32),
        "wdq": np.ascontiguousarray(inputs["L0_attn_W_DQ"], dtype=np.float32),
        "wkr": np.ascontiguousarray(inputs["L0_attn_W_KR"], dtype=np.float32),
        "wuk": np.ascontiguousarray(inputs["L0_attn_W_UK"], dtype=np.float32),
        "wuv": np.ascontiguousarray(inputs["L0_attn_W_UV"], dtype=np.float32),
        "wuq": np.ascontiguousarray(inputs["L0_attn_W_UQ"], dtype=np.float32),
        "wqr": np.ascontiguousarray(inputs["L0_attn_W_QR"], dtype=np.float32),
        "wo": np.ascontiguousarray(inputs["L0_attn_W_O"], dtype=np.float32),
        "cosk": cosk, "sink": sink,
        "rt": rtm,
        "ident": np.eye(P, dtype=np.float32),
        "rwsg": _pack_rwsg(inputs),
        "shgt": np.ascontiguousarray(np.asarray(inputs["L0_ffn_sh_gate_w"], dtype=np.float32).T),
        "shupt": np.ascontiguousarray(np.asarray(inputs["L0_ffn_sh_up_w"], dtype=np.float32).T),
        "shdt": np.ascontiguousarray(np.asarray(inputs["L0_ffn_sh_down_w"], dtype=np.float32).T),
    }
    gup = np.asarray(inputs["L0_ffn_gate_up_proj"], dtype=np.float32)
    dwn = np.asarray(inputs["L0_ffn_down_proj"], dtype=np.float32)
    for e in range(E_):
        common[f"gupt{e}"] = np.ascontiguousarray(gup[e].T)
        common[f"downt{e}"] = np.ascontiguousarray(dwn[e].T)
    in_maps = []
    for c in range(NCORES):
        m = dict(common)
        m["x_own"] = np.ascontiguousarray(x[c * TOK:(c + 1) * TOK])
        m["cosq"] = np.ascontiguousarray(cosk[:, c * TOK:(c + 1) * TOK])
        m["sinq"] = np.ascontiguousarray(sink[:, c * TOK:(c + 1) * TOK])
        in_maps.append(m)
    return in_maps


def kernel(**inputs):
    k = K(debug=0)
    nc = k.build()
    in_maps = make_in_maps(inputs)
    res = run_bass_kernel_spmd(nc, in_maps, core_ids=list(range(NCORES)))
    out = np.concatenate([res.results[c]["out"] for c in range(NCORES)], axis=0)
    return out.reshape(1, S, D).astype(np.float32)


# version 22
# speedup vs baseline: 1.0106x; 1.0106x over previous
import numpy as np
from contextlib import ExitStack
import concourse.bass as bass
import concourse.mybir as mybir
from concourse.bass_utils import run_bass_kernel_spmd

F32 = mybir.dt.float32
BF16 = mybir.dt.bfloat16
AF = mybir.ActivationFunctionType
ALU = mybir.AluOpType
AX = mybir.AxisListType

NCORES = 8
P = 128
S = 8192
D = 2048
TOK = S // NCORES
LAT = 512
H, HD, PD = 16, 128, 64
NKT = D // P
NMT = LAT // P
NKEY = S // P
NST = S // 512
NQB = TOK // P
NQC = TOK // 512
E_ = 8
IM = 1408
IMT = IM // P
ISH = 5632
ATT_SCALE = float(1.0 / np.sqrt(HD + PD))
EPS = 1e-5

SBUF_LO = 16512
SBUF_HI = 229344
KB = 1024

PH = {"p1": 0, "at": 1, "p3": 2, "p4": 3}


def _dtsize(dt):
    return {F32: 4, BF16: 2}[dt]


class Lay:
    def __init__(self, nc):
        self.nc = nc
        self.recs = []

    def at(self, name, shape, dtype, off, ph0, ph1=None):
        ph1 = ph0 if ph1 is None else ph1
        a, b = PH[ph0], PH[ph1]
        size = int(np.prod(shape[1:])) * _dtsize(dtype)
        off = int(off)
        assert off % 64 == 0, f"{name} misaligned {off}"
        assert off >= SBUF_LO and off + size <= SBUF_HI, f"{name}: {off}..{off + size} out of range"
        for (n2, o2, s2, a2, b2) in self.recs:
            if off < o2 + s2 and o2 < off + size and a <= b2 and a2 <= b:
                raise AssertionError(f"SBUF overlap: {name}[{ph0}-{ph1}] vs {n2}")
        self.recs.append((name, off, size, a, b))
        return self.nc.alloc_sbuf_tensor_at(name, shape, dtype, offset=off)


class Ctr:
    def __init__(self, handle):
        self.h = handle
        self.n = 0

    def inc(self, k=1):
        self.n += k
        return self.n


class K:
    DBG_H = 3

    def __init__(self, debug=0):
        self.debug = debug
        self.nc = bass.Bass("TRN2", target_bir_lowering=False)
        self.ops = {e: [] for e in ("tensor", "vector", "scalar", "gpsimd", "sync")}
        self.bankfree = {}
        self.last_dne = [0] * NQB
        self._dw_u = 0
        self._xnb_pe = [None, None]
        self._krc_dv = [None, None]
        self._pend_epilogue = None

    def t(self, fn): self.ops["tensor"].append(fn)
    def v(self, fn): self.ops["vector"].append(fn)
    def s(self, fn): self.ops["scalar"].append(fn)
    def g(self, fn): self.ops["gpsimd"].append(fn)
    def y(self, fn): self.ops["sync"].append(fn)

    def ctr(self, eng):
        return {"tensor": self.pe, "vector": self.dv, "scalar": self.ac,
                "gpsimd": self.gp}[eng]

    def dma(self, engq, sem, out_ap, in_ap, waits=()):
        pre = sem.n
        t = sem.inc(16)
        def th(e, pre=pre, out_ap=out_ap, in_ap=in_ap, waits=tuple(waits), sem=sem):
            for eng, val in waits:
                e.wait_ge(self.ctr(eng).h, val)
            e.wait_ge(sem.h, pre)
            e.dma_start(out_ap, in_ap).then_inc(sem.h, 16)
        self.ops[engq].append(th)
        return t

    def wait_bf(self, e, f):
        if f is not None:
            e.wait_ge(self.ctr(f[0]).h, f[1])

    def build(self):
        nc = self.nc
        self.declare_io()
        self.alloc()
        with ExitStack() as stack:
            for nm in ("pe", "dv", "ac", "gp", "swa", "swo"):
                setattr(self, nm, Ctr(stack.enter_context(nc.semaphore(nm))))
            for nm in ("sckc", "scks", "suk", "suv", "suq", "sqr", "sgw", "suw"):
                setattr(self, nm, [Ctr(stack.enter_context(nc.semaphore(f"{nm}{i}"))) for i in range(2)])
            self.sx = [Ctr(stack.enter_context(nc.semaphore(f"sx{i}"))) for i in range(5)]
            self.sdw = [Ctr(stack.enter_context(nc.semaphore(f"sdw{i}"))) for i in range(4)]
            self.so = [Ctr(stack.enter_context(nc.semaphore(f"so{i}"))) for i in range(4)]
            block = stack.enter_context(nc.Block())
            self.emit_all()
            for eng in ("tensor", "vector", "scalar", "gpsimd", "sync"):
                lst = self.ops[eng]
                getattr(block, eng)(lambda e, _l=lst: [f(e) for f in _l])
        return nc

    def declare_io(self):
        nc = self.nc
        di = {}
        def inp(name, shape, dtype=F32):
            di[name] = nc.dram_tensor(name, shape, dtype, kind="ExternalInput").ap()
        inp("x_full", [S, D]); inp("x_own", [TOK, D])
        inp("wdkv", [D, LAT]); inp("wdq", [D, LAT]); inp("wkr", [D, PD])
        inp("wuk", [LAT, D]); inp("wuv", [LAT, D]); inp("wuq", [LAT, D])
        inp("wqr", [LAT, H * PD]); inp("wo", [D, D])
        inp("cosk", [PD, S]); inp("sink", [PD, S])
        inp("cosq", [PD, TOK]); inp("sinq", [PD, TOK])
        inp("rt", [PD, PD]); inp("ident", [P, P]); inp("rwsg", [P, NKT * 9])
        for e in range(E_):
            inp(f"gupt{e}", [D, 2 * IM])
            inp(f"downt{e}", [IM, D])
        inp("shgt", [D, ISH]); inp("shupt", [D, ISH]); inp("shdt", [ISH, D])
        self.io = di
        self.out = nc.dram_tensor("out", [TOK, D], F32, kind="ExternalOutput").ap()
        self.dbg = {}
        if self.debug:
            def dout(name, shape, dtype=F32):
                self.dbg[name] = nc.dram_tensor(name, shape, dtype, kind="ExternalOutput").ap()
            dout("d_ckvt", [P, NMT * S], BF16)
            dout("d_krt", [PD + 1, S], BF16)
            dout("d_cqt", [P, NMT * TOK], BF16)
            dout("d_kct", [P, S], BF16)
            dout("d_qct", [P, TOK], BF16)
            dout("d_qrt", [PD + 1, TOK], BF16)
            dout("d_vt", [P, 32 * 130], BF16)
            dout("d_ot", [P, H * TOK], BF16)
            dout("d_accp3", [P, NQB * D])
            dout("d_accp4s", [P, NQB * D])
            dout("d_comb", [P, NQB * 8])
            dout("d_sig", [P, NQB])
            dout("d_xnt", [P, NKT * TOK], BF16)
            dout("d_shact", [P, IMT * TOK], BF16)
            dout("d_bnd", [1, 2])

    def emit_dbg(self, pairs, waits):
        if not self.debug:
            return
        sem = self.so[3]
        for name, src in pairs:
            pre = sem.n
            sem.inc(16)
            def th(sy, name=name, src=src, pre=pre, waits=tuple(waits), sem=sem):
                for eng, val in waits:
                    sy.wait_ge(self.ctr(eng).h, val)
                sy.wait_ge(sem.h, pre)
                sy.dma_start(self.dbg[name], src).then_inc(sem.h, 16)
            self.y(th)
        tso = sem.n
        pre_dv = self.dv.n
        tdv = self.dv.inc()
        self.v(lambda v, w=tso, p=pre_dv: (v.wait_ge(sem.h, w), v.wait_ge(self.dv.h, p),
                                           v.memset(self.dbg_v[:, :], 0.0).then_inc(self.dv.h, 1)))
        tpe = self.pe.inc()
        self.t(lambda t, w=tso: (t.wait_ge(sem.h, w), t.nop().then_inc(self.pe.h, 1)))
        pre_ac = self.ac.n
        tac = self.ac.inc()
        self.s(lambda s, w=tso, p=pre_ac: (s.wait_ge(sem.h, w), s.wait_ge(self.ac.h, p),
                                           s.activation(self.dbg_s[:, :], self.eps[:, :], AF.Copy).then_inc(self.ac.h, 1)))

    def alloc(self):
        nc = self.nc
        lay = Lay(nc)
        self.lay = lay
        cur = [SBUF_LO]
        def seq(name, shape, dtype):
            cur[0] = (cur[0] + 63) // 64 * 64
            t = lay.at(name, shape, dtype, cur[0], "p1", "p4")
            cur[0] += int(np.prod(shape[1:])) * _dtsize(dtype)
            return t
        self.eps = seq("eps", [P, 1], F32)
        self.ones128f = seq("ones128f", [P, 1], F32)
        self.onesrow = seq("onesrow", [1, 1024], BF16)
        self.ident_bf = seq("ident_bf", [P, P], BF16)
        self.rt_bf = seq("rt_bf", [PD, PD], BF16)
        self.bn_a = seq("bn_a", [P, 96], F32)
        self.mv2_a = seq("mv2_a", [P, 8], F32)
        self.mean_a = seq("mean_a", [P, 4], F32)
        self.sq_a = seq("sq_a", [P, 4], F32)
        self.rstd_a4 = seq("rstd_a4", [P, 4], F32)
        self.negm_a4 = seq("negm_a4", [P, 4], F32)
        self.kabs = seq("kabs", [P, 1], F32)
        self.qabs = seq("qabs", [P, 1], F32)
        self.qabs2 = seq("qabs2", [P, 1], F32)
        self.kr2s = seq("kr2s", [1, 1], F32)
        self.bsum = seq("bsum", [1, 2], F32)
        self.negb = seq("negb", [1, 1], F32)
        self.rd = seq("rd", [P, 1], F32)
        self.combine = seq("combine", [P, NQB * 8], F32)
        self.dbg_v = seq("dbg_v", [P, 1], F32)
        self.dbg_s = seq("dbg_s", [P, 1], F32)
        self.sig = seq("sig", [P, NQB], F32)
        self.negm_sv = seq("negm_sv", [P, NQB], F32)
        self.rstd_sv = seq("rstd_sv", [P, NQB], F32)
        A = (cur[0] + 1023) // 1024 * 1024
        self.wdkv_b = lay.at("wdkv_b", [P, NKT * LAT], BF16, A, "p1")
        self.wkr_b = lay.at("wkr_b", [P, NKT * PD], BF16, A + 16 * KB, "p1")
        self.coskb = [lay.at(f"coskb{i}", [PD, 512], F32, A + 18 * KB + 2 * KB * i, "p1") for i in range(2)]
        self.sinkb = [lay.at(f"sinkb{i}", [PD, 512], F32, A + 22 * KB + 2 * KB * i, "p1") for i in range(2)]
        self.ropet1 = lay.at("ropet1", [PD, 512], F32, A + 26 * KB, "p1", "at")
        self.ropet2 = lay.at("ropet2", [PD, 512], F32, A + 28 * KB, "p1", "at")
        self.kr_bf = lay.at("kr_bf", [PD, 512], BF16, A + 30 * KB, "p1")
        self.wdq_b = lay.at("wdq_b", [P, NKT * LAT], BF16, A + 31 * KB, "p1")
        self.xn_bf = [lay.at(f"xn_bf{i}", [P, D], BF16, A + 47 * KB + 4 * KB * i, "p1", "p4") for i in range(2)]
        AEND = A + 55 * KB
        self.ukb = [lay.at(f"ukb{i}", [P, NMT * P], BF16, A + KB * i, "at") for i in range(2)]
        self.uvb = [lay.at(f"uvb{i}", [P, NMT * P], BF16, A + 2 * KB + KB * i, "at") for i in range(2)]
        self.uqb = [lay.at(f"uqb{i}", [P, NMT * P], BF16, A + 4 * KB + KB * i, "at") for i in range(2)]
        self.qrb = [lay.at(f"qrb{i}", [P, NMT * PD], BF16, A + 6 * KB + 512 * i, "at") for i in range(2)]
        self.e_bf = ([lay.at(f"e_bf{i}", [P, 512], BF16, A + 7 * KB + KB * i, "at") for i in range(2)]
                     + [lay.at(f"e_bf{i+2}", [P, 512], BF16, A + 24 * KB + KB * i, "at") for i in range(2)])
        self.o_bf = lay.at("o_bf", [P, P], BF16, A + 9 * KB, "at")
        self.qct = lay.at("qct", [P, TOK], BF16, A + 10 * KB, "at")
        self.qrt = lay.at("qrt", [P, TOK], BF16, A + 12 * KB, "at")
        self.qr_bf = lay.at("qr_bf", [PD, 512], BF16, A + 14 * KB, "at")
        self.cosq = lay.at("cosq", [PD, TOK], F32, A + 16 * KB, "at")
        self.sinq = lay.at("sinq", [PD, TOK], F32, A + 20 * KB, "at")
        self.oacc = lay.at("oacc", [P, 8 * 132], F32, A + 31 * KB, "at")
        self.xwo = [lay.at(f"xwo{i}", [P, TOK], F32, A + 4 * KB * i, "p3") for i in range(4)]
        self.gwb = [lay.at(f"gwb{i}", [P, NKT * P], BF16, A + 4 * KB * i, "p4") for i in range(2)]
        self.uwb = [lay.at(f"uwb{i}", [P, NKT * P], BF16, A + 8 * KB + 4 * KB * i, "p4") for i in range(2)]
        self.dwb = ([lay.at(f"dwb{i}", [P, 2048], BF16, A + 16 * KB + 4 * KB * i, "p4") for i in range(2)]
                    + [lay.at(f"dwb{i+2}", [P, 2048], BF16, A + 32 * KB + 4 * KB * i, "p4") for i in range(2)])
        self.rwsg_b = lay.at("rwsg_b", [P, NKT * 9], BF16, A + 24 * KB, "p4")
        self.probs_all = lay.at("probs_all", [P, NQB * 8], F32, A + 25 * KB, "p4").ap()
        self.mask_all = lay.at("mask_all", [P, NQB * 8], F32, A + 25 * KB + 256, "p4").ap()
        self.keep_all = lay.at("keep_all", [P, NQB * 8], F32, A + 25 * KB + 512, "p4").ap()
        self.m1s = lay.at("m1s", [P, NQB], F32, A + 25 * KB + 768, "p4").ap()
        self.m2s = lay.at("m2s", [P, NQB], F32, A + 25 * KB + 832, "p4").ap()
        self.d8 = lay.at("d8", [P, NQB], F32, A + 25 * KB + 896, "p4").ap()
        self.r8 = lay.at("r8", [P, NQB], F32, A + 25 * KB + 960, "p4").ap()
        self.silu_t = [lay.at(f"silu_t{i}", [P, 512], F32, A + 28 * KB + 2 * KB * i, "p4") for i in range(2)]
        R = AEND
        self.xbuf = lay.at("xbuf", [P, 5 * D], F32, R, "p1")
        self.oT = lay.at("oT", [P, H * TOK], BF16, R, "at", "p3")
        self.xnt = lay.at("xnt", [P, NKT * TOK], BF16, R, "p4")
        self.xnt_scr = lay.at("xnt_scr", [P, NKT * 512], BF16, R + 40 * KB, "p1")
        self.kct = lay.at("kct", [P, S], BF16, R + 32 * KB, "at")
        self.wo_b = lay.at("wo_b", [P, NKT * TOK], BF16, R + 32 * KB, "p3")
        self.shact = lay.at("shact", [P, IMT * TOK], BF16, R + 32 * KB, "p4")
        self.vbuf = lay.at("vbuf", [P, 32 * 130], BF16, R + 48 * KB, "at")
        self.actt = lay.at("actt", [P, IMT * TOK], BF16, R + 54 * KB, "p4")
        CK = R + 57 * KB
        self.ckvt = lay.at("ckvt", [P, NMT * S], BF16, CK, "p1", "at")
        self.krt = lay.at("krt", [P, S], BF16, CK + 64 * KB, "p1", "at")
        self.cqt = lay.at("cqt", [P, NMT * TOK], BF16, CK + 80 * KB, "p1", "at")
        assert CK + 88 * KB <= SBUF_HI
        OA = (SBUF_HI - 64 * KB) // 64 * 64
        self.out_acc = lay.at("out_acc", [P, NQB * D], F32, OA, "p3", "p4")
        assert OA >= R + 76 * KB, "out_acc overlaps p4 actt"
        assert OA >= R + 64 * KB, "out_acc overlaps p3 wo_b"
        self.pb = [nc.alloc_psum_tensor(f"pb{i}", [P, 512], F32) for i in range(8)]
        self.pb_bf = [b.ap().bitcast(BF16) for b in self.pb]

    def emit_all(self):
        self.emit_init()
        self.emit_stage_a()
        self.emit_cq()
        if self.debug < 9:
            self.emit_attention()
        if self.debug < 8:
            self.emit_wo()
            self.emit_ffn_pre()
            self.emit_shared()
            self.emit_moe()
        self.emit_store()

    def emit_init(self):
        io = self.io
        def ms(v):
            v.memset(self.eps[:, :], EPS)
            v.memset(self.ones128f[:, :], 1.0)
            v.memset(self.onesrow[:, :], 1.0)
            v.memset(self.krt.ap()[64:65, :], 1.0).then_inc(self.dv.h, 1)
        self.init_dv = self.dv.inc()
        self.v(ms)
        self.dma("gpsimd", self.swa, self.ident_bf.ap()[:, :], io["ident"][:, :])
        self.rt_ld = self.dma("gpsimd", self.swa, self.rt_bf.ap()[:, :], io["rt"][:, :])
        self.dma("gpsimd", self.swa,
                 self.wdkv_b.ap().rearrange("p (kt n) -> p kt n", n=LAT),
                 io["wdkv"].rearrange("(kt p) n -> p kt n", p=P))
        self.dma("gpsimd", self.swa,
                 self.wkr_b.ap().rearrange("p (kt n) -> p kt n", n=PD),
                 io["wkr"].rearrange("(kt p) n -> p kt n", p=P))
        self.wdq_ld = self.dma("gpsimd", self.swa,
                               self.wdq_b.ap().rearrange("p (kt n) -> p kt n", n=LAT),
                               io["wdq"].rearrange("(kt p) n -> p kt n", p=P))
        self.wa_ld = self.wdq_ld

    def emit_ln_bn(self, xap, slot, xwait=None):
        dv = self.dv
        j = slot % 4
        bn = self.bn_a[:, 24 * j:24 * (j + 1)]
        pre = dv.n
        def st_(v, xap=xap, bn=bn, pre=pre, xw=xwait):
            v.wait_ge(self.dv.h, pre)
            if xw is not None:
                v.wait_ge(xw[0].h, xw[1])
            v.bn_stats(bn[:, 0:6], xap[:, 0:512])
            v.bn_stats(bn[:, 6:12], xap[:, 512:1024])
            v.bn_stats(bn[:, 12:18], xap[:, 1024:1536])
            v.bn_stats(bn[:, 18:24], xap[:, 1536:2048]).then_inc(self.dv.h, 1)
        t = dv.inc()
        self.v(st_)
        return t

    def emit_ln_mv4(self, t_bns, save_st=None):
        dv, ac = self.dv, self.ac
        if save_st is None:
            rstd_a, negm_a = self.rstd_a4[:, 0:4], self.negm_a4[:, 0:4]
        else:
            rstd_a = self.rstd_sv[:, 4 * save_st:4 * save_st + 4]
            negm_a = self.negm_sv[:, 4 * save_st:4 * save_st + 4]
        negm_cols = [negm_a[:, tt:tt + 1] for tt in range(4)]
        pre = dv.n
        def agg(v, w=t_bns, pre=pre):
            v.wait_ge(self.dv.h, max(w, pre))
            for tt in range(3):
                v.bn_aggr(self.mv2_a[:, 2 * tt:2 * tt + 2], self.bn_a[:, 24 * tt:24 * tt + 24])
            v.bn_aggr(self.mv2_a[:, 6:8], self.bn_a[:, 72:96]).then_inc(self.dv.h, 1)
        tagg = dv.inc()
        self.v(agg)
        pre_ac = ac.n
        tac = ac.inc()
        def sq4(s, w=tagg, pre=pre_ac):
            s.wait_ge(self.dv.h, w)
            s.wait_ge(self.ac.h, pre)
            for tt in range(3):
                s.activation(self.sq_a[:, tt:tt + 1], self.mv2_a[:, 2 * tt + 1:2 * tt + 2],
                             AF.Sqrt, bias=self.eps[:, 0:1])
            s.activation(self.sq_a[:, 3:4], self.mv2_a[:, 7:8],
                         AF.Sqrt, bias=self.eps[:, 0:1]).then_inc(self.ac.h, 1)
        self.s(sq4)
        def fin(v, w=tac, rstd_a=rstd_a, negm_cols=tuple(negm_cols)):
            v.wait_ge(self.ac.h, w)
            v.reciprocal(rstd_a, self.sq_a[:, 0:4])
            for tt in range(3):
                v.tensor_scalar(negm_cols[tt], self.mv2_a[:, 2 * tt:2 * tt + 1], -1.0, None, ALU.mult)
            v.tensor_scalar(negm_cols[3], self.mv2_a[:, 6:7], -1.0, None, ALU.mult).then_inc(self.dv.h, 1)
        t_fin = dv.inc()
        self.v(fin)
        return rstd_a, negm_a, t_fin

    def emit_ln_apply(self, xap, xn_out, rstd, negm, t_fin):
        dv = self.dv
        def apply(v, xap=xap, xn_out=xn_out, w=t_fin, rstd=rstd, negm=negm):
            v.wait_ge(self.dv.h, w)
            v.tensor_scalar(xn_out[:, :], xap[:, :], negm, rstd,
                            ALU.add, ALU.mult).then_inc(self.dv.h, 1)
        t2 = dv.inc()
        self.v(apply)
        return t2

    def emit_transpose16(self, xn_ap, dst_cols, wait_dv, evac="vector", banks=(2, 3)):
        pe = self.pe
        for half in range(2):
            ev = evac if evac != "mixed" else ("vector" if half == 0 else "scalar")
            bank = banks[half]
            def dotp(t, xn_ap=xn_ap, half=half, bank=bank, wdv=wait_dv, fb=self.bankfree.get(bank)):
                t.wait_ge(self.dv.h, wdv)
                t.wait_ge(self.swa.h, 16)
                self.wait_bf(t, fb)
                for i in range(7):
                    kt = half * 8 + i
                    t.transpose(self.pb_bf[bank][:, P * i:P * (i + 1)],
                                xn_ap[:, P * kt:P * (kt + 1)], self.ident_bf.ap()[:, :])
                t.transpose(self.pb_bf[bank][:, P * 7:P * 8],
                            xn_ap[:, P * (half * 8 + 7):P * (half * 8 + 8)],
                            self.ident_bf.ap()[:, :]).then_inc(self.pe.h, 1)
            tpe = pe.inc()
            self.t(dotp)
            if ev == "vector":
                def doev(v, half=half, bank=bank, dst_cols=dst_cols, w=tpe):
                    v.wait_ge(self.pe.h, w)
                    for i in range(7):
                        v.tensor_copy(dst_cols[half * 8 + i], self.pb_bf[bank][:, P * i:P * (i + 1)])
                    v.tensor_copy(dst_cols[half * 8 + 7], self.pb_bf[bank][:, P * 7:P * 8]).then_inc(self.dv.h, 1)
                tdv = self.dv.inc()
                self.v(doev)
                self.bankfree[bank] = ("vector", tdv)
            else:
                def doev(s, half=half, bank=bank, dst_cols=dst_cols, w=tpe, pre=self.ac.n):
                    s.wait_ge(self.pe.h, w)
                    s.wait_ge(self.ac.h, pre)
                    for i in range(7):
                        s.copy(dst_cols[half * 8 + i], self.pb_bf[bank][:, P * i:P * (i + 1)])
                    s.copy(dst_cols[half * 8 + 7], self.pb_bf[bank][:, P * 7:P * 8]).then_inc(self.ac.h, 1)
                tac = self.ac.inc()
                self.s(doev)
                self.bankfree[bank] = ("scalar", tac)
        return pe.n, (self.dv.n if evac == "vector" else None)

    def emit_stage_a(self):
        io = self.io
        xsrc = io["x_full"].rearrange("(g p) d -> g p d", p=P)
        self._stage_tokens(xsrc, 2 * NST, True)

    def _stage_tokens(self, xsrc, nhalves, is_full):
        nst = nhalves // 2
        xfree = [self.dv.n] * 5
        stats_p = {}
        mvd = {}
        def emit_ls(st):
            for tt in range(4):
                gi = st * 4 + tt
                slot = gi % 5
                tsx = self.dma("sync", self.sx[slot],
                               self.xbuf.ap().rearrange("p (f d) -> p f d", d=D)[:, slot, :],
                               xsrc[gi], [("vector", xfree[slot])])
                xap = self.xbuf.ap().rearrange("p (f d) -> p f d", d=D)[:, slot, :]
                stats_p[gi] = [tt, xap, tsx, slot]
        def emit_bn(gi):
            tt, xap, tsx, slot = stats_p[gi]
            return self.emit_ln_bn(xap, tt, xwait=(self.sx[slot], tsx))
        def emit_mv4(st, t_bns):
            mvd[st] = self.emit_ln_mv4(t_bns, save_st=(None if is_full else st))
        emit_ls(0)
        for tt in range(3):
            emit_bn(tt)
        emit_mv4(0, emit_bn(3))
        for st in range(nst):
            rstd_a, negm_a, t_fin = mvd.pop(st)
            for tt in range(4):
                gi = st * 4 + tt
                ptt, pxap, _, pslot = stats_p.pop(gi)
                self._flush_ln(ptt, pxap, rstd_a[:, tt:tt + 1], negm_a[:, tt:tt + 1], t_fin)
                xfree[pslot] = self.dv.n
            xnt_ready = ("vector", self.dv.n, "scalar", self.ac.n)
            tcb = None
            if st + 1 < nst:
                emit_ls(st + 1)
                def tile_cb(mt, st=st):
                    if mt < 2:
                        emit_bn((st + 1) * 4 + mt)
                    else:
                        emit_bn((st + 1) * 4 + 2)
                        emit_mv4(st + 1, emit_bn((st + 1) * 4 + 3))
                tcb = tile_cb
            if is_full:
                self._stagea_proj(st, xnt_ready, tile_cb=tcb)
            else:
                self._cq_proj(st, xnt_ready)
                if tcb is not None:
                    for mt in range(3):
                        tcb(mt)

    def _flush_ln(self, tt, xap, rstd, negm, t_fin):
        xnb = self.xn_bf[tt % 2]
        prev_pe = self._xnb_pe[tt % 2]
        if prev_pe is not None:
            self.v(lambda v, w=prev_pe: v.wait_ge(self.pe.h, w))
        wdv = self.emit_ln_apply(xap, xnb.ap(), rstd, negm, t_fin)
        dst = [self.xnt_scr.ap()[:, kt * 512 + tt * P: kt * 512 + tt * P + P] for kt in range(NKT)]
        pec, dvc = self.emit_transpose16(xnb.ap(), dst, wdv, evac="mixed",
                                         banks=((2, 3) if tt % 2 == 0 else (4, 5)))
        self._xnb_pe[tt % 2] = pec

    def _stagea_proj(self, st, xnt_ready, tile_cb=None):
        io, pe, dv = self.io, self.pe, self.dv
        slot = st % 2
        kw = [] if self._krc_dv[slot] is None else [("vector", self._krc_dv[slot])]
        t_c = self.dma("gpsimd", self.sckc[slot], self.coskb[slot].ap()[:, :],
                       io["cosk"][:, 512 * st:512 * (st + 1)], kw)
        t_s = self.dma("gpsimd", self.scks[slot], self.sinkb[slot].ap()[:, :],
                       io["sink"][:, 512 * st:512 * (st + 1)], kw)
        def krmm(t, st=st, w=xnt_ready, wld=self.wa_ld, fb=self.bankfree.get(4)):
            t.wait_ge(self.dv.h, w[1])
            t.wait_ge(self.ac.h, w[3])
            self.wait_bf(t, fb)
            t.wait_ge(self.swa.h, wld)
            for kt in range(NKT - 1):
                t.matmul(self.pb[4][0:PD, :],
                         self.wkr_b.ap()[:, kt * PD:(kt + 1) * PD],
                         self.xnt_scr.ap()[:, kt * 512:(kt + 1) * 512],
                         start=(kt == 0), stop=False)
            kt = NKT - 1
            t.matmul(self.pb[4][0:PD, :],
                     self.wkr_b.ap()[:, kt * PD:(kt + 1) * PD],
                     self.xnt_scr.ap()[:, kt * 512:(kt + 1) * 512],
                     start=False, stop=True).then_inc(self.pe.h, 1)
        tpe_kr = pe.inc()
        self.t(krmm)
        def krev(v, w=tpe_kr):
            v.wait_ge(self.pe.h, w)
            v.tensor_copy(self.kr_bf.ap()[:, :], self.pb[4][0:PD, :]).then_inc(self.dv.h, 1)
        tdv_krbf = dv.inc()
        self.v(krev)
        def krrot(t, w=tdv_krbf, wrt=self.rt_ld, fb=self.bankfree.get(5)):
            t.wait_ge(self.dv.h, w)
            self.wait_bf(t, fb)
            t.wait_ge(self.swa.h, wrt)
            t.matmul(self.pb[5][0:PD, :], self.rt_bf.ap()[:, :], self.kr_bf.ap()[:, :],
                     start=True, stop=True).then_inc(self.pe.h, 1)
        tpe_rot = pe.inc()
        self.t(krrot)
        pre_kr = dv.n
        def krcomb(v, st=st, slot=slot, w=tpe_rot, wc=t_c, ws=t_s, pre=pre_kr):
            v.wait_ge(self.pe.h, w)
            v.wait_ge(self.sckc[slot].h, wc)
            v.wait_ge(self.scks[slot].h, ws)
            v.wait_ge(self.dv.h, pre)
            v.tensor_tensor(self.ropet1.ap()[:, :], self.pb[4][0:PD, :], self.coskb[slot].ap()[:, :], ALU.mult)
            v.tensor_tensor(self.ropet2.ap()[:, :], self.pb[5][0:PD, :], self.sinkb[slot].ap()[:, :], ALU.mult).then_inc(self.dv.h, 1)
            v.wait_ge(self.dv.h, pre + 1)
            v.tensor_tensor(self.krt.ap()[0:PD, 512 * st:512 * (st + 1)],
                            self.ropet1.ap()[:, :], self.ropet2.ap()[:, :], ALU.add).then_inc(self.dv.h, 1)
        dv.inc()
        tdv_kr = dv.inc()
        self.v(krcomb)
        self._krc_dv[slot] = tdv_kr
        self.bankfree[4] = ("vector", tdv_kr)
        self.bankfree[5] = ("vector", tdv_kr)
        for mt in range(NMT):
            bank = mt % 2
            def ckv(t, st=st, mt=mt, bank=bank, w=xnt_ready, wld=self.wa_ld, fb=self.bankfree.get(bank)):
                t.wait_ge(self.dv.h, w[1])
                t.wait_ge(self.ac.h, w[3])
                self.wait_bf(t, fb)
                t.wait_ge(self.swa.h, wld)
                for kt in range(NKT - 1):
                    t.matmul(self.pb[bank][:, :],
                             self.wdkv_b.ap()[:, kt * LAT + mt * P: kt * LAT + (mt + 1) * P],
                             self.xnt_scr.ap()[:, kt * 512:(kt + 1) * 512],
                             start=(kt == 0), stop=False)
                kt = NKT - 1
                t.matmul(self.pb[bank][:, :],
                         self.wdkv_b.ap()[:, kt * LAT + mt * P: kt * LAT + (mt + 1) * P],
                         self.xnt_scr.ap()[:, kt * 512:(kt + 1) * 512],
                         start=False, stop=True).then_inc(self.pe.h, 1)
            tpe = pe.inc()
            self.t(ckv)
            def ckve(v, st=st, mt=mt, bank=bank, w=tpe):
                v.wait_ge(self.pe.h, w)
                v.tensor_copy(self.ckvt.ap()[:, mt * S + st * 512: mt * S + st * 512 + 512],
                              self.pb[bank][:, :]).then_inc(self.dv.h, 1)
            tdv = dv.inc()
            self.v(ckve)
            self.bankfree[bank] = ("vector", tdv)
            if tile_cb is not None and mt < 3:
                tile_cb(mt)

    def _cq_proj(self, j, xnt_ready):
        pe, dv = self.pe, self.dv
        for mt in range(NMT):
            bank = mt % 2
            def cq(t, j=j, mt=mt, bank=bank, w=xnt_ready, wld=self.wdq_ld, fb=self.bankfree.get(bank)):
                t.wait_ge(self.dv.h, w[1])
                t.wait_ge(self.ac.h, w[3])
                self.wait_bf(t, fb)
                t.wait_ge(self.swa.h, wld)
                for kt in range(NKT - 1):
                    t.matmul(self.pb[bank][:, :],
                             self.wdq_b.ap()[:, kt * LAT + mt * P: kt * LAT + (mt + 1) * P],
                             self.xnt_scr.ap()[:, kt * 512:(kt + 1) * 512],
                             start=(kt == 0), stop=False)
                kt = NKT - 1
                t.matmul(self.pb[bank][:, :],
                         self.wdq_b.ap()[:, kt * LAT + mt * P: kt * LAT + (mt + 1) * P],
                         self.xnt_scr.ap()[:, kt * 512:(kt + 1) * 512],
                         start=False, stop=True).then_inc(self.pe.h, 1)
            tpe = pe.inc()
            self.t(cq)
            def cqe(v, j=j, mt=mt, bank=bank, w=tpe):
                v.wait_ge(self.pe.h, w)
                v.tensor_copy(self.cqt.ap()[:, mt * TOK + j * 512: mt * TOK + j * 512 + 512],
                              self.pb[bank][:, :]).then_inc(self.dv.h, 1)
            tdv = dv.inc()
            self.v(cqe)
            self.bankfree[bank] = ("vector", tdv)

    def emit_cq(self):
        io = self.io
        xsrc = io["x_own"].rearrange("(g p) d -> g p d", p=P)
        self._stage_tokens(xsrc, 4, False)
        wkr = self.dv.n
        def krb(v, w=wkr):
            v.wait_ge(self.dv.h, w)
            v.reduce_max(self.kabs[0:PD, :], self.krt.ap()[0:PD, :], AX.X,
                         apply_absolute_value=True).then_inc(self.dv.h, 1)
        t1 = self.dv.inc()
        self.v(krb)
        t2 = self.ac.inc()
        self.s(lambda s, w=t1: (s.wait_ge(self.dv.h, w),
                                s.activation(self.kabs[0:PD, :], self.kabs[0:PD, :], AF.Square).then_inc(self.ac.h, 1)))
        def krmm(t, w=t2, fb=self.bankfree.get(0)):
            t.wait_ge(self.ac.h, w)
            self.wait_bf(t, fb)
            t.matmul(self.pb[0][0:1, 0:1], self.ones128f[0:PD, :], self.kabs[0:PD, :],
                     start=True, stop=True).then_inc(self.pe.h, 1)
        t3 = self.pe.inc()
        self.t(krmm)
        def kre(v, w=t3):
            v.wait_ge(self.pe.h, w)
            v.tensor_copy(self.kr2s[0:1, :], self.pb[0][0:1, 0:1]).then_inc(self.dv.h, 1)
        self.kr2s_dv = self.dv.inc()
        self.v(kre)
        self.bankfree[0] = ("vector", self.kr2s_dv)
        self.emit_dbg([("d_ckvt", self.ckvt.ap()[:, :]),
                       ("d_krt", self.krt.ap()[0:PD + 1, :]),
                       ("d_cqt", self.cqt.ap()[:, :])],
                      [("vector", self.dv.n)])

    def emit_attention(self):
        io = self.io
        p1_pe, p1_dv = self.pe.n, self.dv.n
        pre_v = self.dv.n
        def msv(v, wp=p1_pe, pre=pre_v):
            v.wait_ge(self.pe.h, wp)
            v.wait_ge(self.dv.h, pre)
            v.memset(self.vbuf.ap()[:, :], 0.0).then_inc(self.dv.h, 1)
            v.wait_ge(self.dv.h, pre + 1)
            ones_cols = self.vbuf.ap().rearrange("p (k c) -> p k c", c=130)[:, :, 128:129]
            v.memset(ones_cols, 1.0).then_inc(self.dv.h, 1)
        self.dv.inc(2)
        self.v(msv)
        self.cosq_ld = self.dma("gpsimd", self.sckc[0], self.cosq.ap()[:, :], io["cosq"][:, :],
                                [("tensor", p1_pe), ("vector", p1_dv)])
        self.sinq_ld = self.dma("gpsimd", self.scks[0], self.sinq.ap()[:, :], io["sinq"][:, :],
                                [("tensor", p1_pe), ("vector", p1_dv)])
        uk_src = io["wuk"].rearrange("(mt l) n -> l mt n", l=P)
        uv_src = io["wuv"].rearrange("(mt l) n -> l mt n", l=P)
        uq_src = io["wuq"].rearrange("(mt l) n -> l mt n", l=P)
        qr_src = io["wqr"].rearrange("(mt l) n -> l mt n", l=P)
        hwfree = [None, None]
        for h in range(H):
            par = h % 2
            if h == 0:
                waits = [("tensor", p1_pe), ("vector", p1_dv)]
            elif hwfree[par] is None:
                waits = []
            else:
                waits = [("tensor", hwfree[par])]
            wld = {}
            wld["uk"] = self.dma("gpsimd", self.suk[par],
                                 self.ukb[par].ap().rearrange("p (mt n) -> p mt n", n=P),
                                 uk_src[:, :, h * P:(h + 1) * P], waits)
            wld["uv"] = self.dma("gpsimd", self.suv[par],
                                 self.uvb[par].ap().rearrange("p (mt n) -> p mt n", n=P),
                                 uv_src[:, :, h * P:(h + 1) * P], waits)
            wld["uq"] = self.dma("gpsimd", self.suq[par],
                                 self.uqb[par].ap().rearrange("p (mt n) -> p mt n", n=P),
                                 uq_src[:, :, h * P:(h + 1) * P], waits)
            wld["qr"] = self.dma("gpsimd", self.sqr[par],
                                 self.qrb[par].ap().rearrange("p (mt n) -> p mt n", n=PD),
                                 qr_src[:, :, h * PD:(h + 1) * PD], waits)
            self.emit_head(h, wld, par)
            hwfree[par] = self._head_w_done
        if self._pend_epilogue is not None:
            self._pend_epilogue()
            self._pend_epilogue = None
        self.emit_dbg([("d_ot", self.oT.ap()[:, :])], [("vector", self.dv.n)])

    def emit_head(self, h, wld, par):
        pe, dv, ac = self.pe, self.dv, self.ac
        dbg_me = self.debug and h == self.DBG_H
        O_BANKS = (2, 3, 6, 7)
        for ch in range(16):
            bank = ch % 2
            def kup(t, h=h, ch=ch, bank=bank, par=par, w=wld["uk"], fb=self.bankfree.get(bank)):
                t.wait_ge(self.suk[par].h, w)
                self.wait_bf(t, fb)
                for mt in range(NMT - 1):
                    t.matmul(self.pb[bank][:, :],
                             self.ukb[par].ap()[:, mt * P:(mt + 1) * P],
                             self.ckvt.ap()[:, mt * S + ch * 512: mt * S + ch * 512 + 512],
                             start=(mt == 0), stop=False)
                mt = NMT - 1
                t.matmul(self.pb[bank][:, :],
                         self.ukb[par].ap()[:, mt * P:(mt + 1) * P],
                         self.ckvt.ap()[:, mt * S + ch * 512: mt * S + ch * 512 + 512],
                         start=False, stop=True).then_inc(self.pe.h, 1)
            tpe = pe.inc()
            self.t(kup)
            if ch % 2 == 0:
                def kupe(v, ch=ch, bank=bank, w=tpe):
                    v.wait_ge(self.pe.h, w)
                    v.tensor_copy(self.kct.ap()[:, ch * 512:(ch + 1) * 512],
                                  self.pb[bank][:, :]).then_inc(self.dv.h, 1)
                tdv = dv.inc()
                self.v(kupe)
                self.bankfree[bank] = ("vector", tdv)
            else:
                def kupe_s(s, ch=ch, bank=bank, w=tpe, pre=self.ac.n):
                    s.wait_ge(self.pe.h, w)
                    s.wait_ge(self.ac.h, pre)
                    s.copy(self.kct.ap()[:, ch * 512:(ch + 1) * 512],
                           self.pb[bank][:, :]).then_inc(self.ac.h, 1)
                tac = ac.inc()
                self.s(kupe_s)
                self.bankfree[bank] = ("scalar", tac)
        kct_dv, kct_ac = self.dv.n, self.ac.n
        if self._pend_epilogue is not None:
            self._pend_epilogue()
            self._pend_epilogue = None
        for ch in range(2):
            bank = ch % 2
            def qc(t, ch=ch, bank=bank, par=par, w=wld["uq"], fb=self.bankfree.get(bank)):
                t.wait_ge(self.suq[par].h, w)
                self.wait_bf(t, fb)
                for mt in range(NMT - 1):
                    t.matmul(self.pb[bank][:, :],
                             self.uqb[par].ap()[:, mt * P:(mt + 1) * P],
                             self.cqt.ap()[:, mt * TOK + ch * 512: mt * TOK + ch * 512 + 512],
                             start=(mt == 0), stop=False)
                mt = NMT - 1
                t.matmul(self.pb[bank][:, :],
                         self.uqb[par].ap()[:, mt * P:(mt + 1) * P],
                         self.cqt.ap()[:, mt * TOK + ch * 512: mt * TOK + ch * 512 + 512],
                         start=False, stop=True).then_inc(self.pe.h, 1)
            tpe = pe.inc()
            self.t(qc)
            def qce(v, ch=ch, bank=bank, w=tpe):
                v.wait_ge(self.pe.h, w)
                v.tensor_copy(self.qct.ap()[:, ch * 512:(ch + 1) * 512], self.pb[bank][:, :]).then_inc(self.dv.h, 1)
            tdv = dv.inc()
            self.v(qce)
            self.bankfree[bank] = ("vector", tdv)
        def qr_stage_a(ch):
            def qr(t, ch=ch, par=par, w=wld["qr"], fb=self.bankfree.get(4)):
                t.wait_ge(self.sqr[par].h, w)
                self.wait_bf(t, fb)
                for mt in range(NMT - 1):
                    t.matmul(self.pb[4][0:PD, :],
                             self.qrb[par].ap()[:, mt * PD:(mt + 1) * PD],
                             self.cqt.ap()[:, mt * TOK + ch * 512: mt * TOK + ch * 512 + 512],
                             start=(mt == 0), stop=False)
                mt = NMT - 1
                t.matmul(self.pb[4][0:PD, :],
                         self.qrb[par].ap()[:, mt * PD:(mt + 1) * PD],
                         self.cqt.ap()[:, mt * TOK + ch * 512: mt * TOK + ch * 512 + 512],
                         start=False, stop=True).then_inc(self.pe.h, 1)
            tpe_qr = pe.inc()
            self.t(qr)
            def qrev(v, w=tpe_qr):
                v.wait_ge(self.pe.h, w)
                v.tensor_copy(self.qr_bf.ap()[:, :], self.pb[4][0:PD, :]).then_inc(self.dv.h, 1)
            tdv_qrbf = dv.inc()
            self.v(qrev)
            return tdv_qrbf
        def qr_stage_b(ch, tdv_qrbf):
            def qrrot(t, w=tdv_qrbf, fb=self.bankfree.get(5)):
                t.wait_ge(self.dv.h, w)
                self.wait_bf(t, fb)
                t.matmul(self.pb[5][0:PD, :], self.rt_bf.ap()[:, :], self.qr_bf.ap()[:, :],
                         start=True, stop=True).then_inc(self.pe.h, 1)
            tpe_rot = pe.inc()
            self.t(qrrot)
            pre_qr = dv.n
            def qrcomb(v, ch=ch, w=tpe_rot, wc=self.cosq_ld, ws=self.sinq_ld, pre=pre_qr):
                v.wait_ge(self.pe.h, w)
                v.wait_ge(self.sckc[0].h, wc)
                v.wait_ge(self.scks[0].h, ws)
                v.wait_ge(self.dv.h, pre)
                v.tensor_tensor(self.ropet1.ap()[:, :], self.pb[4][0:PD, :],
                                self.cosq.ap()[:, ch * 512:(ch + 1) * 512], ALU.mult)
                v.tensor_tensor(self.ropet2.ap()[:, :], self.pb[5][0:PD, :],
                                self.sinq.ap()[:, ch * 512:(ch + 1) * 512], ALU.mult).then_inc(self.dv.h, 1)
                v.wait_ge(self.dv.h, pre + 1)
                v.tensor_tensor(self.qrt.ap()[0:PD, 512 * ch:512 * (ch + 1)],
                                self.ropet1.ap()[:, :], self.ropet2.ap()[:, :], ALU.add).then_inc(self.dv.h, 1)
            dv.inc()
            tdv_qr = dv.inc()
            self.v(qrcomb)
            self.bankfree[4] = ("vector", tdv_qr)
            self.bankfree[5] = ("vector", tdv_qr)
        def emit_qb():
            qdone = self.dv.n
            def qb_v(v, w=qdone):
                v.wait_ge(self.dv.h, w)
                v.reduce_max(self.qabs[:, :], self.qct.ap()[:, :], AX.X, apply_absolute_value=True)
                v.reduce_max(self.qabs2[0:PD, :], self.qrt.ap()[0:PD, :], AX.X,
                             apply_absolute_value=True).then_inc(self.dv.h, 1)
            t1q = dv.inc()
            self.v(qb_v)
            t2q = ac.inc()
            def qb_s(s, w=t1q):
                s.wait_ge(self.dv.h, w)
                s.activation(self.qabs[:, :], self.qabs[:, :], AF.Square)
                s.activation(self.qabs2[0:PD, :], self.qabs2[0:PD, :], AF.Square).then_inc(self.ac.h, 1)
            self.s(qb_s)
            return t2q
        def kb_chunk(j):
            def kb(v, j=j, wd=kct_dv, wa=kct_ac):
                v.wait_ge(self.dv.h, wd)
                v.wait_ge(self.ac.h, wa)
                v.reduce_max(self.mean_a[:, j:j + 1], self.kct.ap()[:, j * 2048:(j + 1) * 2048],
                             AX.X, apply_absolute_value=True).then_inc(self.dv.h, 1)
            dv.inc()
            self.v(kb)
        def kb_fin():
            pre = dv.n
            def kf(v, pre=pre):
                v.wait_ge(self.dv.h, pre)
                v.reduce_max(self.kabs[:, :], self.mean_a[:, 0:4], AX.X,
                             apply_absolute_value=True).then_inc(self.dv.h, 1)
            t1k = dv.inc()
            self.v(kf)
            t2k = ac.inc()
            self.s(lambda s, w=t1k: (s.wait_ge(self.dv.h, w),
                                     s.activation(self.kabs[:, :], self.kabs[:, :], AF.Square).then_inc(self.ac.h, 1)))
            return t2k

        def emit_bound_mms(t2k, t2q):
            def kb_mm(t, w=t2k, fb=self.bankfree.get(0)):
                t.wait_ge(self.ac.h, w)
                self.wait_bf(t, fb)
                t.matmul(self.pb[0][0:1, 0:1], self.ones128f[:, :], self.kabs[:, :],
                         start=True, stop=True).then_inc(self.pe.h, 1)
            t3 = pe.inc()
            self.t(kb_mm)
            pre_kb = dv.n
            def kb_e(v, w=t3, wkr=self.kr2s_dv, pre=pre_kb):
                v.wait_ge(self.pe.h, w)
                v.wait_ge(self.dv.h, max(wkr, pre))
                v.tensor_copy(self.bsum[0:1, 0:1], self.pb[0][0:1, 0:1]).then_inc(self.dv.h, 1)
                v.wait_ge(self.dv.h, pre + 1)
                v.tensor_tensor(self.bsum[0:1, 0:1], self.bsum[0:1, 0:1], self.kr2s[0:1, 0:1],
                                ALU.add).then_inc(self.dv.h, 1)
            dv.inc()
            kb_dv = dv.inc()
            self.v(kb_e)
            self.bankfree[0] = ("vector", kb_dv)
            def qb_mm(t, w=t2q, fb=self.bankfree.get(1)):
                t.wait_ge(self.ac.h, w)
                self.wait_bf(t, fb)
                t.matmul(self.pb[1][0:1, 1:2], self.ones128f[:, :], self.qabs[:, :], start=True, stop=False)
                t.matmul(self.pb[1][0:1, 1:2], self.ones128f[0:PD, :], self.qabs2[0:PD, :],
                         start=False, stop=True).then_inc(self.pe.h, 1)
            t3q = pe.inc()
            self.t(qb_mm)
            pre_qb = dv.n
            def qb_e(v, w=t3q, pre=pre_qb):
                v.wait_ge(self.pe.h, w)
                v.wait_ge(self.dv.h, pre)
                v.tensor_copy(self.bsum[0:1, 1:2], self.pb[1][0:1, 1:2]).then_inc(self.dv.h, 1)
                v.wait_ge(self.dv.h, pre + 1)
                v.tensor_mul(self.negb[0:1, :], self.bsum[0:1, 0:1], self.bsum[0:1, 1:2]).then_inc(self.dv.h, 1)
            dv.inc()
            t5 = dv.inc()
            self.v(qb_e)
            self.bankfree[1] = ("vector", t5)
            t6 = ac.inc()
            self.s(lambda s, w=t5: (s.wait_ge(self.dv.h, w),
                                    s.activation(self.negb[0:1, :], self.negb[0:1, :], AF.Sqrt).then_inc(self.ac.h, 1)))
            pre_r64 = dv.n
            def row64(v, w=t6, pre=pre_r64):
                v.wait_ge(self.ac.h, w)
                v.wait_ge(self.dv.h, pre)
                v.tensor_scalar(self.negb[0:1, :], self.negb[0:1, :], -1.0, None, ALU.mult).then_inc(self.dv.h, 1)
                v.wait_ge(self.dv.h, pre + 1)
                v.tensor_scalar(self.qrt.ap()[PD:PD + 1, :], self.onesrow.ap()[0:1, :],
                                self.negb[0:1, 0:1], None, ALU.mult).then_inc(self.dv.h, 1)
            dv.inc()
            b_dv = dv.inc()
            self.v(row64)
            return b_dv
        bound_dv = None
        t2k = t2q = None
        t_qrev = [None, None]
        for vh in range(2):
            for bg in range(8):
                bank = O_BANKS[bg % 4]
                ktl0 = bg * 4
                def vupg(t, vh=vh, bank=bank, ktl0=ktl0, par=par, w=wld["uv"],
                         fb=self.bankfree.get(bank)):
                    t.wait_ge(self.suv[par].h, w)
                    self.wait_bf(t, fb)
                    for kti in range(4):
                        kt = vh * 32 + ktl0 + kti
                        for mt in range(NMT):
                            ins = t.matmul(self.pb[bank][:, kti * P:(kti + 1) * P],
                                           self.ckvt.ap()[:, mt * S + kt * P: mt * S + (kt + 1) * P],
                                           self.uvb[par].ap()[:, mt * P:(mt + 1) * P],
                                           start=(kti == 0 and mt == 0),
                                           stop=(kti == 3 and mt == NMT - 1))
                    ins.then_inc(self.pe.h, 1)
                tpe = pe.inc()
                self.t(vupg)
                dstv = self.vbuf.ap().rearrange("p (k c) -> p k c", c=130)[:, ktl0:ktl0 + 4, 0:P]
                srcv = self.pb[bank].ap().rearrange("p (k c) -> p k c", c=P)[:, 0:4, :]
                if bg % 2 == 0:
                    def vupe(v, dstv=dstv, srcv=srcv, w=tpe):
                        v.wait_ge(self.pe.h, w)
                        v.tensor_copy(dstv, srcv).then_inc(self.dv.h, 1)
                    tdv = dv.inc()
                    self.v(vupe)
                    self.bankfree[bank] = ("vector", tdv)
                else:
                    def vupe_s(s, dstv=dstv, srcv=srcv, w=tpe, pre=self.ac.n):
                        s.wait_ge(self.pe.h, w)
                        s.wait_ge(self.ac.h, pre)
                        s.copy(dstv, srcv).then_inc(self.ac.h, 1)
                    tac = ac.inc()
                    self.s(vupe_s)
                    self.bankfree[bank] = ("scalar", tac)
                if vh == 0:
                    if bg == 0:
                        t_qrev[0] = qr_stage_a(0)
                    elif bg == 1:
                        qr_stage_b(0, t_qrev[0])
                        t_qrev[1] = qr_stage_a(1)
                    elif bg == 2:
                        qr_stage_b(1, t_qrev[1])
                        t2q = emit_qb()
                    elif bg == 3:
                        kb_chunk(0)
                        kb_chunk(1)
                    elif bg == 4:
                        kb_chunk(2)
                        kb_chunk(3)
                    elif bg == 5:
                        t2k = kb_fin()
            if vh == 0:
                bound_dv = emit_bound_mms(t2k, t2q)
                if dbg_me:
                    self.emit_dbg([("d_kct", self.kct.ap()[:, :]),
                                   ("d_qct", self.qct.ap()[:, :]),
                                   ("d_qrt", self.qrt.ap()[0:PD + 1, :]),
                                   ("d_bnd", self.bsum[0:1, 0:2])],
                                  [("vector", bound_dv)])
            vdone = self.dv.n
            vdone_ac = self.ac.n
            if dbg_me and vh == 1:
                self.emit_dbg([("d_vt", self.vbuf.ap()[:, :])], [("vector", vdone), ("scalar", vdone_ac)])
            for qc in range(NQC):
                SC_BANKS = (0, 1, 4, 5)
                tac_list = [None] * 32
                def emit_sc(ktl, qc=qc, vh=vh, vdone=vdone, vdone_ac=vdone_ac):
                    kt = vh * 32 + ktl
                    bank = SC_BANKS[ktl % 4]
                    def sc(t, qc=qc, kt=kt, ktl=ktl, bank=bank, wb=bound_dv, wv=vdone, wva=vdone_ac,
                           fb=self.bankfree.get(bank)):
                        if ktl == 0 and qc == 0:
                            t.wait_ge(self.dv.h, max(wb, wv))
                            t.wait_ge(self.ac.h, wva)
                        self.wait_bf(t, fb)
                        t.matmul(self.pb[bank][:, :],
                                 self.kct.ap()[:, kt * P:(kt + 1) * P],
                                 self.qct.ap()[:, qc * 512:(qc + 1) * 512],
                                 start=True, stop=False)
                        t.matmul(self.pb[bank][:, :],
                                 self.krt.ap()[0:PD + 1, kt * P:(kt + 1) * P],
                                 self.qrt.ap()[0:PD + 1, qc * 512:(qc + 1) * 512],
                                 start=False, stop=True).then_inc(self.pe.h, 1)
                    tpe_sc = self.pe.inc()
                    self.t(sc)
                    def ex(s, bank=bank, w=tpe_sc):
                        s.wait_ge(self.pe.h, w)
                        s.activation(self.e_bf[bank if bank < 2 else bank - 2].ap()[:, :],
                                     self.pb[bank][:, :], AF.Exp,
                                     scale=ATT_SCALE).then_inc(self.ac.h, 1)
                    tac = self.ac.inc()
                    self.s(ex)
                    tac_list[ktl] = tac
                    self.bankfree[bank] = ("scalar", tac)
                emit_sc(0)
                emit_sc(1)
                emit_sc(2)
                for ktl in range(32):
                    if ktl + 3 < 32:
                        emit_sc(ktl + 3)
                    sb = SC_BANKS[ktl % 4]
                    ebuf = self.e_bf[sb if sb < 2 else sb - 2]
                    def om(t, ktl=ktl, ebuf=ebuf, w=tac_list[ktl],
                           fbs=tuple(self.bankfree.get(b) for b in O_BANKS) if ktl == 0 else ()):
                        t.wait_ge(self.ac.h, w)
                        for f in fbs:
                            self.wait_bf(t, f)
                        for j in range(4):
                            ins = t.matmul(self.pb[O_BANKS[j]][:, 0:129],
                                           ebuf.ap()[:, j * P:(j + 1) * P],
                                           self.vbuf.ap()[:, ktl * 130: ktl * 130 + 129],
                                           start=(ktl == 0), stop=(ktl == 31))
                        ins.then_inc(self.pe.h, 1)
                    tpe_o = self.pe.inc()
                    self.t(om)
                pre_oe = dv.n
                def oev(v, qc=qc, vh=vh, w=tpe_o, pre=pre_oe):
                    v.wait_ge(self.pe.h, w)
                    v.wait_ge(self.dv.h, pre)
                    for j in range(4):
                        dst = self.oacc.ap()[:, (qc * 4 + j) * 132: (qc * 4 + j) * 132 + 129]
                        if vh == 0:
                            ins = v.tensor_copy(dst, self.pb[O_BANKS[j]][:, 0:129])
                        else:
                            ins = v.tensor_tensor(dst, dst, self.pb[O_BANKS[j]][:, 0:129], ALU.add)
                    ins.then_inc(self.dv.h, 1)
                t_oe = dv.inc()
                self.v(oev)
                for b in O_BANKS:
                    self.bankfree[b] = ("vector", t_oe)
        oacc_dv = self.dv.n
        def emit_epilogue(h=h, oacc_dv=oacc_dv):
            dv, pe = self.dv, self.pe
            for qb in range(NQB):
                off = qb * 132
                pre_ep = dv.n
                prev_tp = getattr(self, "_prev_obf_pe", None)
                def ep12(v, off=off, w=oacc_dv, pre=pre_ep, ptp=prev_tp):
                    if ptp is not None:
                        v.wait_ge(self.pe.h, ptp)
                    v.wait_ge(self.dv.h, max(w, pre))
                    v.reciprocal(self.rd[:, :], self.oacc.ap()[:, off + 128:off + 129]).then_inc(self.dv.h, 1)
                    v.wait_ge(self.dv.h, pre + 1)
                    v.tensor_scalar(self.o_bf.ap()[:, :], self.oacc.ap()[:, off:off + P],
                                    self.rd[:, 0:1], None, ALU.mult).then_inc(self.dv.h, 1)
                dv.inc()
                t2 = dv.inc()
                self.v(ep12)
                def ep3(t, w=t2, fb=self.bankfree.get(1)):
                    t.wait_ge(self.dv.h, w)
                    self.wait_bf(t, fb)
                    t.transpose(self.pb_bf[1][:, 0:P], self.o_bf.ap()[:, :],
                                self.ident_bf.ap()[:, :]).then_inc(self.pe.h, 1)
                t3 = pe.inc()
                self._prev_obf_pe = t3
                self.t(ep3)
                def ep4(v, h=h, qb=qb, w=t3):
                    v.wait_ge(self.pe.h, w)
                    v.tensor_copy(self.oT.ap()[:, h * TOK + qb * P: h * TOK + (qb + 1) * P],
                                  self.pb_bf[1][:, 0:P]).then_inc(self.dv.h, 1)
                t4 = dv.inc()
                self.v(ep4)
                self.bankfree[1] = ("vector", t4)
        self._pend_epilogue = emit_epilogue
        self._head_w_done = self.pe.n

    def emit_wo(self):
        io, pe, dv = self.io, self.pe, self.dv
        attn_pe = self.pe.n
        attn_dv = self.dv.n
        wo_src = io["wo"].rearrange("(kt p) n -> p kt n", p=P)
        xsrc = io["x_own"].rearrange("(qb p) d -> qb p d", p=P)
        half_pe = [None, None]
        prev_hf_dv = attn_dv
        wo_sems = [self.swo, self.sckc[0]]
        for hf in range(2):
            wwaits = [("tensor", attn_pe)] if hf == 0 else [("tensor", half_pe[0])]
            wo_lds = []
            for wc in range(8):
                wo_lds.append(self.dma(
                    "gpsimd", wo_sems[wc % 2],
                    self.wo_b.ap().rearrange("p (kt n) -> p kt n", n=TOK)[:, 2 * wc:2 * (wc + 1), :],
                    wo_src[:, 2 * wc:2 * (wc + 1), hf * TOK:(hf + 1) * TOK],
                    wwaits if wc < 2 else []))
            xld = [None] * NQB
            for qb in range(NQB):
                xwaits = [("vector", prev_hf_dv)] if qb < 2 else [("vector", xld[qb - 2])]
                tsx = self.dma("sync", self.sx[qb % 2], self.xwo[qb % 2].ap()[:, :],
                               xsrc[qb][:, hf * TOK:(hf + 1) * TOK], xwaits)
                for ch in range(2):
                    bank = (qb * 2 + ch) % 4
                    def wo(t, hf=hf, qb=qb, ch=ch, bank=bank, wlds=tuple(wo_lds), fb=self.bankfree.get(bank)):
                        self.wait_bf(t, fb)
                        for kt in range(NKT):
                            if kt % 2 == 0:
                                t.wait_ge(wo_sems[(kt // 2) % 2].h, wlds[kt // 2])
                            ins = t.matmul(self.pb[bank][:, :],
                                           self.oT.ap()[:, kt * TOK + qb * P: kt * TOK + (qb + 1) * P],
                                           self.wo_b.ap()[:, kt * TOK + ch * 512: kt * TOK + ch * 512 + 512],
                                           start=(kt == 0), stop=(kt == NKT - 1))
                        ins.then_inc(self.pe.h, 1)
                    tpe = pe.inc()
                    self.t(wo)
                    def woe(v, hf=hf, qb=qb, ch=ch, bank=bank, w=tpe, wx=tsx):
                        v.wait_ge(self.pe.h, w)
                        v.wait_ge(self.sx[qb % 2].h, wx)
                        off = qb * D + hf * TOK + ch * 512
                        v.tensor_tensor(self.out_acc.ap()[:, off:off + 512], self.pb[bank][:, :],
                                        self.xwo[qb % 2].ap()[:, ch * 512:(ch + 1) * 512],
                                        ALU.add).then_inc(self.dv.h, 1)
                    tdv = dv.inc()
                    self.v(woe)
                    self.bankfree[bank] = ("vector", tdv)
                xld[qb] = dv.n
            half_pe[hf] = self.pe.n
            prev_hf_dv = self.dv.n
        self.p3_dv = self.dv.n
        self.emit_dbg([("d_accp3", self.out_acc.ap()[:, :])], [("vector", self.dv.n)])

    def emit_ffn_pre(self):
        io, pe, dv, ac = self.io, self.pe, self.dv, self.ac
        rw_ld = self.dma("gpsimd", self.swa, self.rwsg_b.ap()[:, :], io["rwsg"][:, :],
                         [("vector", self.p3_dv)])
        xsrc = io["x_own"].rearrange("(qb p) d -> qb p d", p=P)
        xld = [None] * NQB
        base_dv = self.p3_dv
        tpe_r = [None] * NQB
        for qb in range(NQB):
            p_ = qb % 2
            b0, b1 = self.xwo[2 * p_], self.xwo[2 * p_ + 1]
            if qb < 2:
                xwaits = [("vector", self.p3_dv)]
            else:
                xwaits = [("vector", xld[qb - 2])]
            tx0 = self.dma("sync", self.sx[0], b0.ap()[:, :], xsrc[qb][:, 0:TOK], xwaits)
            tx1 = self.dma("sync", self.sx[1], b1.ap()[:, :], xsrc[qb][:, TOK:D], xwaits)
            xnb = self.xn_bf[p_]
            prev_pe = self._xnb_pe[p_]
            negm = self.negm_sv[:, qb:qb + 1]
            rstd = self.rstd_sv[:, qb:qb + 1]
            def apply(v, xnb=xnb, a=tx0, b=tx1, b0=b0, b1=b1, negm=negm, rstd=rstd,
                      w=base_dv, wp=prev_pe):
                v.wait_ge(self.sx[0].h, a)
                v.wait_ge(self.sx[1].h, b)
                v.wait_ge(self.dv.h, w)
                if wp is not None:
                    v.wait_ge(self.pe.h, wp)
                v.tensor_scalar(xnb.ap()[:, 0:TOK], b0.ap()[:, :], negm, rstd, ALU.add, ALU.mult)
                v.tensor_scalar(xnb.ap()[:, TOK:D], b1.ap()[:, :], negm, rstd,
                                ALU.add, ALU.mult).then_inc(self.dv.h, 1)
            t2 = dv.inc()
            self.v(apply)
            dst = [self.xnt.ap()[:, kt * TOK + qb * P: kt * TOK + (qb + 1) * P] for kt in range(NKT)]
            pec, dvc = self.emit_transpose16(xnb.ap(), dst, t2, evac="mixed",
                                             banks=((2, 3) if qb % 2 == 0 else (4, 5)))
            self._xnb_pe[p_] = pec
            xld[qb] = t2
            xnt_dv, xnt_ac = self.dv.n, self.ac.n
            def rmm(t, qb=qb, w=xnt_dv, wa=xnt_ac, wld=rw_ld,
                    fb=(self.bankfree.get(7) if qb == 0 else None)):
                t.wait_ge(self.dv.h, w)
                t.wait_ge(self.ac.h, wa)
                t.wait_ge(self.swa.h, wld)
                self.wait_bf(t, fb)
                for kt in range(NKT - 1):
                    t.matmul(self.pb[7][:, qb * 9: qb * 9 + 9],
                             self.xnt.ap()[:, kt * TOK + qb * P: kt * TOK + (qb + 1) * P],
                             self.rwsg_b.ap()[:, kt * 9: kt * 9 + 9],
                             start=(qb == 0 and kt == 0), stop=False)
                kt = NKT - 1
                t.matmul(self.pb[7][:, qb * 9: qb * 9 + 9],
                         self.xnt.ap()[:, kt * TOK + qb * P: kt * TOK + (qb + 1) * P],
                         self.rwsg_b.ap()[:, kt * 9: kt * 9 + 9],
                         start=False, stop=(qb == NQB - 1)).then_inc(self.pe.h, 1)
            tpe_r[qb] = pe.inc()
            self.t(rmm)
        self.p4a_apply_dv = self.dv.n
        pre_ac2 = ac.n
        def sigs(s, w=tpe_r[NQB - 1], pre=pre_ac2):
            s.wait_ge(self.pe.h, w)
            s.wait_ge(self.ac.h, pre)
            for qb in range(NQB):
                ins = s.activation(self.sig[:, qb:qb + 1],
                                   self.pb[7][:, qb * 9 + 8: qb * 9 + 9], AF.Sigmoid)
            ins.then_inc(self.ac.h, 1)
        t_sig = ac.inc()
        self.s(sigs)
        pre_dv2 = dv.n
        def rmx(v, w=tpe_r[NQB - 1], pre=pre_dv2):
            v.wait_ge(self.pe.h, w)
            v.wait_ge(self.dv.h, pre)
            for qb in range(NQB):
                ins = v.reduce_max(self.m1s[:, qb:qb + 1],
                                   self.pb[7][:, qb * 9: qb * 9 + 8], AX.X, negate=True)
            ins.then_inc(self.dv.h, 1)
        t_rmx = dv.inc()
        self.v(rmx)
        def exps(s, w=t_rmx, pre=t_sig):
            s.wait_ge(self.dv.h, w)
            s.wait_ge(self.ac.h, pre)
            for qb in range(NQB):
                ins = s.activation(self.probs_all[:, qb * 8:(qb + 1) * 8],
                                   self.pb[7][:, qb * 9: qb * 9 + 8],
                                   AF.Exp, bias=self.m1s[:, qb:qb + 1])
            ins.then_inc(self.ac.h, 1)
        t_exp = ac.inc()
        self.s(exps)
        self.bankfree[7] = ("scalar", t_exp)
        b2 = dv.n
        def top2b(v, w=t_exp, b=b2):
            v.wait_ge(self.ac.h, w)
            v.wait_ge(self.dv.h, b)
            for qb in range(NQB):
                ins = v.reduce_max(self.m1s[:, qb:qb + 1],
                                   self.probs_all[:, qb * 8:(qb + 1) * 8], AX.X)
            ins.then_inc(self.dv.h, 1)
            v.wait_ge(self.dv.h, b + 1)
            for qb in range(NQB):
                ins = v.tensor_scalar(self.mask_all[:, qb * 8:(qb + 1) * 8],
                                      self.probs_all[:, qb * 8:(qb + 1) * 8],
                                      self.m1s[:, qb:qb + 1], None, ALU.is_ge)
            ins.then_inc(self.dv.h, 1)
            v.wait_ge(self.dv.h, b + 2)
            v.scalar_tensor_tensor(self.mask_all[:, :], self.mask_all[:, :], -2.0,
                                   self.probs_all[:, :], ALU.mult, ALU.add).then_inc(self.dv.h, 1)
            v.wait_ge(self.dv.h, b + 3)
            for qb in range(NQB):
                ins = v.reduce_max(self.m2s[:, qb:qb + 1],
                                   self.mask_all[:, qb * 8:(qb + 1) * 8], AX.X)
            ins.then_inc(self.dv.h, 1)
            v.wait_ge(self.dv.h, b + 4)
            for qb in range(NQB):
                v.tensor_scalar(self.keep_all[:, qb * 8:(qb + 1) * 8],
                                self.probs_all[:, qb * 8:(qb + 1) * 8],
                                self.m2s[:, qb:qb + 1], None, ALU.is_ge)
            v.tensor_tensor(self.d8[:, :], self.m1s[:, :], self.m2s[:, :],
                            ALU.add).then_inc(self.dv.h, 1)
            v.wait_ge(self.dv.h, b + 5)
            v.reciprocal(self.r8[:, :], self.d8[:, :]).then_inc(self.dv.h, 1)
            v.wait_ge(self.dv.h, b + 6)
            for qb in range(NQB):
                ins = v.scalar_tensor_tensor(self.combine[:, qb * 8:(qb + 1) * 8],
                                             self.probs_all[:, qb * 8:(qb + 1) * 8],
                                             self.r8[:, qb:qb + 1],
                                             self.keep_all[:, qb * 8:(qb + 1) * 8],
                                             ALU.mult, ALU.mult)
            ins.then_inc(self.dv.h, 1)
        dv.inc(6)
        t_comb = dv.inc()
        self.v(top2b)
        self.ffnpre_dv = self.dv.n
        self.emit_dbg([("d_xnt", self.xnt.ap()[:, :]),
                       ("d_comb", self.combine[:, :]),
                       ("d_sig", self.sig[:, :])],
                      [("vector", self.dv.n), ("scalar", self.ac.n)])

    def emit_gu_pair(self, gbuf, ubuf, gsem, usem, gld, uld, dst_cols):
        pe, dv, ac = self.pe, self.dv, self.ac
        for ch in range(2):
            bga, bgb = 2 * ch, 2 * ch + 1
            def gu(t, ch=ch, bga=bga, bgb=bgb, gbuf=gbuf, ubuf=ubuf, gsem=gsem, usem=usem, gld=gld, uld=uld,
                   fa=self.bankfree.get(bga), fbb=self.bankfree.get(bgb)):
                t.wait_ge(gsem.h, gld)
                t.wait_ge(usem.h, uld)
                self.wait_bf(t, fa)
                self.wait_bf(t, fbb)
                for kt in range(NKT):
                    t.matmul(self.pb[bga][:, :], gbuf.ap()[:, kt * P:(kt + 1) * P],
                             self.xnt.ap()[:, kt * TOK + ch * 512: kt * TOK + ch * 512 + 512],
                             start=(kt == 0), stop=(kt == NKT - 1))
                for kt in range(NKT - 1):
                    t.matmul(self.pb[bgb][:, :], ubuf.ap()[:, kt * P:(kt + 1) * P],
                             self.xnt.ap()[:, kt * TOK + ch * 512: kt * TOK + ch * 512 + 512],
                             start=(kt == 0), stop=False)
                kt = NKT - 1
                t.matmul(self.pb[bgb][:, :], ubuf.ap()[:, kt * P:(kt + 1) * P],
                         self.xnt.ap()[:, kt * TOK + ch * 512: kt * TOK + ch * 512 + 512],
                         start=False, stop=True).then_inc(self.pe.h, 1)
            tpe = pe.inc()
            self.t(gu)
            def sil(s, ch=ch, bga=bga, w=tpe):
                s.wait_ge(self.pe.h, w)
                s.activation(self.silu_t[ch].ap()[:, :], self.pb[bga][:, :], AF.Silu).then_inc(self.ac.h, 1)
            tac = ac.inc()
            self.s(sil)
            self.bankfree[bga] = ("scalar", tac)
            def mul(v, ch=ch, bgb=bgb, dst=dst_cols[ch], w=tac):
                v.wait_ge(self.ac.h, w)
                v.scalar_tensor_tensor(dst, self.pb[bgb][:, :], 1.0, self.silu_t[ch].ap()[:, :],
                                       ALU.mult, ALU.mult).then_inc(self.dv.h, 1)
            tdv = dv.inc()
            self.v(mul)
            self.bankfree[bgb] = ("vector", tdv)

    def emit_down(self, act_buf, dsrc4_fn, scal_fn):
        pe, dv = self.pe, self.dv
        act_ready = self.dv.n
        NG = (IMT + 3) // 4
        units = [(dc, g) for dc in range(4) for g in range(NG)]
        base_u = self._dw_u
        lds = {}
        def emit_ld(u):
            dc, g = units[u]
            par = (base_u + u) % 4
            free = self.dwfree.get(par)
            waits = [] if free is None else [("tensor", free)]
            nk = min(4, IMT - 4 * g)
            lds[u] = self.dma("gpsimd", self.sdw[par],
                              self.dwb[par].ap().rearrange("p (k n) -> p k n", n=512)[:, 0:nk, :],
                              dsrc4_fn(g, dc), waits)
        DW_DEPTH = 3
        for u in range(DW_DEPTH):
            emit_ld(u)
        for dc in range(4):
            for half in range(2):
                qbs = range(4) if half == 0 else range(4, NQB)
                for g in range(NG):
                    u = dc * NG + g
                    par = (base_u + u) % 4
                    nk = min(4, IMT - 4 * g)
                    for ki in range(nk):
                        kt = 4 * g + ki
                        def dn(t, kt=kt, ki=ki, dc=dc, par=par, wld=lds[u], act_buf=act_buf,
                               qbs=tuple(qbs),
                               fbs=tuple(self.bankfree.get(b) for b in qbs) if kt == 0 else (),
                               war=act_ready if (kt == 0 and dc == 0 and half == 0) else None):
                            t.wait_ge(self.sdw[par].h, wld)
                            if war is not None:
                                t.wait_ge(self.dv.h, war)
                            for f in fbs:
                                self.wait_bf(t, f)
                            for qb in qbs:
                                ins = t.matmul(self.pb[qb][:, :],
                                               act_buf.ap()[:, kt * TOK + qb * P: kt * TOK + (qb + 1) * P],
                                               self.dwb[par].ap()[:, ki * 512:(ki + 1) * 512],
                                               start=(kt == 0), stop=(kt == IMT - 1))
                            ins.then_inc(self.pe.h, 1)
                        tpe = pe.inc()
                        self.t(dn)
                    if half == 1:
                        self.dwfree[par] = tpe
                        if u + DW_DEPTH < len(units):
                            emit_ld(u + DW_DEPTH)
                last_pe = self.pe.n
                for qb in qbs:
                    def dne(v, qb=qb, dc=dc, w=last_pe, sc=scal_fn(qb)):
                        v.wait_ge(self.pe.h, w)
                        off = qb * D + dc * 512
                        v.scalar_tensor_tensor(self.out_acc.ap()[:, off:off + 512], self.pb[qb][:, :],
                                               sc, self.out_acc.ap()[:, off:off + 512],
                                               ALU.mult, ALU.add).then_inc(self.dv.h, 1)
                    tdv = dv.inc()
                    self.v(dne)
                    self.bankfree[qb] = ("vector", tdv)
                    self.last_dne[qb] = tdv
        self._dw_u += len(units)

    def emit_shared(self):
        io = self.io
        self.dwfree = {}
        self.gwfree = {}
        gsrc = io["shgt"].rearrange("(kt p) n -> p kt n", p=P)
        usrc = io["shupt"].rearrange("(kt p) n -> p kt n", p=P)
        pre = self.p4a_apply_dv
        for q in range(4):
            for mt in range(IMT):
                i_col = q * IM + mt * P
                par = mt % 2
                free = self.gwfree.get(par)
                waits = [("vector", pre)] if free is None else [("tensor", free)]
                gld = self.dma("gpsimd", self.sgw[par],
                               self.gwb[par].ap().rearrange("p (kt n) -> p kt n", n=P),
                               gsrc[:, :, i_col:i_col + P], waits)
                uld = self.dma("gpsimd", self.suw[par],
                               self.uwb[par].ap().rearrange("p (kt n) -> p kt n", n=P),
                               usrc[:, :, i_col:i_col + P], waits)
                dst = [self.shact.ap()[:, mt * TOK + ch * 512: mt * TOK + ch * 512 + 512] for ch in range(2)]
                self.emit_gu_pair(self.gwb[par], self.uwb[par], self.sgw[par], self.suw[par], gld, uld, dst)
                self.gwfree[par] = self.pe.n
            def dsrc4(g, dc, q=q):
                nk = min(4, IMT - 4 * g)
                return io["shdt"][q * IM + 4 * g * P: q * IM + (4 * g + nk) * P,
                                  dc * 512:(dc + 1) * 512].rearrange("(k p) n -> p k n", p=P)
            self.emit_down(self.shact, dsrc4, lambda qb: self.sig[:, qb:qb + 1])
        self.emit_dbg([("d_shact", self.shact.ap()[:, :]),
                       ("d_accp4s", self.out_acc.ap()[:, :])], [("vector", self.dv.n)])

    def emit_moe(self):
        io = self.io
        for e in range(E_):
            gsrc = io[f"gupt{e}"].rearrange("(kt p) n -> p kt n", p=P)
            for mt in range(IMT):
                par = mt % 2
                free = self.gwfree.get(par)
                waits = [] if free is None else [("tensor", free)]
                gld = self.dma("gpsimd", self.sgw[par],
                               self.gwb[par].ap().rearrange("p (kt n) -> p kt n", n=P),
                               gsrc[:, :, mt * P:(mt + 1) * P], waits)
                uld = self.dma("gpsimd", self.suw[par],
                               self.uwb[par].ap().rearrange("p (kt n) -> p kt n", n=P),
                               gsrc[:, :, IM + mt * P: IM + (mt + 1) * P], waits)
                dst = [self.actt.ap()[:, mt * TOK + ch * 512: mt * TOK + ch * 512 + 512] for ch in range(2)]
                self.emit_gu_pair(self.gwb[par], self.uwb[par], self.sgw[par], self.suw[par], gld, uld, dst)
                self.gwfree[par] = self.pe.n
            def dsrc4(g, dc, e=e):
                nk = min(4, IMT - 4 * g)
                return io[f"downt{e}"][4 * g * P:(4 * g + nk) * P,
                                       dc * 512:(dc + 1) * 512].rearrange("(k p) n -> p k n", p=P)
            def scal(qb, e=e):
                return self.combine[:, qb * 8 + e: qb * 8 + e + 1]
            self.emit_down(self.actt, dsrc4, scal)

    def emit_store(self):
        if self.debug >= 8:
            def fin(sy, tv=self.so[3].n):
                sy.wait_ge(self.so[3].h, tv)
            self.y(fin)
            return
        osrc = self.out.rearrange("(qb p) d -> qb p d", p=P)
        for qb in range(NQB):
            sem = self.so[qb % 4]
            pre = sem.n
            sem.inc(16)
            def th(sy, qb=qb, pre=pre, sem=sem, w=self.last_dne[qb]):
                sy.wait_ge(self.dv.h, w)
                sy.wait_ge(sem.h, pre)
                sy.dma_start(osrc[qb], self.out_acc.ap()[:, qb * D:(qb + 1) * D]).then_inc(sem.h, 16)
            self.y(th)
        def fin(sy, targets=tuple((i, self.so[i].n) for i in range(4))):
            for i, tv in targets:
                sy.wait_ge(self.so[i].h, tv)
        self.y(fin)


def rope_host_tables():
    inv_freq = 1.0 / 10000.0 ** (np.arange(0, PD, 2, dtype=np.float32) / PD)
    t = np.arange(S, dtype=np.float32)
    freqs = t[:, None] * inv_freq[None, :]
    emb = np.concatenate([freqs, freqs], axis=-1)
    return (np.ascontiguousarray(np.cos(emb).T.astype(np.float32)),
            np.ascontiguousarray(np.sin(emb).T.astype(np.float32)))


def _pack_rwsg(inputs):
    rw = np.concatenate([np.asarray(inputs["L0_ffn_router_weight"], dtype=np.float32).T,
                         np.asarray(inputs["L0_ffn_shared_gate_w"], dtype=np.float32).T], axis=1)
    return np.ascontiguousarray(rw.reshape(NKT, P, 9).transpose(1, 0, 2).reshape(P, NKT * 9))


def make_in_maps(inputs):
    x = np.ascontiguousarray(np.asarray(inputs["L0_attn_hidden_states"], dtype=np.float32).reshape(S, D))
    cosk, sink = rope_host_tables()
    rtm = np.zeros((PD, PD), dtype=np.float32)
    for i in range(32):
        rtm[i + 32, i] = -1.0
        rtm[i, i + 32] = 1.0
    common = {
        "x_full": x,
        "wdkv": np.ascontiguousarray(inputs["L0_attn_W_DKV"], dtype=np.float32),
        "wdq": np.ascontiguousarray(inputs["L0_attn_W_DQ"], dtype=np.float32),
        "wkr": np.ascontiguousarray(inputs["L0_attn_W_KR"], dtype=np.float32),
        "wuk": np.ascontiguousarray(inputs["L0_attn_W_UK"], dtype=np.float32),
        "wuv": np.ascontiguousarray(inputs["L0_attn_W_UV"], dtype=np.float32),
        "wuq": np.ascontiguousarray(inputs["L0_attn_W_UQ"], dtype=np.float32),
        "wqr": np.ascontiguousarray(inputs["L0_attn_W_QR"], dtype=np.float32),
        "wo": np.ascontiguousarray(inputs["L0_attn_W_O"], dtype=np.float32),
        "cosk": cosk, "sink": sink,
        "rt": rtm,
        "ident": np.eye(P, dtype=np.float32),
        "rwsg": _pack_rwsg(inputs),
        "shgt": np.ascontiguousarray(np.asarray(inputs["L0_ffn_sh_gate_w"], dtype=np.float32).T),
        "shupt": np.ascontiguousarray(np.asarray(inputs["L0_ffn_sh_up_w"], dtype=np.float32).T),
        "shdt": np.ascontiguousarray(np.asarray(inputs["L0_ffn_sh_down_w"], dtype=np.float32).T),
    }
    gup = np.asarray(inputs["L0_ffn_gate_up_proj"], dtype=np.float32)
    dwn = np.asarray(inputs["L0_ffn_down_proj"], dtype=np.float32)
    for e in range(E_):
        common[f"gupt{e}"] = np.ascontiguousarray(gup[e].T)
        common[f"downt{e}"] = np.ascontiguousarray(dwn[e].T)
    in_maps = []
    for c in range(NCORES):
        m = dict(common)
        m["x_own"] = np.ascontiguousarray(x[c * TOK:(c + 1) * TOK])
        m["cosq"] = np.ascontiguousarray(cosk[:, c * TOK:(c + 1) * TOK])
        m["sinq"] = np.ascontiguousarray(sink[:, c * TOK:(c + 1) * TOK])
        in_maps.append(m)
    return in_maps


def kernel(**inputs):
    k = K(debug=0)
    nc = k.build()
    in_maps = make_in_maps(inputs)
    res = run_bass_kernel_spmd(nc, in_maps, core_ids=list(range(NCORES)))
    out = np.concatenate([res.results[c]["out"] for c in range(NCORES)], axis=0)
    return out.reshape(1, S, D).astype(np.float32)


# version 23 (final)
# speedup vs baseline: 1.0324x; 1.0216x over previous
import numpy as np
from contextlib import ExitStack
import concourse.bass as bass
import concourse.mybir as mybir
from concourse.bass_utils import run_bass_kernel_spmd

F32 = mybir.dt.float32
BF16 = mybir.dt.bfloat16
AF = mybir.ActivationFunctionType
ALU = mybir.AluOpType
AX = mybir.AxisListType

NCORES = 8
P = 128
S = 8192
D = 2048
TOK = S // NCORES
LAT = 512
H, HD, PD = 16, 128, 64
NKT = D // P
NMT = LAT // P
NKEY = S // P
NST = S // 512
NQB = TOK // P
NQC = TOK // 512
E_ = 8
IM = 1408
IMT = IM // P
ISH = 5632
ATT_SCALE = float(1.0 / np.sqrt(HD + PD))
EPS = 1e-5

SBUF_LO = 16512
SBUF_HI = 229344
KB = 1024

PH = {"p1": 0, "at": 1, "p3": 2, "p4": 3}


def _dtsize(dt):
    return {F32: 4, BF16: 2}[dt]


class Lay:
    def __init__(self, nc):
        self.nc = nc
        self.recs = []

    def at(self, name, shape, dtype, off, ph0, ph1=None):
        ph1 = ph0 if ph1 is None else ph1
        a, b = PH[ph0], PH[ph1]
        size = int(np.prod(shape[1:])) * _dtsize(dtype)
        off = int(off)
        assert off % 64 == 0, f"{name} misaligned {off}"
        assert off >= SBUF_LO and off + size <= SBUF_HI, f"{name}: {off}..{off + size} out of range"
        for (n2, o2, s2, a2, b2) in self.recs:
            if off < o2 + s2 and o2 < off + size and a <= b2 and a2 <= b:
                raise AssertionError(f"SBUF overlap: {name}[{ph0}-{ph1}] vs {n2}")
        self.recs.append((name, off, size, a, b))
        return self.nc.alloc_sbuf_tensor_at(name, shape, dtype, offset=off)


class Ctr:
    def __init__(self, handle):
        self.h = handle
        self.n = 0

    def inc(self, k=1):
        self.n += k
        return self.n


class K:
    DBG_H = 3

    def __init__(self, debug=0):
        self.debug = debug
        self.nc = bass.Bass("TRN2", target_bir_lowering=False)
        self.ops = {e: [] for e in ("tensor", "vector", "scalar", "gpsimd", "sync")}
        self.bankfree = {}
        self.last_dne = [0] * NQB
        self._dw_u = 0
        self._xnb_pe = [None, None]
        self._krc_dv = [None, None]
        self._pend_epilogue = None

    def t(self, fn): self.ops["tensor"].append(fn)
    def v(self, fn): self.ops["vector"].append(fn)
    def s(self, fn): self.ops["scalar"].append(fn)
    def g(self, fn): self.ops["gpsimd"].append(fn)
    def y(self, fn): self.ops["sync"].append(fn)

    def ctr(self, eng):
        return {"tensor": self.pe, "vector": self.dv, "scalar": self.ac,
                "gpsimd": self.gp}[eng]

    def dma(self, engq, sem, out_ap, in_ap, waits=()):
        pre = sem.n
        t = sem.inc(16)
        def th(e, pre=pre, out_ap=out_ap, in_ap=in_ap, waits=tuple(waits), sem=sem):
            for eng, val in waits:
                e.wait_ge(self.ctr(eng).h, val)
            e.wait_ge(sem.h, pre)
            e.dma_start(out_ap, in_ap).then_inc(sem.h, 16)
        self.ops[engq].append(th)
        return t

    def wait_bf(self, e, f):
        if f is not None:
            e.wait_ge(self.ctr(f[0]).h, f[1])

    def build(self):
        nc = self.nc
        self.declare_io()
        self.alloc()
        with ExitStack() as stack:
            for nm in ("pe", "dv", "ac", "gp", "swa", "swo"):
                setattr(self, nm, Ctr(stack.enter_context(nc.semaphore(nm))))
            for nm in ("sckc", "scks", "suk", "suv", "suq", "sqr", "sgw", "suw"):
                setattr(self, nm, [Ctr(stack.enter_context(nc.semaphore(f"{nm}{i}"))) for i in range(2)])
            self.sx = [Ctr(stack.enter_context(nc.semaphore(f"sx{i}"))) for i in range(5)]
            self.sdw = [Ctr(stack.enter_context(nc.semaphore(f"sdw{i}"))) for i in range(4)]
            self.so = [Ctr(stack.enter_context(nc.semaphore(f"so{i}"))) for i in range(4)]
            block = stack.enter_context(nc.Block())
            self.emit_all()
            for eng in ("tensor", "vector", "scalar", "gpsimd", "sync"):
                lst = self.ops[eng]
                getattr(block, eng)(lambda e, _l=lst: [f(e) for f in _l])
        return nc

    def declare_io(self):
        nc = self.nc
        di = {}
        def inp(name, shape, dtype=F32):
            di[name] = nc.dram_tensor(name, shape, dtype, kind="ExternalInput").ap()
        inp("x_full", [S, D]); inp("x_own", [TOK, D])
        inp("wdkv", [D, LAT]); inp("wdq", [D, LAT]); inp("wkr", [D, PD])
        inp("wuk", [LAT, D]); inp("wuv", [LAT, D]); inp("wuq", [LAT, D])
        inp("wqr", [LAT, H * PD]); inp("wo", [D, D])
        inp("cosk", [PD, S]); inp("sink", [PD, S])
        inp("cosq", [PD, TOK]); inp("sinq", [PD, TOK])
        inp("rt", [PD, PD]); inp("ident", [P, P]); inp("rwsg", [P, NKT * 9])
        for e in range(E_):
            inp(f"gupt{e}", [D, 2 * IM])
            inp(f"downt{e}", [IM, D])
        inp("shgt", [D, ISH]); inp("shupt", [D, ISH]); inp("shdt", [ISH, D])
        self.io = di
        self.out = nc.dram_tensor("out", [TOK, D], F32, kind="ExternalOutput").ap()
        self.dbg = {}
        if self.debug:
            def dout(name, shape, dtype=F32):
                self.dbg[name] = nc.dram_tensor(name, shape, dtype, kind="ExternalOutput").ap()
            dout("d_ckvt", [P, NMT * S], BF16)
            dout("d_krt", [PD + 1, S], BF16)
            dout("d_cqt", [P, NMT * TOK], BF16)
            dout("d_kct", [P, S], BF16)
            dout("d_qct", [P, TOK], BF16)
            dout("d_qrt", [PD + 1, TOK], BF16)
            dout("d_vt", [P, 32 * 130], BF16)
            dout("d_ot", [P, H * TOK], BF16)
            dout("d_accp3", [P, NQB * D])
            dout("d_accp4s", [P, NQB * D])
            dout("d_comb", [P, NQB * 8])
            dout("d_sig", [P, NQB])
            dout("d_xnt", [P, NKT * TOK], BF16)
            dout("d_shact", [P, IMT * TOK], BF16)
            dout("d_bnd", [1, 2])

    def emit_dbg(self, pairs, waits):
        if not self.debug:
            return
        sem = self.so[3]
        for name, src in pairs:
            pre = sem.n
            sem.inc(16)
            def th(sy, name=name, src=src, pre=pre, waits=tuple(waits), sem=sem):
                for eng, val in waits:
                    sy.wait_ge(self.ctr(eng).h, val)
                sy.wait_ge(sem.h, pre)
                sy.dma_start(self.dbg[name], src).then_inc(sem.h, 16)
            self.y(th)
        tso = sem.n
        pre_dv = self.dv.n
        tdv = self.dv.inc()
        self.v(lambda v, w=tso, p=pre_dv: (v.wait_ge(sem.h, w), v.wait_ge(self.dv.h, p),
                                           v.memset(self.dbg_v[:, :], 0.0).then_inc(self.dv.h, 1)))
        tpe = self.pe.inc()
        self.t(lambda t, w=tso: (t.wait_ge(sem.h, w), t.nop().then_inc(self.pe.h, 1)))
        pre_ac = self.ac.n
        tac = self.ac.inc()
        self.s(lambda s, w=tso, p=pre_ac: (s.wait_ge(sem.h, w), s.wait_ge(self.ac.h, p),
                                           s.activation(self.dbg_s[:, :], self.eps[:, :], AF.Copy).then_inc(self.ac.h, 1)))

    def alloc(self):
        nc = self.nc
        lay = Lay(nc)
        self.lay = lay
        cur = [SBUF_LO]
        def seq(name, shape, dtype):
            cur[0] = (cur[0] + 63) // 64 * 64
            t = lay.at(name, shape, dtype, cur[0], "p1", "p4")
            cur[0] += int(np.prod(shape[1:])) * _dtsize(dtype)
            return t
        self.eps = seq("eps", [P, 1], F32)
        self.ones128f = seq("ones128f", [P, 1], F32)
        self.onesrow = seq("onesrow", [1, 1024], BF16)
        self.ident_bf = seq("ident_bf", [P, P], BF16)
        self.rt_bf = seq("rt_bf", [PD, PD], BF16)
        self.bn_a = seq("bn_a", [P, 96], F32)
        self.mv2_a = seq("mv2_a", [P, 8], F32)
        self.mean_a = seq("mean_a", [P, 4], F32)
        self.sq_a = seq("sq_a", [P, 4], F32)
        self.rstd_a4 = seq("rstd_a4", [P, 4], F32)
        self.negm_a4 = seq("negm_a4", [P, 4], F32)
        self.kabs = seq("kabs", [P, 1], F32)
        self.qabs = seq("qabs", [P, 1], F32)
        self.qabs2 = seq("qabs2", [P, 1], F32)
        self.kr2s = seq("kr2s", [1, 1], F32)
        self.bsum = seq("bsum", [1, 2], F32)
        self.negb = seq("negb", [1, 1], F32)
        self.rd = seq("rd", [P, 1], F32)
        self.combine = seq("combine", [P, NQB * 8], F32)
        self.dbg_v = seq("dbg_v", [P, 1], F32)
        self.dbg_s = seq("dbg_s", [P, 1], F32)
        self.sig = seq("sig", [P, NQB], F32)
        self.negm_sv = seq("negm_sv", [P, NQB], F32)
        self.rstd_sv = seq("rstd_sv", [P, NQB], F32)
        A = (cur[0] + 1023) // 1024 * 1024
        self.wdkv_b = lay.at("wdkv_b", [P, NKT * LAT], BF16, A, "p1")
        self.wkr_b = lay.at("wkr_b", [P, NKT * PD], BF16, A + 16 * KB, "p1")
        self.coskb = [lay.at(f"coskb{i}", [PD, 512], F32, A + 18 * KB + 2 * KB * i, "p1") for i in range(2)]
        self.sinkb = [lay.at(f"sinkb{i}", [PD, 512], F32, A + 22 * KB + 2 * KB * i, "p1") for i in range(2)]
        self.ropet1 = lay.at("ropet1", [PD, 512], F32, A + 26 * KB, "p1", "at")
        self.ropet2 = lay.at("ropet2", [PD, 512], F32, A + 28 * KB, "p1", "at")
        self.kr_bf = lay.at("kr_bf", [PD, 512], BF16, A + 30 * KB, "p1")
        self.wdq_b = lay.at("wdq_b", [P, NKT * LAT], BF16, A + 31 * KB, "p1")
        self.xn_bf = [lay.at(f"xn_bf{i}", [P, D], BF16, A + 47 * KB + 4 * KB * i, "p1", "p4") for i in range(2)]
        AEND = A + 55 * KB
        self.ukb = [lay.at(f"ukb{i}", [P, NMT * P], BF16, A + KB * i, "at") for i in range(2)]
        self.uvb = [lay.at(f"uvb{i}", [P, NMT * P], BF16, A + 2 * KB + KB * i, "at") for i in range(2)]
        self.uqb = [lay.at(f"uqb{i}", [P, NMT * P], BF16, A + 4 * KB + KB * i, "at") for i in range(2)]
        self.qrb = [lay.at(f"qrb{i}", [P, NMT * PD], BF16, A + 6 * KB + 512 * i, "at") for i in range(2)]
        self.e_bf = ([lay.at(f"e_bf{i}", [P, 512], BF16, A + 7 * KB + KB * i, "at") for i in range(2)]
                     + [lay.at(f"e_bf{i+2}", [P, 512], BF16, A + 24 * KB + KB * i, "at") for i in range(2)])
        self.o_bf = lay.at("o_bf", [P, P], BF16, A + 9 * KB, "at")
        self.qct = lay.at("qct", [P, TOK], BF16, A + 10 * KB, "at")
        self.qrt = lay.at("qrt", [P, TOK], BF16, A + 12 * KB, "at")
        self.qr_bf = lay.at("qr_bf", [PD, 512], BF16, A + 14 * KB, "at")
        self.cosq = lay.at("cosq", [PD, TOK], F32, A + 16 * KB, "at")
        self.sinq = lay.at("sinq", [PD, TOK], F32, A + 20 * KB, "at")
        self.oacc = lay.at("oacc", [P, 8 * 132], F32, A + 31 * KB, "at")
        self.xwo = [lay.at(f"xwo{i}", [P, TOK], F32, A + 4 * KB * i, "p3") for i in range(4)]
        self.gwb = [lay.at(f"gwb{i}", [P, NKT * P], BF16, A + 4 * KB * i, "p4") for i in range(2)]
        self.uwb = [lay.at(f"uwb{i}", [P, NKT * P], BF16, A + 8 * KB + 4 * KB * i, "p4") for i in range(2)]
        self.dwb = ([lay.at(f"dwb{i}", [P, 2048], BF16, A + 16 * KB + 4 * KB * i, "p4") for i in range(2)]
                    + [lay.at(f"dwb{i+2}", [P, 2048], BF16, A + 32 * KB + 4 * KB * i, "p4") for i in range(2)])
        self.rwsg_b = lay.at("rwsg_b", [P, NKT * 9], BF16, A + 24 * KB, "p4")
        self.probs_all = lay.at("probs_all", [P, NQB * 8], F32, A + 25 * KB, "p4").ap()
        self.mask_all = lay.at("mask_all", [P, NQB * 8], F32, A + 25 * KB + 256, "p4").ap()
        self.keep_all = lay.at("keep_all", [P, NQB * 8], F32, A + 25 * KB + 512, "p4").ap()
        self.m1s = lay.at("m1s", [P, NQB], F32, A + 25 * KB + 768, "p4").ap()
        self.m2s = lay.at("m2s", [P, NQB], F32, A + 25 * KB + 832, "p4").ap()
        self.d8 = lay.at("d8", [P, NQB], F32, A + 25 * KB + 896, "p4").ap()
        self.r8 = lay.at("r8", [P, NQB], F32, A + 25 * KB + 960, "p4").ap()
        self.silu_t = [lay.at(f"silu_t{i}", [P, 512], F32, A + 28 * KB + 2 * KB * i, "p4") for i in range(2)]
        R = AEND
        self.xbuf = lay.at("xbuf", [P, 5 * D], F32, R, "p1")
        self.oT = lay.at("oT", [P, H * TOK], BF16, R, "at", "p3")
        self.xnt = lay.at("xnt", [P, NKT * TOK], BF16, R, "p4")
        self.xnt_scr = lay.at("xnt_scr", [P, NKT * 512], BF16, R + 40 * KB, "p1")
        self.kct = lay.at("kct", [P, S], BF16, R + 32 * KB, "at")
        self.wo_b = lay.at("wo_b", [P, NKT * TOK], BF16, R + 32 * KB, "p3")
        self.shact = lay.at("shact", [P, IMT * TOK], BF16, R + 32 * KB, "p4")
        self.vbuf = lay.at("vbuf", [P, 32 * 130], BF16, R + 48 * KB, "at")
        self.actt = lay.at("actt", [P, IMT * TOK], BF16, R + 54 * KB, "p4")
        CK = R + 57 * KB
        self.ckvt = lay.at("ckvt", [P, NMT * S], BF16, CK, "p1", "at")
        self.krt = lay.at("krt", [P, S], BF16, CK + 64 * KB, "p1", "at")
        self.cqt = lay.at("cqt", [P, NMT * TOK], BF16, CK + 80 * KB, "p1", "at")
        assert CK + 88 * KB <= SBUF_HI
        OA = (SBUF_HI - 64 * KB) // 64 * 64
        self.out_acc = lay.at("out_acc", [P, NQB * D], F32, OA, "p3", "p4")
        assert OA >= R + 76 * KB, "out_acc overlaps p4 actt"
        assert OA >= R + 64 * KB, "out_acc overlaps p3 wo_b"
        self.pb = [nc.alloc_psum_tensor(f"pb{i}", [P, 512], F32) for i in range(8)]
        self.pb_bf = [b.ap().bitcast(BF16) for b in self.pb]

    def emit_all(self):
        self.emit_init()
        self.emit_stage_a()
        self.emit_cq()
        if self.debug < 9:
            self.emit_attention()
        if self.debug < 8:
            self.emit_wo()
            self.emit_ffn_pre()
            self.emit_shared()
            self.emit_moe()
        self.emit_store()

    def emit_init(self):
        io = self.io
        def ms(v):
            v.memset(self.eps[:, :], EPS)
            v.memset(self.ones128f[:, :], 1.0)
            v.memset(self.onesrow[:, :], 1.0)
            v.memset(self.krt.ap()[64:65, :], 1.0).then_inc(self.dv.h, 1)
        self.init_dv = self.dv.inc()
        self.v(ms)
        self.dma("gpsimd", self.swa, self.ident_bf.ap()[:, :], io["ident"][:, :])
        self.rt_ld = self.dma("gpsimd", self.swa, self.rt_bf.ap()[:, :], io["rt"][:, :])
        self.dma("gpsimd", self.swa,
                 self.wdkv_b.ap().rearrange("p (kt n) -> p kt n", n=LAT),
                 io["wdkv"].rearrange("(kt p) n -> p kt n", p=P))
        self.dma("gpsimd", self.swa,
                 self.wkr_b.ap().rearrange("p (kt n) -> p kt n", n=PD),
                 io["wkr"].rearrange("(kt p) n -> p kt n", p=P))
        self.wdq_ld = self.dma("gpsimd", self.swa,
                               self.wdq_b.ap().rearrange("p (kt n) -> p kt n", n=LAT),
                               io["wdq"].rearrange("(kt p) n -> p kt n", p=P))
        self.wa_ld = self.wdq_ld

    def emit_ln_bn(self, xap, slot, xwait=None):
        dv = self.dv
        j = slot % 4
        bn = self.bn_a[:, 24 * j:24 * (j + 1)]
        pre = dv.n
        def st_(v, xap=xap, bn=bn, pre=pre, xw=xwait):
            v.wait_ge(self.dv.h, pre)
            if xw is not None:
                v.wait_ge(xw[0].h, xw[1])
            v.bn_stats(bn[:, 0:6], xap[:, 0:512])
            v.bn_stats(bn[:, 6:12], xap[:, 512:1024])
            v.bn_stats(bn[:, 12:18], xap[:, 1024:1536])
            v.bn_stats(bn[:, 18:24], xap[:, 1536:2048]).then_inc(self.dv.h, 1)
        t = dv.inc()
        self.v(st_)
        return t

    def emit_ln_mv4(self, t_bns, save_st=None):
        dv, ac = self.dv, self.ac
        if save_st is None:
            rstd_a, negm_a = self.rstd_a4[:, 0:4], self.negm_a4[:, 0:4]
        else:
            rstd_a = self.rstd_sv[:, 4 * save_st:4 * save_st + 4]
            negm_a = self.negm_sv[:, 4 * save_st:4 * save_st + 4]
        negm_cols = [negm_a[:, tt:tt + 1] for tt in range(4)]
        pre = dv.n
        def agg(v, w=t_bns, pre=pre):
            v.wait_ge(self.dv.h, max(w, pre))
            for tt in range(3):
                v.bn_aggr(self.mv2_a[:, 2 * tt:2 * tt + 2], self.bn_a[:, 24 * tt:24 * tt + 24])
            v.bn_aggr(self.mv2_a[:, 6:8], self.bn_a[:, 72:96]).then_inc(self.dv.h, 1)
        tagg = dv.inc()
        self.v(agg)
        pre_ac = ac.n
        tac = ac.inc()
        def sq4(s, w=tagg, pre=pre_ac):
            s.wait_ge(self.dv.h, w)
            s.wait_ge(self.ac.h, pre)
            for tt in range(3):
                s.activation(self.sq_a[:, tt:tt + 1], self.mv2_a[:, 2 * tt + 1:2 * tt + 2],
                             AF.Sqrt, bias=self.eps[:, 0:1])
            s.activation(self.sq_a[:, 3:4], self.mv2_a[:, 7:8],
                         AF.Sqrt, bias=self.eps[:, 0:1]).then_inc(self.ac.h, 1)
        self.s(sq4)
        def fin(v, w=tac, rstd_a=rstd_a, negm_cols=tuple(negm_cols)):
            v.wait_ge(self.ac.h, w)
            v.reciprocal(rstd_a, self.sq_a[:, 0:4])
            for tt in range(3):
                v.tensor_scalar(negm_cols[tt], self.mv2_a[:, 2 * tt:2 * tt + 1], -1.0, None, ALU.mult)
            v.tensor_scalar(negm_cols[3], self.mv2_a[:, 6:7], -1.0, None, ALU.mult).then_inc(self.dv.h, 1)
        t_fin = dv.inc()
        self.v(fin)
        return rstd_a, negm_a, t_fin

    def emit_ln_apply(self, xap, xn_out, rstd, negm, t_fin):
        dv = self.dv
        def apply(v, xap=xap, xn_out=xn_out, w=t_fin, rstd=rstd, negm=negm):
            v.wait_ge(self.dv.h, w)
            v.tensor_scalar(xn_out[:, :], xap[:, :], negm, rstd,
                            ALU.add, ALU.mult).then_inc(self.dv.h, 1)
        t2 = dv.inc()
        self.v(apply)
        return t2

    def emit_transpose16(self, xn_ap, dst_cols, wait_dv, evac="vector", banks=(2, 3)):
        pe = self.pe
        for half in range(2):
            ev = evac if evac != "mixed" else ("vector" if half == 0 else "scalar")
            bank = banks[half]
            def dotp(t, xn_ap=xn_ap, half=half, bank=bank, wdv=wait_dv, fb=self.bankfree.get(bank)):
                t.wait_ge(self.dv.h, wdv)
                t.wait_ge(self.swa.h, 16)
                self.wait_bf(t, fb)
                for i in range(7):
                    kt = half * 8 + i
                    t.transpose(self.pb_bf[bank][:, P * i:P * (i + 1)],
                                xn_ap[:, P * kt:P * (kt + 1)], self.ident_bf.ap()[:, :])
                t.transpose(self.pb_bf[bank][:, P * 7:P * 8],
                            xn_ap[:, P * (half * 8 + 7):P * (half * 8 + 8)],
                            self.ident_bf.ap()[:, :]).then_inc(self.pe.h, 1)
            tpe = pe.inc()
            self.t(dotp)
            if ev == "vector":
                def doev(v, half=half, bank=bank, dst_cols=dst_cols, w=tpe):
                    v.wait_ge(self.pe.h, w)
                    for i in range(7):
                        v.tensor_copy(dst_cols[half * 8 + i], self.pb_bf[bank][:, P * i:P * (i + 1)])
                    v.tensor_copy(dst_cols[half * 8 + 7], self.pb_bf[bank][:, P * 7:P * 8]).then_inc(self.dv.h, 1)
                tdv = self.dv.inc()
                self.v(doev)
                self.bankfree[bank] = ("vector", tdv)
            else:
                def doev(s, half=half, bank=bank, dst_cols=dst_cols, w=tpe, pre=self.ac.n):
                    s.wait_ge(self.pe.h, w)
                    s.wait_ge(self.ac.h, pre)
                    for i in range(7):
                        s.copy(dst_cols[half * 8 + i], self.pb_bf[bank][:, P * i:P * (i + 1)])
                    s.copy(dst_cols[half * 8 + 7], self.pb_bf[bank][:, P * 7:P * 8]).then_inc(self.ac.h, 1)
                tac = self.ac.inc()
                self.s(doev)
                self.bankfree[bank] = ("scalar", tac)
        return pe.n, (self.dv.n if evac == "vector" else None)

    def emit_stage_a(self):
        io = self.io
        xsrc = io["x_full"].rearrange("(g p) d -> g p d", p=P)
        self._stage_tokens(xsrc, 2 * NST, True)

    def _stage_tokens(self, xsrc, nhalves, is_full):
        nst = nhalves // 2
        xfree = [self.dv.n] * 5
        stats_p = {}
        mvd = {}
        def emit_ls(st):
            for tt in range(4):
                gi = st * 4 + tt
                slot = gi % 5
                tsx = self.dma("sync", self.sx[slot],
                               self.xbuf.ap().rearrange("p (f d) -> p f d", d=D)[:, slot, :],
                               xsrc[gi], [("vector", xfree[slot])])
                xap = self.xbuf.ap().rearrange("p (f d) -> p f d", d=D)[:, slot, :]
                stats_p[gi] = [tt, xap, tsx, slot]
        def emit_bn(gi):
            tt, xap, tsx, slot = stats_p[gi]
            return self.emit_ln_bn(xap, tt, xwait=(self.sx[slot], tsx))
        def emit_mv4(st, t_bns):
            mvd[st] = self.emit_ln_mv4(t_bns, save_st=(None if is_full else st))
        emit_ls(0)
        for tt in range(3):
            emit_bn(tt)
        emit_mv4(0, emit_bn(3))
        for st in range(nst):
            rstd_a, negm_a, t_fin = mvd.pop(st)
            for tt in range(4):
                gi = st * 4 + tt
                ptt, pxap, _, pslot = stats_p.pop(gi)
                self._flush_ln(ptt, pxap, rstd_a[:, tt:tt + 1], negm_a[:, tt:tt + 1], t_fin)
                xfree[pslot] = self.dv.n
            xnt_ready = ("vector", self.dv.n, "scalar", self.ac.n)
            tcb = None
            if st + 1 < nst:
                emit_ls(st + 1)
                def tile_cb(mt, st=st):
                    if mt < 2:
                        emit_bn((st + 1) * 4 + mt)
                    else:
                        emit_bn((st + 1) * 4 + 2)
                        emit_mv4(st + 1, emit_bn((st + 1) * 4 + 3))
                tcb = tile_cb
            if is_full:
                self._stagea_proj(st, xnt_ready, tile_cb=tcb)
            else:
                self._cq_proj(st, xnt_ready)
                if tcb is not None:
                    for mt in range(3):
                        tcb(mt)

    def _flush_ln(self, tt, xap, rstd, negm, t_fin):
        xnb = self.xn_bf[tt % 2]
        prev_pe = self._xnb_pe[tt % 2]
        if prev_pe is not None:
            self.v(lambda v, w=prev_pe: v.wait_ge(self.pe.h, w))
        wdv = self.emit_ln_apply(xap, xnb.ap(), rstd, negm, t_fin)
        dst = [self.xnt_scr.ap()[:, kt * 512 + tt * P: kt * 512 + tt * P + P] for kt in range(NKT)]
        pec, dvc = self.emit_transpose16(xnb.ap(), dst, wdv, evac="mixed",
                                         banks=((2, 3) if tt % 2 == 0 else (4, 5)))
        self._xnb_pe[tt % 2] = pec

    def _stagea_proj(self, st, xnt_ready, tile_cb=None):
        io, pe, dv = self.io, self.pe, self.dv
        slot = st % 2
        kw = [] if self._krc_dv[slot] is None else [("vector", self._krc_dv[slot])]
        t_c = self.dma("gpsimd", self.sckc[slot], self.coskb[slot].ap()[:, :],
                       io["cosk"][:, 512 * st:512 * (st + 1)], kw)
        t_s = self.dma("gpsimd", self.scks[slot], self.sinkb[slot].ap()[:, :],
                       io["sink"][:, 512 * st:512 * (st + 1)], kw)
        def krmm(t, st=st, w=xnt_ready, wld=self.wa_ld, fb=self.bankfree.get(4)):
            t.wait_ge(self.dv.h, w[1])
            t.wait_ge(self.ac.h, w[3])
            self.wait_bf(t, fb)
            t.wait_ge(self.swa.h, wld)
            for kt in range(NKT - 1):
                t.matmul(self.pb[4][0:PD, :],
                         self.wkr_b.ap()[:, kt * PD:(kt + 1) * PD],
                         self.xnt_scr.ap()[:, kt * 512:(kt + 1) * 512],
                         start=(kt == 0), stop=False)
            kt = NKT - 1
            t.matmul(self.pb[4][0:PD, :],
                     self.wkr_b.ap()[:, kt * PD:(kt + 1) * PD],
                     self.xnt_scr.ap()[:, kt * 512:(kt + 1) * 512],
                     start=False, stop=True).then_inc(self.pe.h, 1)
        tpe_kr = pe.inc()
        self.t(krmm)
        def krev(v, w=tpe_kr):
            v.wait_ge(self.pe.h, w)
            v.tensor_copy(self.kr_bf.ap()[:, :], self.pb[4][0:PD, :]).then_inc(self.dv.h, 1)
        tdv_krbf = dv.inc()
        self.v(krev)
        def krrot(t, w=tdv_krbf, wrt=self.rt_ld, fb=self.bankfree.get(5)):
            t.wait_ge(self.dv.h, w)
            self.wait_bf(t, fb)
            t.wait_ge(self.swa.h, wrt)
            t.matmul(self.pb[5][0:PD, :], self.rt_bf.ap()[:, :], self.kr_bf.ap()[:, :],
                     start=True, stop=True).then_inc(self.pe.h, 1)
        tpe_rot = pe.inc()
        self.t(krrot)
        pre_kr = dv.n
        def krcomb(v, st=st, slot=slot, w=tpe_rot, wc=t_c, ws=t_s, pre=pre_kr):
            v.wait_ge(self.pe.h, w)
            v.wait_ge(self.sckc[slot].h, wc)
            v.wait_ge(self.scks[slot].h, ws)
            v.wait_ge(self.dv.h, pre)
            v.tensor_tensor(self.ropet1.ap()[:, :], self.pb[4][0:PD, :], self.coskb[slot].ap()[:, :], ALU.mult)
            v.tensor_tensor(self.ropet2.ap()[:, :], self.pb[5][0:PD, :], self.sinkb[slot].ap()[:, :], ALU.mult).then_inc(self.dv.h, 1)
            v.wait_ge(self.dv.h, pre + 1)
            v.tensor_tensor(self.krt.ap()[0:PD, 512 * st:512 * (st + 1)],
                            self.ropet1.ap()[:, :], self.ropet2.ap()[:, :], ALU.add).then_inc(self.dv.h, 1)
        dv.inc()
        tdv_kr = dv.inc()
        self.v(krcomb)
        self._krc_dv[slot] = tdv_kr
        self.bankfree[4] = ("vector", tdv_kr)
        self.bankfree[5] = ("vector", tdv_kr)
        for mt in range(NMT):
            bank = mt % 2
            def ckv(t, st=st, mt=mt, bank=bank, w=xnt_ready, wld=self.wa_ld, fb=self.bankfree.get(bank)):
                t.wait_ge(self.dv.h, w[1])
                t.wait_ge(self.ac.h, w[3])
                self.wait_bf(t, fb)
                t.wait_ge(self.swa.h, wld)
                for kt in range(NKT - 1):
                    t.matmul(self.pb[bank][:, :],
                             self.wdkv_b.ap()[:, kt * LAT + mt * P: kt * LAT + (mt + 1) * P],
                             self.xnt_scr.ap()[:, kt * 512:(kt + 1) * 512],
                             start=(kt == 0), stop=False)
                kt = NKT - 1
                t.matmul(self.pb[bank][:, :],
                         self.wdkv_b.ap()[:, kt * LAT + mt * P: kt * LAT + (mt + 1) * P],
                         self.xnt_scr.ap()[:, kt * 512:(kt + 1) * 512],
                         start=False, stop=True).then_inc(self.pe.h, 1)
            tpe = pe.inc()
            self.t(ckv)
            def ckve(v, st=st, mt=mt, bank=bank, w=tpe):
                v.wait_ge(self.pe.h, w)
                v.tensor_copy(self.ckvt.ap()[:, mt * S + st * 512: mt * S + st * 512 + 512],
                              self.pb[bank][:, :]).then_inc(self.dv.h, 1)
            tdv = dv.inc()
            self.v(ckve)
            self.bankfree[bank] = ("vector", tdv)
            if tile_cb is not None and mt < 3:
                tile_cb(mt)

    def _cq_proj(self, j, xnt_ready):
        pe, dv = self.pe, self.dv
        for mt in range(NMT):
            bank = mt % 2
            def cq(t, j=j, mt=mt, bank=bank, w=xnt_ready, wld=self.wdq_ld, fb=self.bankfree.get(bank)):
                t.wait_ge(self.dv.h, w[1])
                t.wait_ge(self.ac.h, w[3])
                self.wait_bf(t, fb)
                t.wait_ge(self.swa.h, wld)
                for kt in range(NKT - 1):
                    t.matmul(self.pb[bank][:, :],
                             self.wdq_b.ap()[:, kt * LAT + mt * P: kt * LAT + (mt + 1) * P],
                             self.xnt_scr.ap()[:, kt * 512:(kt + 1) * 512],
                             start=(kt == 0), stop=False)
                kt = NKT - 1
                t.matmul(self.pb[bank][:, :],
                         self.wdq_b.ap()[:, kt * LAT + mt * P: kt * LAT + (mt + 1) * P],
                         self.xnt_scr.ap()[:, kt * 512:(kt + 1) * 512],
                         start=False, stop=True).then_inc(self.pe.h, 1)
            tpe = pe.inc()
            self.t(cq)
            def cqe(v, j=j, mt=mt, bank=bank, w=tpe):
                v.wait_ge(self.pe.h, w)
                v.tensor_copy(self.cqt.ap()[:, mt * TOK + j * 512: mt * TOK + j * 512 + 512],
                              self.pb[bank][:, :]).then_inc(self.dv.h, 1)
            tdv = dv.inc()
            self.v(cqe)
            self.bankfree[bank] = ("vector", tdv)

    def emit_cq(self):
        io = self.io
        xsrc = io["x_own"].rearrange("(g p) d -> g p d", p=P)
        self._stage_tokens(xsrc, 4, False)
        wkr = self.dv.n
        def krb(v, w=wkr):
            v.wait_ge(self.dv.h, w)
            v.reduce_max(self.kabs[0:PD, :], self.krt.ap()[0:PD, :], AX.X,
                         apply_absolute_value=True).then_inc(self.dv.h, 1)
        t1 = self.dv.inc()
        self.v(krb)
        t2 = self.ac.inc()
        self.s(lambda s, w=t1: (s.wait_ge(self.dv.h, w),
                                s.activation(self.kabs[0:PD, :], self.kabs[0:PD, :], AF.Square).then_inc(self.ac.h, 1)))
        def krmm(t, w=t2, fb=self.bankfree.get(0)):
            t.wait_ge(self.ac.h, w)
            self.wait_bf(t, fb)
            t.matmul(self.pb[0][0:1, 0:1], self.ones128f[0:PD, :], self.kabs[0:PD, :],
                     start=True, stop=True).then_inc(self.pe.h, 1)
        t3 = self.pe.inc()
        self.t(krmm)
        def kre(v, w=t3):
            v.wait_ge(self.pe.h, w)
            v.tensor_copy(self.kr2s[0:1, :], self.pb[0][0:1, 0:1]).then_inc(self.dv.h, 1)
        self.kr2s_dv = self.dv.inc()
        self.v(kre)
        self.bankfree[0] = ("vector", self.kr2s_dv)
        self.emit_dbg([("d_ckvt", self.ckvt.ap()[:, :]),
                       ("d_krt", self.krt.ap()[0:PD + 1, :]),
                       ("d_cqt", self.cqt.ap()[:, :])],
                      [("vector", self.dv.n)])

    def emit_attention(self):
        io = self.io
        p1_pe, p1_dv = self.pe.n, self.dv.n
        pre_v = self.dv.n
        def msv(v, wp=p1_pe, pre=pre_v):
            v.wait_ge(self.pe.h, wp)
            v.wait_ge(self.dv.h, pre)
            v.memset(self.vbuf.ap()[:, :], 0.0).then_inc(self.dv.h, 1)
            v.wait_ge(self.dv.h, pre + 1)
            ones_cols = self.vbuf.ap().rearrange("p (k c) -> p k c", c=130)[:, :, 128:129]
            v.memset(ones_cols, 1.0).then_inc(self.dv.h, 1)
        self.dv.inc(2)
        self.v(msv)
        self.cosq_ld = self.dma("gpsimd", self.sckc[0], self.cosq.ap()[:, :], io["cosq"][:, :],
                                [("tensor", p1_pe), ("vector", p1_dv)])
        self.sinq_ld = self.dma("gpsimd", self.scks[0], self.sinq.ap()[:, :], io["sinq"][:, :],
                                [("tensor", p1_pe), ("vector", p1_dv)])
        uk_src = io["wuk"].rearrange("(mt l) n -> l mt n", l=P)
        uv_src = io["wuv"].rearrange("(mt l) n -> l mt n", l=P)
        uq_src = io["wuq"].rearrange("(mt l) n -> l mt n", l=P)
        qr_src = io["wqr"].rearrange("(mt l) n -> l mt n", l=P)
        hwfree = [None, None]
        for h in range(H):
            par = h % 2
            if h == 0:
                waits = [("tensor", p1_pe), ("vector", p1_dv)]
            elif hwfree[par] is None:
                waits = []
            else:
                waits = [("tensor", hwfree[par])]
            wld = {}
            wld["uk"] = self.dma("gpsimd", self.suk[par],
                                 self.ukb[par].ap().rearrange("p (mt n) -> p mt n", n=P),
                                 uk_src[:, :, h * P:(h + 1) * P], waits)
            wld["uv"] = self.dma("gpsimd", self.suv[par],
                                 self.uvb[par].ap().rearrange("p (mt n) -> p mt n", n=P),
                                 uv_src[:, :, h * P:(h + 1) * P], waits)
            wld["uq"] = self.dma("gpsimd", self.suq[par],
                                 self.uqb[par].ap().rearrange("p (mt n) -> p mt n", n=P),
                                 uq_src[:, :, h * P:(h + 1) * P], waits)
            wld["qr"] = self.dma("gpsimd", self.sqr[par],
                                 self.qrb[par].ap().rearrange("p (mt n) -> p mt n", n=PD),
                                 qr_src[:, :, h * PD:(h + 1) * PD], waits)
            self.emit_head(h, wld, par)
            hwfree[par] = self._head_w_done
        if self._pend_epilogue is not None:
            self._pend_epilogue()
            self._pend_epilogue = None
        self.emit_dbg([("d_ot", self.oT.ap()[:, :])], [("vector", self.dv.n)])

    def emit_head(self, h, wld, par):
        pe, dv, ac = self.pe, self.dv, self.ac
        dbg_me = self.debug and h == self.DBG_H
        O_BANKS = (2, 3, 6, 7)
        for ch in range(16):
            bank = (0, 1, 4, 5)[ch % 4]
            def kup(t, h=h, ch=ch, bank=bank, par=par, w=wld["uk"], fb=self.bankfree.get(bank)):
                t.wait_ge(self.suk[par].h, w)
                self.wait_bf(t, fb)
                for mt in range(NMT - 1):
                    t.matmul(self.pb[bank][:, :],
                             self.ukb[par].ap()[:, mt * P:(mt + 1) * P],
                             self.ckvt.ap()[:, mt * S + ch * 512: mt * S + ch * 512 + 512],
                             start=(mt == 0), stop=False)
                mt = NMT - 1
                t.matmul(self.pb[bank][:, :],
                         self.ukb[par].ap()[:, mt * P:(mt + 1) * P],
                         self.ckvt.ap()[:, mt * S + ch * 512: mt * S + ch * 512 + 512],
                         start=False, stop=True).then_inc(self.pe.h, 1)
            tpe = pe.inc()
            self.t(kup)
            if ch % 2 == 0:
                def kupe(v, ch=ch, bank=bank, w=tpe):
                    v.wait_ge(self.pe.h, w)
                    v.tensor_copy(self.kct.ap()[:, ch * 512:(ch + 1) * 512],
                                  self.pb[bank][:, :]).then_inc(self.dv.h, 1)
                tdv = dv.inc()
                self.v(kupe)
                self.bankfree[bank] = ("vector", tdv)
            else:
                def kupe_s(s, ch=ch, bank=bank, w=tpe, pre=self.ac.n):
                    s.wait_ge(self.pe.h, w)
                    s.wait_ge(self.ac.h, pre)
                    s.copy(self.kct.ap()[:, ch * 512:(ch + 1) * 512],
                           self.pb[bank][:, :]).then_inc(self.ac.h, 1)
                tac = ac.inc()
                self.s(kupe_s)
                self.bankfree[bank] = ("scalar", tac)
        kct_dv, kct_ac = self.dv.n, self.ac.n
        if self._pend_epilogue is not None:
            self._pend_epilogue()
            self._pend_epilogue = None
        for ch in range(2):
            bank = ch % 2
            def qc(t, ch=ch, bank=bank, par=par, w=wld["uq"], fb=self.bankfree.get(bank)):
                t.wait_ge(self.suq[par].h, w)
                self.wait_bf(t, fb)
                for mt in range(NMT - 1):
                    t.matmul(self.pb[bank][:, :],
                             self.uqb[par].ap()[:, mt * P:(mt + 1) * P],
                             self.cqt.ap()[:, mt * TOK + ch * 512: mt * TOK + ch * 512 + 512],
                             start=(mt == 0), stop=False)
                mt = NMT - 1
                t.matmul(self.pb[bank][:, :],
                         self.uqb[par].ap()[:, mt * P:(mt + 1) * P],
                         self.cqt.ap()[:, mt * TOK + ch * 512: mt * TOK + ch * 512 + 512],
                         start=False, stop=True).then_inc(self.pe.h, 1)
            tpe = pe.inc()
            self.t(qc)
            def qce(v, ch=ch, bank=bank, w=tpe):
                v.wait_ge(self.pe.h, w)
                v.tensor_copy(self.qct.ap()[:, ch * 512:(ch + 1) * 512], self.pb[bank][:, :]).then_inc(self.dv.h, 1)
            tdv = dv.inc()
            self.v(qce)
            self.bankfree[bank] = ("vector", tdv)
        def qr_stage_a(ch):
            def qr(t, ch=ch, par=par, w=wld["qr"], fb=self.bankfree.get(4)):
                t.wait_ge(self.sqr[par].h, w)
                self.wait_bf(t, fb)
                for mt in range(NMT - 1):
                    t.matmul(self.pb[4][0:PD, :],
                             self.qrb[par].ap()[:, mt * PD:(mt + 1) * PD],
                             self.cqt.ap()[:, mt * TOK + ch * 512: mt * TOK + ch * 512 + 512],
                             start=(mt == 0), stop=False)
                mt = NMT - 1
                t.matmul(self.pb[4][0:PD, :],
                         self.qrb[par].ap()[:, mt * PD:(mt + 1) * PD],
                         self.cqt.ap()[:, mt * TOK + ch * 512: mt * TOK + ch * 512 + 512],
                         start=False, stop=True).then_inc(self.pe.h, 1)
            tpe_qr = pe.inc()
            self.t(qr)
            def qrev(v, w=tpe_qr):
                v.wait_ge(self.pe.h, w)
                v.tensor_copy(self.qr_bf.ap()[:, :], self.pb[4][0:PD, :]).then_inc(self.dv.h, 1)
            tdv_qrbf = dv.inc()
            self.v(qrev)
            return tdv_qrbf
        def qr_stage_b(ch, tdv_qrbf):
            def qrrot(t, w=tdv_qrbf, fb=self.bankfree.get(5)):
                t.wait_ge(self.dv.h, w)
                self.wait_bf(t, fb)
                t.matmul(self.pb[5][0:PD, :], self.rt_bf.ap()[:, :], self.qr_bf.ap()[:, :],
                         start=True, stop=True).then_inc(self.pe.h, 1)
            tpe_rot = pe.inc()
            self.t(qrrot)
            pre_qr = dv.n
            def qrcomb(v, ch=ch, w=tpe_rot, wc=self.cosq_ld, ws=self.sinq_ld, pre=pre_qr):
                v.wait_ge(self.pe.h, w)
                v.wait_ge(self.sckc[0].h, wc)
                v.wait_ge(self.scks[0].h, ws)
                v.wait_ge(self.dv.h, pre)
                v.tensor_tensor(self.ropet1.ap()[:, :], self.pb[4][0:PD, :],
                                self.cosq.ap()[:, ch * 512:(ch + 1) * 512], ALU.mult)
                v.tensor_tensor(self.ropet2.ap()[:, :], self.pb[5][0:PD, :],
                                self.sinq.ap()[:, ch * 512:(ch + 1) * 512], ALU.mult).then_inc(self.dv.h, 1)
                v.wait_ge(self.dv.h, pre + 1)
                v.tensor_tensor(self.qrt.ap()[0:PD, 512 * ch:512 * (ch + 1)],
                                self.ropet1.ap()[:, :], self.ropet2.ap()[:, :], ALU.add).then_inc(self.dv.h, 1)
            dv.inc()
            tdv_qr = dv.inc()
            self.v(qrcomb)
            self.bankfree[4] = ("vector", tdv_qr)
            self.bankfree[5] = ("vector", tdv_qr)
        def emit_qb():
            qdone = self.dv.n
            def qb_v(v, w=qdone):
                v.wait_ge(self.dv.h, w)
                v.reduce_max(self.qabs[:, :], self.qct.ap()[:, :], AX.X, apply_absolute_value=True)
                v.reduce_max(self.qabs2[0:PD, :], self.qrt.ap()[0:PD, :], AX.X,
                             apply_absolute_value=True).then_inc(self.dv.h, 1)
            t1q = dv.inc()
            self.v(qb_v)
            t2q = ac.inc()
            def qb_s(s, w=t1q):
                s.wait_ge(self.dv.h, w)
                s.activation(self.qabs[:, :], self.qabs[:, :], AF.Square)
                s.activation(self.qabs2[0:PD, :], self.qabs2[0:PD, :], AF.Square).then_inc(self.ac.h, 1)
            self.s(qb_s)
            return t2q
        def kb_chunk(j):
            def kb(v, j=j, wd=kct_dv, wa=kct_ac):
                v.wait_ge(self.dv.h, wd)
                v.wait_ge(self.ac.h, wa)
                v.reduce_max(self.mean_a[:, j:j + 1], self.kct.ap()[:, j * 2048:(j + 1) * 2048],
                             AX.X, apply_absolute_value=True).then_inc(self.dv.h, 1)
            dv.inc()
            self.v(kb)
        def kb_fin():
            pre = dv.n
            def kf(v, pre=pre):
                v.wait_ge(self.dv.h, pre)
                v.reduce_max(self.kabs[:, :], self.mean_a[:, 0:4], AX.X,
                             apply_absolute_value=True).then_inc(self.dv.h, 1)
            t1k = dv.inc()
            self.v(kf)
            t2k = ac.inc()
            self.s(lambda s, w=t1k: (s.wait_ge(self.dv.h, w),
                                     s.activation(self.kabs[:, :], self.kabs[:, :], AF.Square).then_inc(self.ac.h, 1)))
            return t2k

        def emit_bound_mms(t2k, t2q):
            def kb_mm(t, w=t2k, fb=self.bankfree.get(0)):
                t.wait_ge(self.ac.h, w)
                self.wait_bf(t, fb)
                t.matmul(self.pb[0][0:1, 0:1], self.ones128f[:, :], self.kabs[:, :],
                         start=True, stop=True).then_inc(self.pe.h, 1)
            t3 = pe.inc()
            self.t(kb_mm)
            pre_kb = dv.n
            def kb_e(v, w=t3, wkr=self.kr2s_dv, pre=pre_kb):
                v.wait_ge(self.pe.h, w)
                v.wait_ge(self.dv.h, max(wkr, pre))
                v.tensor_copy(self.bsum[0:1, 0:1], self.pb[0][0:1, 0:1]).then_inc(self.dv.h, 1)
                v.wait_ge(self.dv.h, pre + 1)
                v.tensor_tensor(self.bsum[0:1, 0:1], self.bsum[0:1, 0:1], self.kr2s[0:1, 0:1],
                                ALU.add).then_inc(self.dv.h, 1)
            dv.inc()
            kb_dv = dv.inc()
            self.v(kb_e)
            self.bankfree[0] = ("vector", kb_dv)
            def qb_mm(t, w=t2q, fb=self.bankfree.get(1)):
                t.wait_ge(self.ac.h, w)
                self.wait_bf(t, fb)
                t.matmul(self.pb[1][0:1, 1:2], self.ones128f[:, :], self.qabs[:, :], start=True, stop=False)
                t.matmul(self.pb[1][0:1, 1:2], self.ones128f[0:PD, :], self.qabs2[0:PD, :],
                         start=False, stop=True).then_inc(self.pe.h, 1)
            t3q = pe.inc()
            self.t(qb_mm)
            pre_qb = dv.n
            def qb_e(v, w=t3q, pre=pre_qb):
                v.wait_ge(self.pe.h, w)
                v.wait_ge(self.dv.h, pre)
                v.tensor_copy(self.bsum[0:1, 1:2], self.pb[1][0:1, 1:2]).then_inc(self.dv.h, 1)
                v.wait_ge(self.dv.h, pre + 1)
                v.tensor_mul(self.negb[0:1, :], self.bsum[0:1, 0:1], self.bsum[0:1, 1:2]).then_inc(self.dv.h, 1)
            dv.inc()
            t5 = dv.inc()
            self.v(qb_e)
            self.bankfree[1] = ("vector", t5)
            t6 = ac.inc()
            self.s(lambda s, w=t5: (s.wait_ge(self.dv.h, w),
                                    s.activation(self.negb[0:1, :], self.negb[0:1, :], AF.Sqrt).then_inc(self.ac.h, 1)))
            pre_r64 = dv.n
            def row64(v, w=t6, pre=pre_r64):
                v.wait_ge(self.ac.h, w)
                v.wait_ge(self.dv.h, pre)
                v.tensor_scalar(self.negb[0:1, :], self.negb[0:1, :], -1.0, None, ALU.mult).then_inc(self.dv.h, 1)
                v.wait_ge(self.dv.h, pre + 1)
                v.tensor_scalar(self.qrt.ap()[PD:PD + 1, :], self.onesrow.ap()[0:1, :],
                                self.negb[0:1, 0:1], None, ALU.mult).then_inc(self.dv.h, 1)
            dv.inc()
            b_dv = dv.inc()
            self.v(row64)
            return b_dv
        bound_dv = None
        t2k = t2q = None
        t_qrev = [None, None]
        for vh in range(2):
            for bg in range(8):
                bank = O_BANKS[bg % 4]
                ktl0 = bg * 4
                def vupg(t, vh=vh, bank=bank, ktl0=ktl0, par=par, w=wld["uv"],
                         fb=self.bankfree.get(bank)):
                    t.wait_ge(self.suv[par].h, w)
                    self.wait_bf(t, fb)
                    for kti in range(4):
                        kt = vh * 32 + ktl0 + kti
                        for mt in range(NMT):
                            ins = t.matmul(self.pb[bank][:, kti * P:(kti + 1) * P],
                                           self.ckvt.ap()[:, mt * S + kt * P: mt * S + (kt + 1) * P],
                                           self.uvb[par].ap()[:, mt * P:(mt + 1) * P],
                                           start=(kti == 0 and mt == 0),
                                           stop=(kti == 3 and mt == NMT - 1))
                    ins.then_inc(self.pe.h, 1)
                tpe = pe.inc()
                self.t(vupg)
                dstv = self.vbuf.ap().rearrange("p (k c) -> p k c", c=130)[:, ktl0:ktl0 + 4, 0:P]
                srcv = self.pb[bank].ap().rearrange("p (k c) -> p k c", c=P)[:, 0:4, :]
                if bg % 2 == 0:
                    def vupe(v, dstv=dstv, srcv=srcv, w=tpe):
                        v.wait_ge(self.pe.h, w)
                        v.tensor_copy(dstv, srcv).then_inc(self.dv.h, 1)
                    tdv = dv.inc()
                    self.v(vupe)
                    self.bankfree[bank] = ("vector", tdv)
                else:
                    def vupe_s(s, dstv=dstv, srcv=srcv, w=tpe, pre=self.ac.n):
                        s.wait_ge(self.pe.h, w)
                        s.wait_ge(self.ac.h, pre)
                        s.copy(dstv, srcv).then_inc(self.ac.h, 1)
                    tac = ac.inc()
                    self.s(vupe_s)
                    self.bankfree[bank] = ("scalar", tac)
                if vh == 0:
                    if bg == 0:
                        t_qrev[0] = qr_stage_a(0)
                    elif bg == 1:
                        qr_stage_b(0, t_qrev[0])
                        t_qrev[1] = qr_stage_a(1)
                    elif bg == 2:
                        qr_stage_b(1, t_qrev[1])
                        t2q = emit_qb()
                    elif bg == 3:
                        kb_chunk(0)
                        kb_chunk(1)
                    elif bg == 4:
                        kb_chunk(2)
                        kb_chunk(3)
                    elif bg == 5:
                        t2k = kb_fin()
            if vh == 0:
                bound_dv = emit_bound_mms(t2k, t2q)
                if dbg_me:
                    self.emit_dbg([("d_kct", self.kct.ap()[:, :]),
                                   ("d_qct", self.qct.ap()[:, :]),
                                   ("d_qrt", self.qrt.ap()[0:PD + 1, :]),
                                   ("d_bnd", self.bsum[0:1, 0:2])],
                                  [("vector", bound_dv)])
            vdone = self.dv.n
            vdone_ac = self.ac.n
            if dbg_me and vh == 1:
                self.emit_dbg([("d_vt", self.vbuf.ap()[:, :])], [("vector", vdone), ("scalar", vdone_ac)])
            for qc in range(NQC):
                SC_BANKS = (0, 1, 4, 5)
                tac_list = [None] * 32
                def emit_sc(ktl, qc=qc, vh=vh, vdone=vdone, vdone_ac=vdone_ac):
                    kt = vh * 32 + ktl
                    bank = SC_BANKS[ktl % 4]
                    def sc(t, qc=qc, kt=kt, ktl=ktl, bank=bank, wb=bound_dv, wv=vdone, wva=vdone_ac,
                           fb=self.bankfree.get(bank)):
                        if ktl == 0 and qc == 0:
                            t.wait_ge(self.dv.h, max(wb, wv))
                            t.wait_ge(self.ac.h, wva)
                        self.wait_bf(t, fb)
                        t.matmul(self.pb[bank][:, :],
                                 self.kct.ap()[:, kt * P:(kt + 1) * P],
                                 self.qct.ap()[:, qc * 512:(qc + 1) * 512],
                                 start=True, stop=False)
                        t.matmul(self.pb[bank][:, :],
                                 self.krt.ap()[0:PD + 1, kt * P:(kt + 1) * P],
                                 self.qrt.ap()[0:PD + 1, qc * 512:(qc + 1) * 512],
                                 start=False, stop=True).then_inc(self.pe.h, 1)
                    tpe_sc = self.pe.inc()
                    self.t(sc)
                    def ex(s, bank=bank, w=tpe_sc):
                        s.wait_ge(self.pe.h, w)
                        s.activation(self.e_bf[bank if bank < 2 else bank - 2].ap()[:, :],
                                     self.pb[bank][:, :], AF.Exp,
                                     scale=ATT_SCALE).then_inc(self.ac.h, 1)
                    tac = self.ac.inc()
                    self.s(ex)
                    tac_list[ktl] = tac
                    self.bankfree[bank] = ("scalar", tac)
                emit_sc(0)
                emit_sc(1)
                emit_sc(2)
                for ktl in range(32):
                    if ktl + 3 < 32:
                        emit_sc(ktl + 3)
                    sb = SC_BANKS[ktl % 4]
                    ebuf = self.e_bf[sb if sb < 2 else sb - 2]
                    def om(t, ktl=ktl, ebuf=ebuf, w=tac_list[ktl],
                           fbs=tuple(self.bankfree.get(b) for b in O_BANKS) if ktl == 0 else ()):
                        t.wait_ge(self.ac.h, w)
                        for f in fbs:
                            self.wait_bf(t, f)
                        for j in range(4):
                            ins = t.matmul(self.pb[O_BANKS[j]][:, 0:129],
                                           ebuf.ap()[:, j * P:(j + 1) * P],
                                           self.vbuf.ap()[:, ktl * 130: ktl * 130 + 129],
                                           start=(ktl == 0), stop=(ktl == 31))
                        ins.then_inc(self.pe.h, 1)
                    tpe_o = self.pe.inc()
                    self.t(om)
                pre_oe = dv.n
                def oev(v, qc=qc, vh=vh, w=tpe_o, pre=pre_oe):
                    v.wait_ge(self.pe.h, w)
                    v.wait_ge(self.dv.h, pre)
                    for j in range(4):
                        dst = self.oacc.ap()[:, (qc * 4 + j) * 132: (qc * 4 + j) * 132 + 129]
                        if vh == 0:
                            ins = v.tensor_copy(dst, self.pb[O_BANKS[j]][:, 0:129])
                        else:
                            ins = v.tensor_tensor(dst, dst, self.pb[O_BANKS[j]][:, 0:129], ALU.add)
                    ins.then_inc(self.dv.h, 1)
                t_oe = dv.inc()
                self.v(oev)
                for b in O_BANKS:
                    self.bankfree[b] = ("vector", t_oe)
        oacc_dv = self.dv.n
        def emit_epilogue(h=h, oacc_dv=oacc_dv):
            dv, pe = self.dv, self.pe
            for qb in range(NQB):
                off = qb * 132
                pre_ep = dv.n
                prev_tp = getattr(self, "_prev_obf_pe", None)
                def ep12(v, off=off, w=oacc_dv, pre=pre_ep, ptp=prev_tp):
                    if ptp is not None:
                        v.wait_ge(self.pe.h, ptp)
                    v.wait_ge(self.dv.h, max(w, pre))
                    v.reciprocal(self.rd[:, :], self.oacc.ap()[:, off + 128:off + 129]).then_inc(self.dv.h, 1)
                    v.wait_ge(self.dv.h, pre + 1)
                    v.tensor_scalar(self.o_bf.ap()[:, :], self.oacc.ap()[:, off:off + P],
                                    self.rd[:, 0:1], None, ALU.mult).then_inc(self.dv.h, 1)
                dv.inc()
                t2 = dv.inc()
                self.v(ep12)
                def ep3(t, w=t2, fb=self.bankfree.get(1)):
                    t.wait_ge(self.dv.h, w)
                    self.wait_bf(t, fb)
                    t.transpose(self.pb_bf[1][:, 0:P], self.o_bf.ap()[:, :],
                                self.ident_bf.ap()[:, :]).then_inc(self.pe.h, 1)
                t3 = pe.inc()
                self._prev_obf_pe = t3
                self.t(ep3)
                def ep4(v, h=h, qb=qb, w=t3):
                    v.wait_ge(self.pe.h, w)
                    v.tensor_copy(self.oT.ap()[:, h * TOK + qb * P: h * TOK + (qb + 1) * P],
                                  self.pb_bf[1][:, 0:P]).then_inc(self.dv.h, 1)
                t4 = dv.inc()
                self.v(ep4)
                self.bankfree[1] = ("vector", t4)
        self._pend_epilogue = emit_epilogue
        self._head_w_done = self.pe.n

    def emit_wo(self):
        io, pe, dv = self.io, self.pe, self.dv
        attn_pe = self.pe.n
        attn_dv = self.dv.n
        wo_src = io["wo"].rearrange("(kt p) n -> p kt n", p=P)
        xsrc = io["x_own"].rearrange("(qb p) d -> qb p d", p=P)
        half_pe = [None, None]
        prev_hf_dv = attn_dv
        wo_sems = [self.swo, self.sckc[0]]
        for hf in range(2):
            wwaits = [("tensor", attn_pe)] if hf == 0 else [("tensor", half_pe[0])]
            wo_lds = []
            for wc in range(8):
                wo_lds.append(self.dma(
                    "gpsimd", wo_sems[wc % 2],
                    self.wo_b.ap().rearrange("p (kt n) -> p kt n", n=TOK)[:, 2 * wc:2 * (wc + 1), :],
                    wo_src[:, 2 * wc:2 * (wc + 1), hf * TOK:(hf + 1) * TOK],
                    wwaits if wc < 2 else []))
            xld = [None] * NQB
            for qb in range(NQB):
                xwaits = [("vector", prev_hf_dv)] if qb < 2 else [("vector", xld[qb - 2])]
                tsx = self.dma("sync", self.sx[qb % 2], self.xwo[qb % 2].ap()[:, :],
                               xsrc[qb][:, hf * TOK:(hf + 1) * TOK], xwaits)
                for ch in range(2):
                    bank = (qb * 2 + ch) % 4
                    def wo(t, hf=hf, qb=qb, ch=ch, bank=bank, wlds=tuple(wo_lds), fb=self.bankfree.get(bank)):
                        self.wait_bf(t, fb)
                        for kt in range(NKT):
                            if kt % 2 == 0:
                                t.wait_ge(wo_sems[(kt // 2) % 2].h, wlds[kt // 2])
                            ins = t.matmul(self.pb[bank][:, :],
                                           self.oT.ap()[:, kt * TOK + qb * P: kt * TOK + (qb + 1) * P],
                                           self.wo_b.ap()[:, kt * TOK + ch * 512: kt * TOK + ch * 512 + 512],
                                           start=(kt == 0), stop=(kt == NKT - 1))
                        ins.then_inc(self.pe.h, 1)
                    tpe = pe.inc()
                    self.t(wo)
                    def woe(v, hf=hf, qb=qb, ch=ch, bank=bank, w=tpe, wx=tsx):
                        v.wait_ge(self.pe.h, w)
                        v.wait_ge(self.sx[qb % 2].h, wx)
                        off = qb * D + hf * TOK + ch * 512
                        v.tensor_tensor(self.out_acc.ap()[:, off:off + 512], self.pb[bank][:, :],
                                        self.xwo[qb % 2].ap()[:, ch * 512:(ch + 1) * 512],
                                        ALU.add).then_inc(self.dv.h, 1)
                    tdv = dv.inc()
                    self.v(woe)
                    self.bankfree[bank] = ("vector", tdv)
                xld[qb] = dv.n
            half_pe[hf] = self.pe.n
            prev_hf_dv = self.dv.n
        self.p3_dv = self.dv.n
        self.emit_dbg([("d_accp3", self.out_acc.ap()[:, :])], [("vector", self.dv.n)])

    def emit_ffn_pre(self):
        io, pe, dv, ac = self.io, self.pe, self.dv, self.ac
        rw_ld = self.dma("gpsimd", self.swa, self.rwsg_b.ap()[:, :], io["rwsg"][:, :],
                         [("vector", self.p3_dv)])
        xsrc = io["x_own"].rearrange("(qb p) d -> qb p d", p=P)
        xld = [None] * NQB
        base_dv = self.p3_dv
        tpe_r = [None] * NQB
        for qb in range(NQB):
            p_ = qb % 2
            b0, b1 = self.xwo[2 * p_], self.xwo[2 * p_ + 1]
            if qb < 2:
                xwaits = [("vector", self.p3_dv)]
            else:
                xwaits = [("vector", xld[qb - 2])]
            tx0 = self.dma("sync", self.sx[0], b0.ap()[:, :], xsrc[qb][:, 0:TOK], xwaits)
            tx1 = self.dma("sync", self.sx[1], b1.ap()[:, :], xsrc[qb][:, TOK:D], xwaits)
            xnb = self.xn_bf[p_]
            prev_pe = self._xnb_pe[p_]
            negm = self.negm_sv[:, qb:qb + 1]
            rstd = self.rstd_sv[:, qb:qb + 1]
            def apply(v, xnb=xnb, a=tx0, b=tx1, b0=b0, b1=b1, negm=negm, rstd=rstd,
                      w=base_dv, wp=prev_pe):
                v.wait_ge(self.sx[0].h, a)
                v.wait_ge(self.sx[1].h, b)
                v.wait_ge(self.dv.h, w)
                if wp is not None:
                    v.wait_ge(self.pe.h, wp)
                v.tensor_scalar(xnb.ap()[:, 0:TOK], b0.ap()[:, :], negm, rstd, ALU.add, ALU.mult)
                v.tensor_scalar(xnb.ap()[:, TOK:D], b1.ap()[:, :], negm, rstd,
                                ALU.add, ALU.mult).then_inc(self.dv.h, 1)
            t2 = dv.inc()
            self.v(apply)
            dst = [self.xnt.ap()[:, kt * TOK + qb * P: kt * TOK + (qb + 1) * P] for kt in range(NKT)]
            pec, dvc = self.emit_transpose16(xnb.ap(), dst, t2, evac="mixed",
                                             banks=((2, 3) if qb % 2 == 0 else (4, 5)))
            self._xnb_pe[p_] = pec
            xld[qb] = t2
            xnt_dv, xnt_ac = self.dv.n, self.ac.n
            def rmm(t, qb=qb, w=xnt_dv, wa=xnt_ac, wld=rw_ld,
                    fb=(self.bankfree.get(7) if qb == 0 else None)):
                t.wait_ge(self.dv.h, w)
                t.wait_ge(self.ac.h, wa)
                t.wait_ge(self.swa.h, wld)
                self.wait_bf(t, fb)
                for kt in range(NKT - 1):
                    t.matmul(self.pb[7][:, qb * 9: qb * 9 + 9],
                             self.xnt.ap()[:, kt * TOK + qb * P: kt * TOK + (qb + 1) * P],
                             self.rwsg_b.ap()[:, kt * 9: kt * 9 + 9],
                             start=(qb == 0 and kt == 0), stop=False)
                kt = NKT - 1
                t.matmul(self.pb[7][:, qb * 9: qb * 9 + 9],
                         self.xnt.ap()[:, kt * TOK + qb * P: kt * TOK + (qb + 1) * P],
                         self.rwsg_b.ap()[:, kt * 9: kt * 9 + 9],
                         start=False, stop=(qb == NQB - 1)).then_inc(self.pe.h, 1)
            tpe_r[qb] = pe.inc()
            self.t(rmm)
        self.p4a_apply_dv = self.dv.n
        pre_ac2 = ac.n
        def sigs(s, w=tpe_r[NQB - 1], pre=pre_ac2):
            s.wait_ge(self.pe.h, w)
            s.wait_ge(self.ac.h, pre)
            for qb in range(NQB):
                ins = s.activation(self.sig[:, qb:qb + 1],
                                   self.pb[7][:, qb * 9 + 8: qb * 9 + 9], AF.Sigmoid)
            ins.then_inc(self.ac.h, 1)
        t_sig = ac.inc()
        self.s(sigs)
        pre_dv2 = dv.n
        def rmx(v, w=tpe_r[NQB - 1], pre=pre_dv2):
            v.wait_ge(self.pe.h, w)
            v.wait_ge(self.dv.h, pre)
            for qb in range(NQB):
                ins = v.reduce_max(self.m1s[:, qb:qb + 1],
                                   self.pb[7][:, qb * 9: qb * 9 + 8], AX.X, negate=True)
            ins.then_inc(self.dv.h, 1)
        t_rmx = dv.inc()
        self.v(rmx)
        def exps(s, w=t_rmx, pre=t_sig):
            s.wait_ge(self.dv.h, w)
            s.wait_ge(self.ac.h, pre)
            for qb in range(NQB):
                ins = s.activation(self.probs_all[:, qb * 8:(qb + 1) * 8],
                                   self.pb[7][:, qb * 9: qb * 9 + 8],
                                   AF.Exp, bias=self.m1s[:, qb:qb + 1])
            ins.then_inc(self.ac.h, 1)
        t_exp = ac.inc()
        self.s(exps)
        self.bankfree[7] = ("scalar", t_exp)
        b2 = dv.n
        def top2b(v, w=t_exp, b=b2):
            v.wait_ge(self.ac.h, w)
            v.wait_ge(self.dv.h, b)
            for qb in range(NQB):
                ins = v.reduce_max(self.m1s[:, qb:qb + 1],
                                   self.probs_all[:, qb * 8:(qb + 1) * 8], AX.X)
            ins.then_inc(self.dv.h, 1)
            v.wait_ge(self.dv.h, b + 1)
            for qb in range(NQB):
                ins = v.tensor_scalar(self.mask_all[:, qb * 8:(qb + 1) * 8],
                                      self.probs_all[:, qb * 8:(qb + 1) * 8],
                                      self.m1s[:, qb:qb + 1], None, ALU.is_ge)
            ins.then_inc(self.dv.h, 1)
            v.wait_ge(self.dv.h, b + 2)
            v.scalar_tensor_tensor(self.mask_all[:, :], self.mask_all[:, :], -2.0,
                                   self.probs_all[:, :], ALU.mult, ALU.add).then_inc(self.dv.h, 1)
            v.wait_ge(self.dv.h, b + 3)
            for qb in range(NQB):
                ins = v.reduce_max(self.m2s[:, qb:qb + 1],
                                   self.mask_all[:, qb * 8:(qb + 1) * 8], AX.X)
            ins.then_inc(self.dv.h, 1)
            v.wait_ge(self.dv.h, b + 4)
            for qb in range(NQB):
                v.tensor_scalar(self.keep_all[:, qb * 8:(qb + 1) * 8],
                                self.probs_all[:, qb * 8:(qb + 1) * 8],
                                self.m2s[:, qb:qb + 1], None, ALU.is_ge)
            v.tensor_tensor(self.d8[:, :], self.m1s[:, :], self.m2s[:, :],
                            ALU.add).then_inc(self.dv.h, 1)
            v.wait_ge(self.dv.h, b + 5)
            v.reciprocal(self.r8[:, :], self.d8[:, :]).then_inc(self.dv.h, 1)
            v.wait_ge(self.dv.h, b + 6)
            for qb in range(NQB):
                ins = v.scalar_tensor_tensor(self.combine[:, qb * 8:(qb + 1) * 8],
                                             self.probs_all[:, qb * 8:(qb + 1) * 8],
                                             self.r8[:, qb:qb + 1],
                                             self.keep_all[:, qb * 8:(qb + 1) * 8],
                                             ALU.mult, ALU.mult)
            ins.then_inc(self.dv.h, 1)
        dv.inc(6)
        t_comb = dv.inc()
        self.v(top2b)
        self.ffnpre_dv = self.dv.n
        self.emit_dbg([("d_xnt", self.xnt.ap()[:, :]),
                       ("d_comb", self.combine[:, :]),
                       ("d_sig", self.sig[:, :])],
                      [("vector", self.dv.n), ("scalar", self.ac.n)])

    def emit_gu_pair(self, gbuf, ubuf, gsem, usem, gld, uld, dst_cols):
        pe, dv, ac = self.pe, self.dv, self.ac
        for ch in range(2):
            bga, bgb = 2 * ch, 2 * ch + 1
            def gu(t, ch=ch, bga=bga, bgb=bgb, gbuf=gbuf, ubuf=ubuf, gsem=gsem, usem=usem, gld=gld, uld=uld,
                   fa=self.bankfree.get(bga), fbb=self.bankfree.get(bgb)):
                t.wait_ge(gsem.h, gld)
                t.wait_ge(usem.h, uld)
                self.wait_bf(t, fa)
                self.wait_bf(t, fbb)
                for kt in range(NKT):
                    t.matmul(self.pb[bga][:, :], gbuf.ap()[:, kt * P:(kt + 1) * P],
                             self.xnt.ap()[:, kt * TOK + ch * 512: kt * TOK + ch * 512 + 512],
                             start=(kt == 0), stop=(kt == NKT - 1))
                for kt in range(NKT - 1):
                    t.matmul(self.pb[bgb][:, :], ubuf.ap()[:, kt * P:(kt + 1) * P],
                             self.xnt.ap()[:, kt * TOK + ch * 512: kt * TOK + ch * 512 + 512],
                             start=(kt == 0), stop=False)
                kt = NKT - 1
                t.matmul(self.pb[bgb][:, :], ubuf.ap()[:, kt * P:(kt + 1) * P],
                         self.xnt.ap()[:, kt * TOK + ch * 512: kt * TOK + ch * 512 + 512],
                         start=False, stop=True).then_inc(self.pe.h, 1)
            tpe = pe.inc()
            self.t(gu)
            def sil(s, ch=ch, bga=bga, w=tpe):
                s.wait_ge(self.pe.h, w)
                s.activation(self.silu_t[ch].ap()[:, :], self.pb[bga][:, :], AF.Silu).then_inc(self.ac.h, 1)
            tac = ac.inc()
            self.s(sil)
            self.bankfree[bga] = ("scalar", tac)
            def mul(v, ch=ch, bgb=bgb, dst=dst_cols[ch], w=tac):
                v.wait_ge(self.ac.h, w)
                v.scalar_tensor_tensor(dst, self.pb[bgb][:, :], 1.0, self.silu_t[ch].ap()[:, :],
                                       ALU.mult, ALU.mult).then_inc(self.dv.h, 1)
            tdv = dv.inc()
            self.v(mul)
            self.bankfree[bgb] = ("vector", tdv)

    def emit_down(self, act_buf, dsrc4_fn, scal_fn):
        pe, dv = self.pe, self.dv
        act_ready = self.dv.n
        NG = (IMT + 3) // 4
        units = [(dc, g) for dc in range(4) for g in range(NG)]
        base_u = self._dw_u
        lds = {}
        def emit_ld(u):
            dc, g = units[u]
            par = (base_u + u) % 4
            free = self.dwfree.get(par)
            waits = [] if free is None else [("tensor", free)]
            nk = min(4, IMT - 4 * g)
            lds[u] = self.dma("gpsimd", self.sdw[par],
                              self.dwb[par].ap().rearrange("p (k n) -> p k n", n=512)[:, 0:nk, :],
                              dsrc4_fn(g, dc), waits)
        DW_DEPTH = 3
        for u in range(DW_DEPTH):
            emit_ld(u)
        for dc in range(4):
            for half in range(2):
                qbs = range(4) if half == 0 else range(4, NQB)
                for g in range(NG):
                    u = dc * NG + g
                    par = (base_u + u) % 4
                    nk = min(4, IMT - 4 * g)
                    for ki in range(nk):
                        kt = 4 * g + ki
                        def dn(t, kt=kt, ki=ki, dc=dc, par=par, wld=lds[u], act_buf=act_buf,
                               qbs=tuple(qbs),
                               fbs=tuple(self.bankfree.get(b) for b in qbs) if kt == 0 else (),
                               war=act_ready if (kt == 0 and dc == 0 and half == 0) else None):
                            t.wait_ge(self.sdw[par].h, wld)
                            if war is not None:
                                t.wait_ge(self.dv.h, war)
                            for f in fbs:
                                self.wait_bf(t, f)
                            for qb in qbs:
                                ins = t.matmul(self.pb[qb][:, :],
                                               act_buf.ap()[:, kt * TOK + qb * P: kt * TOK + (qb + 1) * P],
                                               self.dwb[par].ap()[:, ki * 512:(ki + 1) * 512],
                                               start=(kt == 0), stop=(kt == IMT - 1))
                            ins.then_inc(self.pe.h, 1)
                        tpe = pe.inc()
                        self.t(dn)
                    if half == 1:
                        self.dwfree[par] = tpe
                        if u + DW_DEPTH < len(units):
                            emit_ld(u + DW_DEPTH)
                last_pe = self.pe.n
                for qb in qbs:
                    def dne(v, qb=qb, dc=dc, w=last_pe, sc=scal_fn(qb)):
                        v.wait_ge(self.pe.h, w)
                        off = qb * D + dc * 512
                        v.scalar_tensor_tensor(self.out_acc.ap()[:, off:off + 512], self.pb[qb][:, :],
                                               sc, self.out_acc.ap()[:, off:off + 512],
                                               ALU.mult, ALU.add).then_inc(self.dv.h, 1)
                    tdv = dv.inc()
                    self.v(dne)
                    self.bankfree[qb] = ("vector", tdv)
                    self.last_dne[qb] = tdv
        self._dw_u += len(units)

    def emit_shared(self):
        io = self.io
        self.dwfree = {}
        self.gwfree = {}
        gsrc = io["shgt"].rearrange("(kt p) n -> p kt n", p=P)
        usrc = io["shupt"].rearrange("(kt p) n -> p kt n", p=P)
        pre = self.p4a_apply_dv
        for q in range(4):
            for mt in range(IMT):
                i_col = q * IM + mt * P
                par = mt % 2
                free = self.gwfree.get(par)
                waits = [("vector", pre)] if free is None else [("tensor", free)]
                gld = self.dma("gpsimd", self.sgw[par],
                               self.gwb[par].ap().rearrange("p (kt n) -> p kt n", n=P),
                               gsrc[:, :, i_col:i_col + P], waits)
                uld = self.dma("gpsimd", self.suw[par],
                               self.uwb[par].ap().rearrange("p (kt n) -> p kt n", n=P),
                               usrc[:, :, i_col:i_col + P], waits)
                dst = [self.shact.ap()[:, mt * TOK + ch * 512: mt * TOK + ch * 512 + 512] for ch in range(2)]
                self.emit_gu_pair(self.gwb[par], self.uwb[par], self.sgw[par], self.suw[par], gld, uld, dst)
                self.gwfree[par] = self.pe.n
            def dsrc4(g, dc, q=q):
                nk = min(4, IMT - 4 * g)
                return io["shdt"][q * IM + 4 * g * P: q * IM + (4 * g + nk) * P,
                                  dc * 512:(dc + 1) * 512].rearrange("(k p) n -> p k n", p=P)
            self.emit_down(self.shact, dsrc4, lambda qb: self.sig[:, qb:qb + 1])
        self.emit_dbg([("d_shact", self.shact.ap()[:, :]),
                       ("d_accp4s", self.out_acc.ap()[:, :])], [("vector", self.dv.n)])

    def emit_moe(self):
        io = self.io
        for e in range(E_):
            gsrc = io[f"gupt{e}"].rearrange("(kt p) n -> p kt n", p=P)
            for mt in range(IMT):
                par = mt % 2
                free = self.gwfree.get(par)
                waits = [] if free is None else [("tensor", free)]
                gld = self.dma("gpsimd", self.sgw[par],
                               self.gwb[par].ap().rearrange("p (kt n) -> p kt n", n=P),
                               gsrc[:, :, mt * P:(mt + 1) * P], waits)
                uld = self.dma("gpsimd", self.suw[par],
                               self.uwb[par].ap().rearrange("p (kt n) -> p kt n", n=P),
                               gsrc[:, :, IM + mt * P: IM + (mt + 1) * P], waits)
                dst = [self.actt.ap()[:, mt * TOK + ch * 512: mt * TOK + ch * 512 + 512] for ch in range(2)]
                self.emit_gu_pair(self.gwb[par], self.uwb[par], self.sgw[par], self.suw[par], gld, uld, dst)
                self.gwfree[par] = self.pe.n
            def dsrc4(g, dc, e=e):
                nk = min(4, IMT - 4 * g)
                return io[f"downt{e}"][4 * g * P:(4 * g + nk) * P,
                                       dc * 512:(dc + 1) * 512].rearrange("(k p) n -> p k n", p=P)
            def scal(qb, e=e):
                return self.combine[:, qb * 8 + e: qb * 8 + e + 1]
            self.emit_down(self.actt, dsrc4, scal)

    def emit_store(self):
        if self.debug >= 8:
            def fin(sy, tv=self.so[3].n):
                sy.wait_ge(self.so[3].h, tv)
            self.y(fin)
            return
        osrc = self.out.rearrange("(qb p) d -> qb p d", p=P)
        for qb in range(NQB):
            sem = self.so[qb % 4]
            pre = sem.n
            sem.inc(16)
            def th(sy, qb=qb, pre=pre, sem=sem, w=self.last_dne[qb]):
                sy.wait_ge(self.dv.h, w)
                sy.wait_ge(sem.h, pre)
                sy.dma_start(osrc[qb], self.out_acc.ap()[:, qb * D:(qb + 1) * D]).then_inc(sem.h, 16)
            self.y(th)
        def fin(sy, targets=tuple((i, self.so[i].n) for i in range(4))):
            for i, tv in targets:
                sy.wait_ge(self.so[i].h, tv)
        self.y(fin)


def rope_host_tables():
    inv_freq = 1.0 / 10000.0 ** (np.arange(0, PD, 2, dtype=np.float32) / PD)
    t = np.arange(S, dtype=np.float32)
    freqs = t[:, None] * inv_freq[None, :]
    emb = np.concatenate([freqs, freqs], axis=-1)
    return (np.ascontiguousarray(np.cos(emb).T.astype(np.float32)),
            np.ascontiguousarray(np.sin(emb).T.astype(np.float32)))


def _pack_rwsg(inputs):
    rw = np.concatenate([np.asarray(inputs["L0_ffn_router_weight"], dtype=np.float32).T,
                         np.asarray(inputs["L0_ffn_shared_gate_w"], dtype=np.float32).T], axis=1)
    return np.ascontiguousarray(rw.reshape(NKT, P, 9).transpose(1, 0, 2).reshape(P, NKT * 9))


def make_in_maps(inputs):
    x = np.ascontiguousarray(np.asarray(inputs["L0_attn_hidden_states"], dtype=np.float32).reshape(S, D))
    cosk, sink = rope_host_tables()
    rtm = np.zeros((PD, PD), dtype=np.float32)
    for i in range(32):
        rtm[i + 32, i] = -1.0
        rtm[i, i + 32] = 1.0
    common = {
        "x_full": x,
        "wdkv": np.ascontiguousarray(inputs["L0_attn_W_DKV"], dtype=np.float32),
        "wdq": np.ascontiguousarray(inputs["L0_attn_W_DQ"], dtype=np.float32),
        "wkr": np.ascontiguousarray(inputs["L0_attn_W_KR"], dtype=np.float32),
        "wuk": np.ascontiguousarray(inputs["L0_attn_W_UK"], dtype=np.float32),
        "wuv": np.ascontiguousarray(inputs["L0_attn_W_UV"], dtype=np.float32),
        "wuq": np.ascontiguousarray(inputs["L0_attn_W_UQ"], dtype=np.float32),
        "wqr": np.ascontiguousarray(inputs["L0_attn_W_QR"], dtype=np.float32),
        "wo": np.ascontiguousarray(inputs["L0_attn_W_O"], dtype=np.float32),
        "cosk": cosk, "sink": sink,
        "rt": rtm,
        "ident": np.eye(P, dtype=np.float32),
        "rwsg": _pack_rwsg(inputs),
        "shgt": np.ascontiguousarray(np.asarray(inputs["L0_ffn_sh_gate_w"], dtype=np.float32).T),
        "shupt": np.ascontiguousarray(np.asarray(inputs["L0_ffn_sh_up_w"], dtype=np.float32).T),
        "shdt": np.ascontiguousarray(np.asarray(inputs["L0_ffn_sh_down_w"], dtype=np.float32).T),
    }
    gup = np.asarray(inputs["L0_ffn_gate_up_proj"], dtype=np.float32)
    dwn = np.asarray(inputs["L0_ffn_down_proj"], dtype=np.float32)
    for e in range(E_):
        common[f"gupt{e}"] = np.ascontiguousarray(gup[e].T)
        common[f"downt{e}"] = np.ascontiguousarray(dwn[e].T)
    in_maps = []
    for c in range(NCORES):
        m = dict(common)
        m["x_own"] = np.ascontiguousarray(x[c * TOK:(c + 1) * TOK])
        m["cosq"] = np.ascontiguousarray(cosk[:, c * TOK:(c + 1) * TOK])
        m["sinq"] = np.ascontiguousarray(sink[:, c * TOK:(c + 1) * TOK])
        in_maps.append(m)
    return in_maps


def kernel(**inputs):
    k = K(debug=0)
    nc = k.build()
    in_maps = make_in_maps(inputs)
    res = run_bass_kernel_spmd(nc, in_maps, core_ids=list(range(NCORES)))
    out = np.concatenate([res.results[c]["out"] for c in range(NCORES)], axis=0)
    return out.reshape(1, S, D).astype(np.float32)


# version 24
# speedup vs baseline: 1.0334x; 1.0009x over previous
import numpy as np
from contextlib import ExitStack
import concourse.bass as bass
import concourse.mybir as mybir
from concourse.bass_utils import run_bass_kernel_spmd

F32 = mybir.dt.float32
BF16 = mybir.dt.bfloat16
AF = mybir.ActivationFunctionType
ALU = mybir.AluOpType
AX = mybir.AxisListType

NCORES = 8
P = 128
S = 8192
D = 2048
TOK = S // NCORES
LAT = 512
H, HD, PD = 16, 128, 64
NKT = D // P
NMT = LAT // P
NKEY = S // P
NST = S // 512
NQB = TOK // P
NQC = TOK // 512
E_ = 8
IM = 1408
IMT = IM // P
ISH = 5632
ATT_SCALE = float(1.0 / np.sqrt(HD + PD))
EPS = 1e-5

SBUF_LO = 16512
SBUF_HI = 229344
KB = 1024

PH = {"p1": 0, "at": 1, "p3": 2, "p4": 3}


def _dtsize(dt):
    return {F32: 4, BF16: 2}[dt]


class Lay:
    def __init__(self, nc):
        self.nc = nc
        self.recs = []

    def at(self, name, shape, dtype, off, ph0, ph1=None):
        ph1 = ph0 if ph1 is None else ph1
        a, b = PH[ph0], PH[ph1]
        size = int(np.prod(shape[1:])) * _dtsize(dtype)
        off = int(off)
        assert off % 64 == 0, f"{name} misaligned {off}"
        assert off >= SBUF_LO and off + size <= SBUF_HI, f"{name}: {off}..{off + size} out of range"
        for (n2, o2, s2, a2, b2) in self.recs:
            if off < o2 + s2 and o2 < off + size and a <= b2 and a2 <= b:
                raise AssertionError(f"SBUF overlap: {name}[{ph0}-{ph1}] vs {n2}")
        self.recs.append((name, off, size, a, b))
        return self.nc.alloc_sbuf_tensor_at(name, shape, dtype, offset=off)


class Ctr:
    def __init__(self, handle):
        self.h = handle
        self.n = 0

    def inc(self, k=1):
        self.n += k
        return self.n


class K:
    DBG_H = 3

    def __init__(self, debug=0):
        self.debug = debug
        self.nc = bass.Bass("TRN2", target_bir_lowering=False)
        self.ops = {e: [] for e in ("tensor", "vector", "scalar", "gpsimd", "sync")}
        self.bankfree = {}
        self.last_dne = [0] * NQB
        self._dw_u = 0
        self._xnb_pe = [None, None]
        self._krc_dv = [None, None]
        self._pend_epilogue = None

    def t(self, fn): self.ops["tensor"].append(fn)
    def v(self, fn): self.ops["vector"].append(fn)
    def s(self, fn): self.ops["scalar"].append(fn)
    def g(self, fn): self.ops["gpsimd"].append(fn)
    def y(self, fn): self.ops["sync"].append(fn)

    def ctr(self, eng):
        return {"tensor": self.pe, "vector": self.dv, "scalar": self.ac,
                "gpsimd": self.gp}[eng]

    def dma(self, engq, sem, out_ap, in_ap, waits=()):
        pre = sem.n
        t = sem.inc(16)
        def th(e, pre=pre, out_ap=out_ap, in_ap=in_ap, waits=tuple(waits), sem=sem):
            for eng, val in waits:
                e.wait_ge(self.ctr(eng).h, val)
            e.wait_ge(sem.h, pre)
            e.dma_start(out_ap, in_ap).then_inc(sem.h, 16)
        self.ops[engq].append(th)
        return t

    def wait_bf(self, e, f):
        if f is not None:
            e.wait_ge(self.ctr(f[0]).h, f[1])

    def build(self):
        nc = self.nc
        self.declare_io()
        self.alloc()
        with ExitStack() as stack:
            for nm in ("pe", "dv", "ac", "gp", "swa", "swo"):
                setattr(self, nm, Ctr(stack.enter_context(nc.semaphore(nm))))
            for nm in ("sckc", "scks", "suk", "suv", "suq", "sqr", "sgw", "suw"):
                setattr(self, nm, [Ctr(stack.enter_context(nc.semaphore(f"{nm}{i}"))) for i in range(2)])
            self.sx = [Ctr(stack.enter_context(nc.semaphore(f"sx{i}"))) for i in range(5)]
            self.sdw = [Ctr(stack.enter_context(nc.semaphore(f"sdw{i}"))) for i in range(4)]
            self.so = [Ctr(stack.enter_context(nc.semaphore(f"so{i}"))) for i in range(4)]
            block = stack.enter_context(nc.Block())
            self.emit_all()
            for eng in ("tensor", "vector", "scalar", "gpsimd", "sync"):
                lst = self.ops[eng]
                getattr(block, eng)(lambda e, _l=lst: [f(e) for f in _l])
        return nc

    def declare_io(self):
        nc = self.nc
        di = {}
        def inp(name, shape, dtype=F32):
            di[name] = nc.dram_tensor(name, shape, dtype, kind="ExternalInput").ap()
        inp("x_full", [S, D]); inp("x_own", [TOK, D])
        inp("wdkv", [D, LAT]); inp("wdq", [D, LAT]); inp("wkr", [D, PD])
        inp("wuk", [LAT, D]); inp("wuv", [LAT, D]); inp("wuq", [LAT, D])
        inp("wqr", [LAT, H * PD]); inp("wo", [D, D])
        inp("cosk", [PD, S]); inp("sink", [PD, S])
        inp("cosq", [PD, TOK]); inp("sinq", [PD, TOK])
        inp("rt", [PD, PD]); inp("ident", [P, P]); inp("rwsg", [P, NKT * 9])
        for e in range(E_):
            inp(f"gupt{e}", [D, 2 * IM])
            inp(f"downt{e}", [IM, D])
        inp("shgt", [D, ISH]); inp("shupt", [D, ISH]); inp("shdt", [ISH, D])
        self.io = di
        self.out = nc.dram_tensor("out", [TOK, D], F32, kind="ExternalOutput").ap()
        self.dbg = {}
        if self.debug:
            def dout(name, shape, dtype=F32):
                self.dbg[name] = nc.dram_tensor(name, shape, dtype, kind="ExternalOutput").ap()
            dout("d_ckvt", [P, NMT * S], BF16)
            dout("d_krt", [PD + 1, S], BF16)
            dout("d_cqt", [P, NMT * TOK], BF16)
            dout("d_kct", [P, S], BF16)
            dout("d_qct", [P, TOK], BF16)
            dout("d_qrt", [PD + 1, TOK], BF16)
            dout("d_vt", [P, 32 * 130], BF16)
            dout("d_ot", [P, H * TOK], BF16)
            dout("d_accp3", [P, NQB * D])
            dout("d_accp4s", [P, NQB * D])
            dout("d_comb", [P, NQB * 8])
            dout("d_sig", [P, NQB])
            dout("d_xnt", [P, NKT * TOK], BF16)
            dout("d_shact", [P, IMT * TOK], BF16)
            dout("d_bnd", [1, 2])

    def emit_dbg(self, pairs, waits):
        if not self.debug:
            return
        sem = self.so[3]
        for name, src in pairs:
            pre = sem.n
            sem.inc(16)
            def th(sy, name=name, src=src, pre=pre, waits=tuple(waits), sem=sem):
                for eng, val in waits:
                    sy.wait_ge(self.ctr(eng).h, val)
                sy.wait_ge(sem.h, pre)
                sy.dma_start(self.dbg[name], src).then_inc(sem.h, 16)
            self.y(th)
        tso = sem.n
        pre_dv = self.dv.n
        tdv = self.dv.inc()
        self.v(lambda v, w=tso, p=pre_dv: (v.wait_ge(sem.h, w), v.wait_ge(self.dv.h, p),
                                           v.memset(self.dbg_v[:, :], 0.0).then_inc(self.dv.h, 1)))
        tpe = self.pe.inc()
        self.t(lambda t, w=tso: (t.wait_ge(sem.h, w), t.nop().then_inc(self.pe.h, 1)))
        pre_ac = self.ac.n
        tac = self.ac.inc()
        self.s(lambda s, w=tso, p=pre_ac: (s.wait_ge(sem.h, w), s.wait_ge(self.ac.h, p),
                                           s.activation(self.dbg_s[:, :], self.eps[:, :], AF.Copy).then_inc(self.ac.h, 1)))

    def alloc(self):
        nc = self.nc
        lay = Lay(nc)
        self.lay = lay
        cur = [SBUF_LO]
        def seq(name, shape, dtype):
            cur[0] = (cur[0] + 63) // 64 * 64
            t = lay.at(name, shape, dtype, cur[0], "p1", "p4")
            cur[0] += int(np.prod(shape[1:])) * _dtsize(dtype)
            return t
        self.eps = seq("eps", [P, 1], F32)
        self.ones128f = seq("ones128f", [P, 1], F32)
        self.onesrow = seq("onesrow", [1, 1024], BF16)
        self.ident_bf = seq("ident_bf", [P, P], BF16)
        self.rt_bf = seq("rt_bf", [PD, PD], BF16)
        self.bn_a = seq("bn_a", [P, 96], F32)
        self.mv2_a = seq("mv2_a", [P, 8], F32)
        self.mean_a = seq("mean_a", [P, 4], F32)
        self.sq_a = seq("sq_a", [P, 4], F32)
        self.rstd_a4 = seq("rstd_a4", [P, 4], F32)
        self.negm_a4 = seq("negm_a4", [P, 4], F32)
        self.kabs = seq("kabs", [P, 1], F32)
        self.qabs = seq("qabs", [P, 1], F32)
        self.qabs2 = seq("qabs2", [P, 1], F32)
        self.kr2s = seq("kr2s", [1, 1], F32)
        self.bsum = seq("bsum", [1, 2], F32)
        self.negb = seq("negb", [1, 1], F32)
        self.rd = seq("rd", [P, 1], F32)
        self.combine = seq("combine", [P, NQB * 8], F32)
        self.dbg_v = seq("dbg_v", [P, 1], F32)
        self.dbg_s = seq("dbg_s", [P, 1], F32)
        self.sig = seq("sig", [P, NQB], F32)
        self.negm_sv = seq("negm_sv", [P, NQB], F32)
        self.rstd_sv = seq("rstd_sv", [P, NQB], F32)
        A = (cur[0] + 1023) // 1024 * 1024
        self.wdkv_b = lay.at("wdkv_b", [P, NKT * LAT], BF16, A, "p1")
        self.wkr_b = lay.at("wkr_b", [P, NKT * PD], BF16, A + 16 * KB, "p1")
        self.coskb = [lay.at(f"coskb{i}", [PD, 512], F32, A + 18 * KB + 2 * KB * i, "p1") for i in range(2)]
        self.sinkb = [lay.at(f"sinkb{i}", [PD, 512], F32, A + 22 * KB + 2 * KB * i, "p1") for i in range(2)]
        self.ropet1 = lay.at("ropet1", [PD, 512], F32, A + 26 * KB, "p1", "at")
        self.ropet2 = lay.at("ropet2", [PD, 512], F32, A + 28 * KB, "p1", "at")
        self.kr_bf = lay.at("kr_bf", [PD, 512], BF16, A + 30 * KB, "p1")
        self.wdq_b = lay.at("wdq_b", [P, NKT * LAT], BF16, A + 31 * KB, "p1")
        self.xn_bf = [lay.at(f"xn_bf{i}", [P, D], BF16, A + 47 * KB + 4 * KB * i, "p1", "p4") for i in range(2)]
        AEND = A + 55 * KB
        self.ukb = [lay.at(f"ukb{i}", [P, NMT * P], BF16, A + KB * i, "at") for i in range(2)]
        self.uvb = [lay.at(f"uvb{i}", [P, NMT * P], BF16, A + 2 * KB + KB * i, "at") for i in range(2)]
        self.uqb = [lay.at(f"uqb{i}", [P, NMT * P], BF16, A + 4 * KB + KB * i, "at") for i in range(2)]
        self.qrb = [lay.at(f"qrb{i}", [P, NMT * PD], BF16, A + 6 * KB + 512 * i, "at") for i in range(2)]
        self.e_bf = ([lay.at(f"e_bf{i}", [P, 512], BF16, A + 7 * KB + KB * i, "at") for i in range(2)]
                     + [lay.at(f"e_bf{i+2}", [P, 512], BF16, A + 24 * KB + KB * i, "at") for i in range(2)])
        self.o_bf = lay.at("o_bf", [P, P], BF16, A + 9 * KB, "at")
        self.qct = lay.at("qct", [P, TOK], BF16, A + 10 * KB, "at")
        self.qrt = lay.at("qrt", [P, TOK], BF16, A + 12 * KB, "at")
        self.qr_bf = lay.at("qr_bf", [PD, 512], BF16, A + 14 * KB, "at")
        self.cosq = lay.at("cosq", [PD, TOK], F32, A + 16 * KB, "at")
        self.sinq = lay.at("sinq", [PD, TOK], F32, A + 20 * KB, "at")
        self.oacc = lay.at("oacc", [P, 8 * 132], F32, A + 31 * KB, "at")
        self.xwo = [lay.at(f"xwo{i}", [P, TOK], F32, A + 4 * KB * i, "p3") for i in range(4)]
        self.gwb = [lay.at(f"gwb{i}", [P, NKT * P], BF16, A + 4 * KB * i, "p4") for i in range(2)]
        self.uwb = [lay.at(f"uwb{i}", [P, NKT * P], BF16, A + 8 * KB + 4 * KB * i, "p4") for i in range(2)]
        self.dwb = ([lay.at(f"dwb{i}", [P, 2048], BF16, A + 16 * KB + 4 * KB * i, "p4") for i in range(2)]
                    + [lay.at(f"dwb{i+2}", [P, 2048], BF16, A + 32 * KB + 4 * KB * i, "p4") for i in range(2)])
        self.rwsg_b = lay.at("rwsg_b", [P, NKT * 9], BF16, A + 24 * KB, "p4")
        self.probs_all = lay.at("probs_all", [P, NQB * 8], F32, A + 25 * KB, "p4").ap()
        self.mask_all = lay.at("mask_all", [P, NQB * 8], F32, A + 25 * KB + 256, "p4").ap()
        self.keep_all = lay.at("keep_all", [P, NQB * 8], F32, A + 25 * KB + 512, "p4").ap()
        self.m1s = lay.at("m1s", [P, NQB], F32, A + 25 * KB + 768, "p4").ap()
        self.m2s = lay.at("m2s", [P, NQB], F32, A + 25 * KB + 832, "p4").ap()
        self.d8 = lay.at("d8", [P, NQB], F32, A + 25 * KB + 896, "p4").ap()
        self.r8 = lay.at("r8", [P, NQB], F32, A + 25 * KB + 960, "p4").ap()
        self.silu_t = [lay.at(f"silu_t{i}", [P, 512], F32, A + 28 * KB + 2 * KB * i, "p4") for i in range(2)]
        R = AEND
        self.xbuf = lay.at("xbuf", [P, 5 * D], F32, R, "p1")
        self.oT = lay.at("oT", [P, H * TOK], BF16, R, "at", "p3")
        self.xnt = lay.at("xnt", [P, NKT * TOK], BF16, R, "p4")
        self.xnt_scr = lay.at("xnt_scr", [P, NKT * 512], BF16, R + 40 * KB, "p1")
        self.kct = lay.at("kct", [P, S], BF16, R + 32 * KB, "at")
        self.wo_b = lay.at("wo_b", [P, NKT * TOK], BF16, R + 32 * KB, "p3")
        self.shact = lay.at("shact", [P, IMT * TOK], BF16, R + 32 * KB, "p4")
        self.vbuf = lay.at("vbuf", [P, 32 * 130], BF16, R + 48 * KB, "at")
        self.actt = lay.at("actt", [P, IMT * TOK], BF16, R + 54 * KB, "p4")
        CK = R + 57 * KB
        self.ckvt = lay.at("ckvt", [P, NMT * S], BF16, CK, "p1", "at")
        self.krt = lay.at("krt", [P, S], BF16, CK + 64 * KB, "p1", "at")
        self.cqt = lay.at("cqt", [P, NMT * TOK], BF16, CK + 80 * KB, "p1", "at")
        assert CK + 88 * KB <= SBUF_HI
        OA = (SBUF_HI - 64 * KB) // 64 * 64
        self.out_acc = lay.at("out_acc", [P, NQB * D], F32, OA, "p3", "p4")
        assert OA >= R + 76 * KB, "out_acc overlaps p4 actt"
        assert OA >= R + 64 * KB, "out_acc overlaps p3 wo_b"
        self.pb = [nc.alloc_psum_tensor(f"pb{i}", [P, 512], F32) for i in range(8)]
        self.pb_bf = [b.ap().bitcast(BF16) for b in self.pb]

    def emit_all(self):
        self.emit_init()
        self.emit_stage_a()
        self.emit_cq()
        if self.debug < 9:
            self.emit_attention()
        if self.debug < 8:
            self.emit_wo()
            self.emit_ffn_pre()
            self.emit_shared()
            self.emit_moe()
        self.emit_store()

    def emit_init(self):
        io = self.io
        def ms(v):
            v.memset(self.eps[:, :], EPS)
            v.memset(self.ones128f[:, :], 1.0)
            v.memset(self.onesrow[:, :], 1.0)
            v.memset(self.krt.ap()[64:65, :], 1.0).then_inc(self.dv.h, 1)
        self.init_dv = self.dv.inc()
        self.v(ms)
        self.dma("gpsimd", self.swa, self.ident_bf.ap()[:, :], io["ident"][:, :])
        self.rt_ld = self.dma("gpsimd", self.swa, self.rt_bf.ap()[:, :], io["rt"][:, :])
        self.dma("gpsimd", self.swa,
                 self.wdkv_b.ap().rearrange("p (kt n) -> p kt n", n=LAT),
                 io["wdkv"].rearrange("(kt p) n -> p kt n", p=P))
        self.dma("gpsimd", self.swa,
                 self.wkr_b.ap().rearrange("p (kt n) -> p kt n", n=PD),
                 io["wkr"].rearrange("(kt p) n -> p kt n", p=P))
        self.wdq_ld = self.dma("gpsimd", self.swa,
                               self.wdq_b.ap().rearrange("p (kt n) -> p kt n", n=LAT),
                               io["wdq"].rearrange("(kt p) n -> p kt n", p=P))
        self.wa_ld = self.wdq_ld

    def emit_ln_bn(self, xap, slot, xwait=None):
        dv = self.dv
        j = slot % 4
        bn = self.bn_a[:, 24 * j:24 * (j + 1)]
        pre = dv.n
        def st_(v, xap=xap, bn=bn, pre=pre, xw=xwait):
            v.wait_ge(self.dv.h, pre)
            if xw is not None:
                v.wait_ge(xw[0].h, xw[1])
            v.bn_stats(bn[:, 0:6], xap[:, 0:512])
            v.bn_stats(bn[:, 6:12], xap[:, 512:1024])
            v.bn_stats(bn[:, 12:18], xap[:, 1024:1536])
            v.bn_stats(bn[:, 18:24], xap[:, 1536:2048]).then_inc(self.dv.h, 1)
        t = dv.inc()
        self.v(st_)
        return t

    def emit_ln_mv4(self, t_bns, save_st=None):
        dv, ac = self.dv, self.ac
        if save_st is None:
            rstd_a, negm_a = self.rstd_a4[:, 0:4], self.negm_a4[:, 0:4]
        else:
            rstd_a = self.rstd_sv[:, 4 * save_st:4 * save_st + 4]
            negm_a = self.negm_sv[:, 4 * save_st:4 * save_st + 4]
        negm_cols = [negm_a[:, tt:tt + 1] for tt in range(4)]
        pre = dv.n
        def agg(v, w=t_bns, pre=pre):
            v.wait_ge(self.dv.h, max(w, pre))
            for tt in range(3):
                v.bn_aggr(self.mv2_a[:, 2 * tt:2 * tt + 2], self.bn_a[:, 24 * tt:24 * tt + 24])
            v.bn_aggr(self.mv2_a[:, 6:8], self.bn_a[:, 72:96]).then_inc(self.dv.h, 1)
        tagg = dv.inc()
        self.v(agg)
        pre_ac = ac.n
        tac = ac.inc()
        def sq4(s, w=tagg, pre=pre_ac):
            s.wait_ge(self.dv.h, w)
            s.wait_ge(self.ac.h, pre)
            for tt in range(3):
                s.activation(self.sq_a[:, tt:tt + 1], self.mv2_a[:, 2 * tt + 1:2 * tt + 2],
                             AF.Sqrt, bias=self.eps[:, 0:1])
            s.activation(self.sq_a[:, 3:4], self.mv2_a[:, 7:8],
                         AF.Sqrt, bias=self.eps[:, 0:1]).then_inc(self.ac.h, 1)
        self.s(sq4)
        def fin(v, w=tac, rstd_a=rstd_a, negm_cols=tuple(negm_cols)):
            v.wait_ge(self.ac.h, w)
            v.reciprocal(rstd_a, self.sq_a[:, 0:4])
            for tt in range(3):
                v.tensor_scalar(negm_cols[tt], self.mv2_a[:, 2 * tt:2 * tt + 1], -1.0, None, ALU.mult)
            v.tensor_scalar(negm_cols[3], self.mv2_a[:, 6:7], -1.0, None, ALU.mult).then_inc(self.dv.h, 1)
        t_fin = dv.inc()
        self.v(fin)
        return rstd_a, negm_a, t_fin

    def emit_ln_apply(self, xap, xn_out, rstd, negm, t_fin):
        dv = self.dv
        def apply(v, xap=xap, xn_out=xn_out, w=t_fin, rstd=rstd, negm=negm):
            v.wait_ge(self.dv.h, w)
            v.tensor_scalar(xn_out[:, :], xap[:, :], negm, rstd,
                            ALU.add, ALU.mult).then_inc(self.dv.h, 1)
        t2 = dv.inc()
        self.v(apply)
        return t2

    def emit_transpose16(self, xn_ap, dst_cols, wait_dv, evac="vector", banks=(2, 3)):
        pe = self.pe
        for half in range(2):
            ev = evac if evac != "mixed" else ("vector" if half == 0 else "scalar")
            bank = banks[half]
            def dotp(t, xn_ap=xn_ap, half=half, bank=bank, wdv=wait_dv, fb=self.bankfree.get(bank)):
                t.wait_ge(self.dv.h, wdv)
                t.wait_ge(self.swa.h, 16)
                self.wait_bf(t, fb)
                for i in range(7):
                    kt = half * 8 + i
                    t.transpose(self.pb_bf[bank][:, P * i:P * (i + 1)],
                                xn_ap[:, P * kt:P * (kt + 1)], self.ident_bf.ap()[:, :])
                t.transpose(self.pb_bf[bank][:, P * 7:P * 8],
                            xn_ap[:, P * (half * 8 + 7):P * (half * 8 + 8)],
                            self.ident_bf.ap()[:, :]).then_inc(self.pe.h, 1)
            tpe = pe.inc()
            self.t(dotp)
            if ev == "vector":
                def doev(v, half=half, bank=bank, dst_cols=dst_cols, w=tpe):
                    v.wait_ge(self.pe.h, w)
                    for i in range(7):
                        v.tensor_copy(dst_cols[half * 8 + i], self.pb_bf[bank][:, P * i:P * (i + 1)])
                    v.tensor_copy(dst_cols[half * 8 + 7], self.pb_bf[bank][:, P * 7:P * 8]).then_inc(self.dv.h, 1)
                tdv = self.dv.inc()
                self.v(doev)
                self.bankfree[bank] = ("vector", tdv)
            else:
                def doev(s, half=half, bank=bank, dst_cols=dst_cols, w=tpe, pre=self.ac.n):
                    s.wait_ge(self.pe.h, w)
                    s.wait_ge(self.ac.h, pre)
                    for i in range(7):
                        s.copy(dst_cols[half * 8 + i], self.pb_bf[bank][:, P * i:P * (i + 1)])
                    s.copy(dst_cols[half * 8 + 7], self.pb_bf[bank][:, P * 7:P * 8]).then_inc(self.ac.h, 1)
                tac = self.ac.inc()
                self.s(doev)
                self.bankfree[bank] = ("scalar", tac)
        return pe.n, (self.dv.n if evac == "vector" else None)

    def emit_stage_a(self):
        io = self.io
        xsrc = io["x_full"].rearrange("(g p) d -> g p d", p=P)
        self._stage_tokens(xsrc, 2 * NST, True)

    def _stage_tokens(self, xsrc, nhalves, is_full):
        nst = nhalves // 2
        xfree = [self.dv.n] * 5
        stats_p = {}
        mvd = {}
        def emit_ls(st):
            for tt in range(4):
                gi = st * 4 + tt
                slot = gi % 5
                tsx = self.dma("sync", self.sx[slot],
                               self.xbuf.ap().rearrange("p (f d) -> p f d", d=D)[:, slot, :],
                               xsrc[gi], [("vector", xfree[slot])])
                xap = self.xbuf.ap().rearrange("p (f d) -> p f d", d=D)[:, slot, :]
                stats_p[gi] = [tt, xap, tsx, slot]
        def emit_bn(gi):
            tt, xap, tsx, slot = stats_p[gi]
            return self.emit_ln_bn(xap, tt, xwait=(self.sx[slot], tsx))
        def emit_mv4(st, t_bns):
            mvd[st] = self.emit_ln_mv4(t_bns, save_st=(None if is_full else st))
        emit_ls(0)
        for tt in range(3):
            emit_bn(tt)
        emit_mv4(0, emit_bn(3))
        for st in range(nst):
            rstd_a, negm_a, t_fin = mvd.pop(st)
            for tt in range(4):
                gi = st * 4 + tt
                ptt, pxap, _, pslot = stats_p.pop(gi)
                self._flush_ln(ptt, pxap, rstd_a[:, tt:tt + 1], negm_a[:, tt:tt + 1], t_fin)
                xfree[pslot] = self.dv.n
            xnt_ready = ("vector", self.dv.n, "scalar", self.ac.n)
            tcb = None
            if st + 1 < nst:
                emit_ls(st + 1)
                def tile_cb(mt, st=st):
                    if mt < 2:
                        emit_bn((st + 1) * 4 + mt)
                    else:
                        emit_bn((st + 1) * 4 + 2)
                        emit_mv4(st + 1, emit_bn((st + 1) * 4 + 3))
                tcb = tile_cb
            if is_full:
                self._stagea_proj(st, xnt_ready, tile_cb=tcb)
            else:
                self._cq_proj(st, xnt_ready)
                if tcb is not None:
                    for mt in range(3):
                        tcb(mt)

    def _flush_ln(self, tt, xap, rstd, negm, t_fin):
        xnb = self.xn_bf[tt % 2]
        prev_pe = self._xnb_pe[tt % 2]
        if prev_pe is not None:
            self.v(lambda v, w=prev_pe: v.wait_ge(self.pe.h, w))
        wdv = self.emit_ln_apply(xap, xnb.ap(), rstd, negm, t_fin)
        dst = [self.xnt_scr.ap()[:, kt * 512 + tt * P: kt * 512 + tt * P + P] for kt in range(NKT)]
        pec, dvc = self.emit_transpose16(xnb.ap(), dst, wdv, evac="mixed",
                                         banks=((2, 3) if tt % 2 == 0 else (4, 5)))
        self._xnb_pe[tt % 2] = pec

    def _stagea_proj(self, st, xnt_ready, tile_cb=None):
        io, pe, dv = self.io, self.pe, self.dv
        slot = st % 2
        kw = [] if self._krc_dv[slot] is None else [("vector", self._krc_dv[slot])]
        t_c = self.dma("gpsimd", self.sckc[slot], self.coskb[slot].ap()[:, :],
                       io["cosk"][:, 512 * st:512 * (st + 1)], kw)
        t_s = self.dma("gpsimd", self.scks[slot], self.sinkb[slot].ap()[:, :],
                       io["sink"][:, 512 * st:512 * (st + 1)], kw)
        def krmm(t, st=st, w=xnt_ready, wld=self.wa_ld, fb=self.bankfree.get(4)):
            t.wait_ge(self.dv.h, w[1])
            t.wait_ge(self.ac.h, w[3])
            self.wait_bf(t, fb)
            t.wait_ge(self.swa.h, wld)
            for kt in range(NKT - 1):
                t.matmul(self.pb[4][0:PD, :],
                         self.wkr_b.ap()[:, kt * PD:(kt + 1) * PD],
                         self.xnt_scr.ap()[:, kt * 512:(kt + 1) * 512],
                         start=(kt == 0), stop=False)
            kt = NKT - 1
            t.matmul(self.pb[4][0:PD, :],
                     self.wkr_b.ap()[:, kt * PD:(kt + 1) * PD],
                     self.xnt_scr.ap()[:, kt * 512:(kt + 1) * 512],
                     start=False, stop=True).then_inc(self.pe.h, 1)
        tpe_kr = pe.inc()
        self.t(krmm)
        def krev(v, w=tpe_kr):
            v.wait_ge(self.pe.h, w)
            v.tensor_copy(self.kr_bf.ap()[:, :], self.pb[4][0:PD, :]).then_inc(self.dv.h, 1)
        tdv_krbf = dv.inc()
        self.v(krev)
        def krrot(t, w=tdv_krbf, wrt=self.rt_ld, fb=self.bankfree.get(5)):
            t.wait_ge(self.dv.h, w)
            self.wait_bf(t, fb)
            t.wait_ge(self.swa.h, wrt)
            t.matmul(self.pb[5][0:PD, :], self.rt_bf.ap()[:, :], self.kr_bf.ap()[:, :],
                     start=True, stop=True).then_inc(self.pe.h, 1)
        tpe_rot = pe.inc()
        self.t(krrot)
        pre_kr = dv.n
        def krcomb(v, st=st, slot=slot, w=tpe_rot, wc=t_c, ws=t_s, pre=pre_kr):
            v.wait_ge(self.pe.h, w)
            v.wait_ge(self.sckc[slot].h, wc)
            v.wait_ge(self.scks[slot].h, ws)
            v.wait_ge(self.dv.h, pre)
            v.tensor_tensor(self.ropet1.ap()[:, :], self.pb[4][0:PD, :], self.coskb[slot].ap()[:, :], ALU.mult)
            v.tensor_tensor(self.ropet2.ap()[:, :], self.pb[5][0:PD, :], self.sinkb[slot].ap()[:, :], ALU.mult).then_inc(self.dv.h, 1)
            v.wait_ge(self.dv.h, pre + 1)
            v.tensor_tensor(self.krt.ap()[0:PD, 512 * st:512 * (st + 1)],
                            self.ropet1.ap()[:, :], self.ropet2.ap()[:, :], ALU.add).then_inc(self.dv.h, 1)
        dv.inc()
        tdv_kr = dv.inc()
        self.v(krcomb)
        self._krc_dv[slot] = tdv_kr
        self.bankfree[4] = ("vector", tdv_kr)
        self.bankfree[5] = ("vector", tdv_kr)
        for mt in range(NMT):
            bank = mt % 2
            def ckv(t, st=st, mt=mt, bank=bank, w=xnt_ready, wld=self.wa_ld, fb=self.bankfree.get(bank)):
                t.wait_ge(self.dv.h, w[1])
                t.wait_ge(self.ac.h, w[3])
                self.wait_bf(t, fb)
                t.wait_ge(self.swa.h, wld)
                for kt in range(NKT - 1):
                    t.matmul(self.pb[bank][:, :],
                             self.wdkv_b.ap()[:, kt * LAT + mt * P: kt * LAT + (mt + 1) * P],
                             self.xnt_scr.ap()[:, kt * 512:(kt + 1) * 512],
                             start=(kt == 0), stop=False)
                kt = NKT - 1
                t.matmul(self.pb[bank][:, :],
                         self.wdkv_b.ap()[:, kt * LAT + mt * P: kt * LAT + (mt + 1) * P],
                         self.xnt_scr.ap()[:, kt * 512:(kt + 1) * 512],
                         start=False, stop=True).then_inc(self.pe.h, 1)
            tpe = pe.inc()
            self.t(ckv)
            def ckve(v, st=st, mt=mt, bank=bank, w=tpe):
                v.wait_ge(self.pe.h, w)
                v.tensor_copy(self.ckvt.ap()[:, mt * S + st * 512: mt * S + st * 512 + 512],
                              self.pb[bank][:, :]).then_inc(self.dv.h, 1)
            tdv = dv.inc()
            self.v(ckve)
            self.bankfree[bank] = ("vector", tdv)
            if tile_cb is not None and mt < 3:
                tile_cb(mt)

    def _cq_proj(self, j, xnt_ready):
        pe, dv = self.pe, self.dv
        for mt in range(NMT):
            bank = mt % 2
            def cq(t, j=j, mt=mt, bank=bank, w=xnt_ready, wld=self.wdq_ld, fb=self.bankfree.get(bank)):
                t.wait_ge(self.dv.h, w[1])
                t.wait_ge(self.ac.h, w[3])
                self.wait_bf(t, fb)
                t.wait_ge(self.swa.h, wld)
                for kt in range(NKT - 1):
                    t.matmul(self.pb[bank][:, :],
                             self.wdq_b.ap()[:, kt * LAT + mt * P: kt * LAT + (mt + 1) * P],
                             self.xnt_scr.ap()[:, kt * 512:(kt + 1) * 512],
                             start=(kt == 0), stop=False)
                kt = NKT - 1
                t.matmul(self.pb[bank][:, :],
                         self.wdq_b.ap()[:, kt * LAT + mt * P: kt * LAT + (mt + 1) * P],
                         self.xnt_scr.ap()[:, kt * 512:(kt + 1) * 512],
                         start=False, stop=True).then_inc(self.pe.h, 1)
            tpe = pe.inc()
            self.t(cq)
            def cqe(v, j=j, mt=mt, bank=bank, w=tpe):
                v.wait_ge(self.pe.h, w)
                v.tensor_copy(self.cqt.ap()[:, mt * TOK + j * 512: mt * TOK + j * 512 + 512],
                              self.pb[bank][:, :]).then_inc(self.dv.h, 1)
            tdv = dv.inc()
            self.v(cqe)
            self.bankfree[bank] = ("vector", tdv)

    def emit_cq(self):
        io = self.io
        xsrc = io["x_own"].rearrange("(g p) d -> g p d", p=P)
        self._stage_tokens(xsrc, 4, False)
        wkr = self.dv.n
        def krb(v, w=wkr):
            v.wait_ge(self.dv.h, w)
            v.reduce_max(self.kabs[0:PD, :], self.krt.ap()[0:PD, :], AX.X,
                         apply_absolute_value=True).then_inc(self.dv.h, 1)
        t1 = self.dv.inc()
        self.v(krb)
        t2 = self.ac.inc()
        self.s(lambda s, w=t1: (s.wait_ge(self.dv.h, w),
                                s.activation(self.kabs[0:PD, :], self.kabs[0:PD, :], AF.Square).then_inc(self.ac.h, 1)))
        def krmm(t, w=t2, fb=self.bankfree.get(0)):
            t.wait_ge(self.ac.h, w)
            self.wait_bf(t, fb)
            t.matmul(self.pb[0][0:1, 0:1], self.ones128f[0:PD, :], self.kabs[0:PD, :],
                     start=True, stop=True).then_inc(self.pe.h, 1)
        t3 = self.pe.inc()
        self.t(krmm)
        def kre(v, w=t3):
            v.wait_ge(self.pe.h, w)
            v.tensor_copy(self.kr2s[0:1, :], self.pb[0][0:1, 0:1]).then_inc(self.dv.h, 1)
        self.kr2s_dv = self.dv.inc()
        self.v(kre)
        self.bankfree[0] = ("vector", self.kr2s_dv)
        self.emit_dbg([("d_ckvt", self.ckvt.ap()[:, :]),
                       ("d_krt", self.krt.ap()[0:PD + 1, :]),
                       ("d_cqt", self.cqt.ap()[:, :])],
                      [("vector", self.dv.n)])

    def emit_attention(self):
        io = self.io
        p1_pe, p1_dv = self.pe.n, self.dv.n
        pre_v = self.dv.n
        def msv(v, wp=p1_pe, pre=pre_v):
            v.wait_ge(self.pe.h, wp)
            v.wait_ge(self.dv.h, pre)
            v.memset(self.vbuf.ap()[:, :], 0.0).then_inc(self.dv.h, 1)
            v.wait_ge(self.dv.h, pre + 1)
            ones_cols = self.vbuf.ap().rearrange("p (k c) -> p k c", c=130)[:, :, 128:129]
            v.memset(ones_cols, 1.0).then_inc(self.dv.h, 1)
        self.dv.inc(2)
        self.v(msv)
        self.cosq_ld = self.dma("gpsimd", self.sckc[0], self.cosq.ap()[:, :], io["cosq"][:, :],
                                [("tensor", p1_pe), ("vector", p1_dv)])
        self.sinq_ld = self.dma("gpsimd", self.scks[0], self.sinq.ap()[:, :], io["sinq"][:, :],
                                [("tensor", p1_pe), ("vector", p1_dv)])
        uk_src = io["wuk"].rearrange("(mt l) n -> l mt n", l=P)
        uv_src = io["wuv"].rearrange("(mt l) n -> l mt n", l=P)
        uq_src = io["wuq"].rearrange("(mt l) n -> l mt n", l=P)
        qr_src = io["wqr"].rearrange("(mt l) n -> l mt n", l=P)
        hwfree = [None, None]
        for h in range(H):
            par = h % 2
            if h == 0:
                waits = [("tensor", p1_pe), ("vector", p1_dv)]
            elif hwfree[par] is None:
                waits = []
            else:
                waits = [("tensor", hwfree[par])]
            wld = {}
            wld["uk"] = self.dma("gpsimd", self.suk[par],
                                 self.ukb[par].ap().rearrange("p (mt n) -> p mt n", n=P),
                                 uk_src[:, :, h * P:(h + 1) * P], waits)
            wld["uv"] = self.dma("gpsimd", self.suv[par],
                                 self.uvb[par].ap().rearrange("p (mt n) -> p mt n", n=P),
                                 uv_src[:, :, h * P:(h + 1) * P], waits)
            wld["uq"] = self.dma("gpsimd", self.suq[par],
                                 self.uqb[par].ap().rearrange("p (mt n) -> p mt n", n=P),
                                 uq_src[:, :, h * P:(h + 1) * P], waits)
            wld["qr"] = self.dma("gpsimd", self.sqr[par],
                                 self.qrb[par].ap().rearrange("p (mt n) -> p mt n", n=PD),
                                 qr_src[:, :, h * PD:(h + 1) * PD], waits)
            self.emit_head(h, wld, par)
            hwfree[par] = self._head_w_done
        if self._pend_epilogue is not None:
            self._pend_epilogue()
            self._pend_epilogue = None
        self.emit_dbg([("d_ot", self.oT.ap()[:, :])], [("vector", self.dv.n)])

    def emit_head(self, h, wld, par):
        pe, dv, ac = self.pe, self.dv, self.ac
        dbg_me = self.debug and h == self.DBG_H
        O_BANKS = (2, 3, 6, 7)
        for ch in range(16):
            bank = (0, 1, 4, 5)[ch % 4]
            def kup(t, h=h, ch=ch, bank=bank, par=par, w=wld["uk"], fb=self.bankfree.get(bank)):
                t.wait_ge(self.suk[par].h, w)
                self.wait_bf(t, fb)
                for mt in range(NMT - 1):
                    t.matmul(self.pb[bank][:, :],
                             self.ukb[par].ap()[:, mt * P:(mt + 1) * P],
                             self.ckvt.ap()[:, mt * S + ch * 512: mt * S + ch * 512 + 512],
                             start=(mt == 0), stop=False)
                mt = NMT - 1
                t.matmul(self.pb[bank][:, :],
                         self.ukb[par].ap()[:, mt * P:(mt + 1) * P],
                         self.ckvt.ap()[:, mt * S + ch * 512: mt * S + ch * 512 + 512],
                         start=False, stop=True).then_inc(self.pe.h, 1)
            tpe = pe.inc()
            self.t(kup)
            if ch % 2 == 0:
                def kupe(v, ch=ch, bank=bank, w=tpe):
                    v.wait_ge(self.pe.h, w)
                    v.tensor_copy(self.kct.ap()[:, ch * 512:(ch + 1) * 512],
                                  self.pb[bank][:, :]).then_inc(self.dv.h, 1)
                tdv = dv.inc()
                self.v(kupe)
                self.bankfree[bank] = ("vector", tdv)
            else:
                def kupe_s(s, ch=ch, bank=bank, w=tpe, pre=self.ac.n):
                    s.wait_ge(self.pe.h, w)
                    s.wait_ge(self.ac.h, pre)
                    s.copy(self.kct.ap()[:, ch * 512:(ch + 1) * 512],
                           self.pb[bank][:, :]).then_inc(self.ac.h, 1)
                tac = ac.inc()
                self.s(kupe_s)
                self.bankfree[bank] = ("scalar", tac)
        kct_dv, kct_ac = self.dv.n, self.ac.n
        if self._pend_epilogue is not None:
            self._pend_epilogue()
            self._pend_epilogue = None
        for ch in range(2):
            bank = ch % 2
            def qc(t, ch=ch, bank=bank, par=par, w=wld["uq"], fb=self.bankfree.get(bank)):
                t.wait_ge(self.suq[par].h, w)
                self.wait_bf(t, fb)
                for mt in range(NMT - 1):
                    t.matmul(self.pb[bank][:, :],
                             self.uqb[par].ap()[:, mt * P:(mt + 1) * P],
                             self.cqt.ap()[:, mt * TOK + ch * 512: mt * TOK + ch * 512 + 512],
                             start=(mt == 0), stop=False)
                mt = NMT - 1
                t.matmul(self.pb[bank][:, :],
                         self.uqb[par].ap()[:, mt * P:(mt + 1) * P],
                         self.cqt.ap()[:, mt * TOK + ch * 512: mt * TOK + ch * 512 + 512],
                         start=False, stop=True).then_inc(self.pe.h, 1)
            tpe = pe.inc()
            self.t(qc)
            def qce(v, ch=ch, bank=bank, w=tpe):
                v.wait_ge(self.pe.h, w)
                v.tensor_copy(self.qct.ap()[:, ch * 512:(ch + 1) * 512], self.pb[bank][:, :]).then_inc(self.dv.h, 1)
            tdv = dv.inc()
            self.v(qce)
            self.bankfree[bank] = ("vector", tdv)
        def qr_stage_a(ch):
            def qr(t, ch=ch, par=par, w=wld["qr"], fb=self.bankfree.get(4)):
                t.wait_ge(self.sqr[par].h, w)
                self.wait_bf(t, fb)
                for mt in range(NMT - 1):
                    t.matmul(self.pb[4][0:PD, :],
                             self.qrb[par].ap()[:, mt * PD:(mt + 1) * PD],
                             self.cqt.ap()[:, mt * TOK + ch * 512: mt * TOK + ch * 512 + 512],
                             start=(mt == 0), stop=False)
                mt = NMT - 1
                t.matmul(self.pb[4][0:PD, :],
                         self.qrb[par].ap()[:, mt * PD:(mt + 1) * PD],
                         self.cqt.ap()[:, mt * TOK + ch * 512: mt * TOK + ch * 512 + 512],
                         start=False, stop=True).then_inc(self.pe.h, 1)
            tpe_qr = pe.inc()
            self.t(qr)
            def qrev(v, w=tpe_qr):
                v.wait_ge(self.pe.h, w)
                v.tensor_copy(self.qr_bf.ap()[:, :], self.pb[4][0:PD, :]).then_inc(self.dv.h, 1)
            tdv_qrbf = dv.inc()
            self.v(qrev)
            return tdv_qrbf
        def qr_stage_b(ch, tdv_qrbf):
            def qrrot(t, w=tdv_qrbf, fb=self.bankfree.get(5)):
                t.wait_ge(self.dv.h, w)
                self.wait_bf(t, fb)
                t.matmul(self.pb[5][0:PD, :], self.rt_bf.ap()[:, :], self.qr_bf.ap()[:, :],
                         start=True, stop=True).then_inc(self.pe.h, 1)
            tpe_rot = pe.inc()
            self.t(qrrot)
            pre_qr = dv.n
            def qrcomb(v, ch=ch, w=tpe_rot, wc=self.cosq_ld, ws=self.sinq_ld, pre=pre_qr):
                v.wait_ge(self.pe.h, w)
                v.wait_ge(self.sckc[0].h, wc)
                v.wait_ge(self.scks[0].h, ws)
                v.wait_ge(self.dv.h, pre)
                v.tensor_tensor(self.ropet1.ap()[:, :], self.pb[4][0:PD, :],
                                self.cosq.ap()[:, ch * 512:(ch + 1) * 512], ALU.mult)
                v.tensor_tensor(self.ropet2.ap()[:, :], self.pb[5][0:PD, :],
                                self.sinq.ap()[:, ch * 512:(ch + 1) * 512], ALU.mult).then_inc(self.dv.h, 1)
                v.wait_ge(self.dv.h, pre + 1)
                v.tensor_tensor(self.qrt.ap()[0:PD, 512 * ch:512 * (ch + 1)],
                                self.ropet1.ap()[:, :], self.ropet2.ap()[:, :], ALU.add).then_inc(self.dv.h, 1)
            dv.inc()
            tdv_qr = dv.inc()
            self.v(qrcomb)
            self.bankfree[4] = ("vector", tdv_qr)
            self.bankfree[5] = ("vector", tdv_qr)
        def emit_qb():
            qdone = self.dv.n
            def qb_v(v, w=qdone):
                v.wait_ge(self.dv.h, w)
                v.reduce_max(self.qabs[:, :], self.qct.ap()[:, :], AX.X, apply_absolute_value=True)
                v.reduce_max(self.qabs2[0:PD, :], self.qrt.ap()[0:PD, :], AX.X,
                             apply_absolute_value=True).then_inc(self.dv.h, 1)
            t1q = dv.inc()
            self.v(qb_v)
            t2q = ac.inc()
            def qb_s(s, w=t1q):
                s.wait_ge(self.dv.h, w)
                s.activation(self.qabs[:, :], self.qabs[:, :], AF.Square)
                s.activation(self.qabs2[0:PD, :], self.qabs2[0:PD, :], AF.Square).then_inc(self.ac.h, 1)
            self.s(qb_s)
            return t2q
        def kb_chunk(j):
            def kb(v, j=j, wd=kct_dv, wa=kct_ac):
                v.wait_ge(self.dv.h, wd)
                v.wait_ge(self.ac.h, wa)
                v.reduce_max(self.mean_a[:, j:j + 1], self.kct.ap()[:, j * 2048:(j + 1) * 2048],
                             AX.X, apply_absolute_value=True).then_inc(self.dv.h, 1)
            dv.inc()
            self.v(kb)
        def kb_fin():
            pre = dv.n
            def kf(v, pre=pre):
                v.wait_ge(self.dv.h, pre)
                v.reduce_max(self.kabs[:, :], self.mean_a[:, 0:4], AX.X,
                             apply_absolute_value=True).then_inc(self.dv.h, 1)
            t1k = dv.inc()
            self.v(kf)
            t2k = ac.inc()
            self.s(lambda s, w=t1k: (s.wait_ge(self.dv.h, w),
                                     s.activation(self.kabs[:, :], self.kabs[:, :], AF.Square).then_inc(self.ac.h, 1)))
            return t2k

        def emit_bound_mms(t2k, t2q):
            def kb_mm(t, w=t2k, fb=self.bankfree.get(0)):
                t.wait_ge(self.ac.h, w)
                self.wait_bf(t, fb)
                t.matmul(self.pb[0][0:1, 0:1], self.ones128f[:, :], self.kabs[:, :],
                         start=True, stop=True).then_inc(self.pe.h, 1)
            t3 = pe.inc()
            self.t(kb_mm)
            pre_kb = dv.n
            def kb_e(v, w=t3, wkr=self.kr2s_dv, pre=pre_kb):
                v.wait_ge(self.pe.h, w)
                v.wait_ge(self.dv.h, max(wkr, pre))
                v.tensor_copy(self.bsum[0:1, 0:1], self.pb[0][0:1, 0:1]).then_inc(self.dv.h, 1)
                v.wait_ge(self.dv.h, pre + 1)
                v.tensor_tensor(self.bsum[0:1, 0:1], self.bsum[0:1, 0:1], self.kr2s[0:1, 0:1],
                                ALU.add).then_inc(self.dv.h, 1)
            dv.inc()
            kb_dv = dv.inc()
            self.v(kb_e)
            self.bankfree[0] = ("vector", kb_dv)
            def qb_mm(t, w=t2q, fb=self.bankfree.get(1)):
                t.wait_ge(self.ac.h, w)
                self.wait_bf(t, fb)
                t.matmul(self.pb[1][0:1, 1:2], self.ones128f[:, :], self.qabs[:, :], start=True, stop=False)
                t.matmul(self.pb[1][0:1, 1:2], self.ones128f[0:PD, :], self.qabs2[0:PD, :],
                         start=False, stop=True).then_inc(self.pe.h, 1)
            t3q = pe.inc()
            self.t(qb_mm)
            pre_qb = dv.n
            def qb_e(v, w=t3q, pre=pre_qb):
                v.wait_ge(self.pe.h, w)
                v.wait_ge(self.dv.h, pre)
                v.tensor_copy(self.bsum[0:1, 1:2], self.pb[1][0:1, 1:2]).then_inc(self.dv.h, 1)
                v.wait_ge(self.dv.h, pre + 1)
                v.tensor_mul(self.negb[0:1, :], self.bsum[0:1, 0:1], self.bsum[0:1, 1:2]).then_inc(self.dv.h, 1)
            dv.inc()
            t5 = dv.inc()
            self.v(qb_e)
            self.bankfree[1] = ("vector", t5)
            t6 = ac.inc()
            self.s(lambda s, w=t5: (s.wait_ge(self.dv.h, w),
                                    s.activation(self.negb[0:1, :], self.negb[0:1, :], AF.Sqrt).then_inc(self.ac.h, 1)))
            pre_r64 = dv.n
            def row64(v, w=t6, pre=pre_r64):
                v.wait_ge(self.ac.h, w)
                v.wait_ge(self.dv.h, pre)
                v.tensor_scalar(self.negb[0:1, :], self.negb[0:1, :], -1.0, None, ALU.mult).then_inc(self.dv.h, 1)
                v.wait_ge(self.dv.h, pre + 1)
                v.tensor_scalar(self.qrt.ap()[PD:PD + 1, :], self.onesrow.ap()[0:1, :],
                                self.negb[0:1, 0:1], None, ALU.mult).then_inc(self.dv.h, 1)
            dv.inc()
            b_dv = dv.inc()
            self.v(row64)
            return b_dv
        bound_dv = None
        t2k = t2q = None
        t_qrev = [None, None]
        for vh in range(2):
            for bg in range(8):
                bank = O_BANKS[bg % 4]
                ktl0 = bg * 4
                def vupg(t, vh=vh, bank=bank, ktl0=ktl0, par=par, w=wld["uv"],
                         fb=self.bankfree.get(bank)):
                    t.wait_ge(self.suv[par].h, w)
                    self.wait_bf(t, fb)
                    for kti in range(4):
                        kt = vh * 32 + ktl0 + kti
                        for mt in range(NMT):
                            ins = t.matmul(self.pb[bank][:, kti * P:(kti + 1) * P],
                                           self.ckvt.ap()[:, mt * S + kt * P: mt * S + (kt + 1) * P],
                                           self.uvb[par].ap()[:, mt * P:(mt + 1) * P],
                                           start=(kti == 0 and mt == 0),
                                           stop=(kti == 3 and mt == NMT - 1))
                    ins.then_inc(self.pe.h, 1)
                tpe = pe.inc()
                self.t(vupg)
                dstv = self.vbuf.ap().rearrange("p (k c) -> p k c", c=130)[:, ktl0:ktl0 + 4, 0:P]
                srcv = self.pb[bank].ap().rearrange("p (k c) -> p k c", c=P)[:, 0:4, :]
                if bg % 2 == 0:
                    def vupe(v, dstv=dstv, srcv=srcv, w=tpe):
                        v.wait_ge(self.pe.h, w)
                        v.tensor_copy(dstv, srcv).then_inc(self.dv.h, 1)
                    tdv = dv.inc()
                    self.v(vupe)
                    self.bankfree[bank] = ("vector", tdv)
                else:
                    def vupe_s(s, dstv=dstv, srcv=srcv, w=tpe, pre=self.ac.n):
                        s.wait_ge(self.pe.h, w)
                        s.wait_ge(self.ac.h, pre)
                        s.copy(dstv, srcv).then_inc(self.ac.h, 1)
                    tac = ac.inc()
                    self.s(vupe_s)
                    self.bankfree[bank] = ("scalar", tac)
                if vh == 0:
                    if bg == 0:
                        t_qrev[0] = qr_stage_a(0)
                    elif bg == 1:
                        qr_stage_b(0, t_qrev[0])
                        t_qrev[1] = qr_stage_a(1)
                    elif bg == 2:
                        qr_stage_b(1, t_qrev[1])
                        t2q = emit_qb()
                        kb_chunk(0)
                    elif bg == 3:
                        kb_chunk(1)
                        kb_chunk(2)
                    elif bg == 4:
                        kb_chunk(3)
                        t2k = kb_fin()
            if vh == 0:
                bound_dv = emit_bound_mms(t2k, t2q)
                if dbg_me:
                    self.emit_dbg([("d_kct", self.kct.ap()[:, :]),
                                   ("d_qct", self.qct.ap()[:, :]),
                                   ("d_qrt", self.qrt.ap()[0:PD + 1, :]),
                                   ("d_bnd", self.bsum[0:1, 0:2])],
                                  [("vector", bound_dv)])
            vdone = self.dv.n
            vdone_ac = self.ac.n
            if dbg_me and vh == 1:
                self.emit_dbg([("d_vt", self.vbuf.ap()[:, :])], [("vector", vdone), ("scalar", vdone_ac)])
            for qc in range(NQC):
                SC_BANKS = (0, 1, 4, 5)
                tac_list = [None] * 32
                def emit_sc(ktl, qc=qc, vh=vh, vdone=vdone, vdone_ac=vdone_ac):
                    kt = vh * 32 + ktl
                    bank = SC_BANKS[ktl % 4]
                    def sc(t, qc=qc, kt=kt, ktl=ktl, bank=bank, wb=bound_dv, wv=vdone, wva=vdone_ac,
                           fb=self.bankfree.get(bank)):
                        if ktl == 0 and qc == 0:
                            t.wait_ge(self.dv.h, max(wb, wv))
                            t.wait_ge(self.ac.h, wva)
                        self.wait_bf(t, fb)
                        t.matmul(self.pb[bank][:, :],
                                 self.kct.ap()[:, kt * P:(kt + 1) * P],
                                 self.qct.ap()[:, qc * 512:(qc + 1) * 512],
                                 start=True, stop=False)
                        t.matmul(self.pb[bank][:, :],
                                 self.krt.ap()[0:PD + 1, kt * P:(kt + 1) * P],
                                 self.qrt.ap()[0:PD + 1, qc * 512:(qc + 1) * 512],
                                 start=False, stop=True).then_inc(self.pe.h, 1)
                    tpe_sc = self.pe.inc()
                    self.t(sc)
                    def ex(s, bank=bank, w=tpe_sc):
                        s.wait_ge(self.pe.h, w)
                        s.activation(self.e_bf[bank if bank < 2 else bank - 2].ap()[:, :],
                                     self.pb[bank][:, :], AF.Exp,
                                     scale=ATT_SCALE).then_inc(self.ac.h, 1)
                    tac = self.ac.inc()
                    self.s(ex)
                    tac_list[ktl] = tac
                    self.bankfree[bank] = ("scalar", tac)
                emit_sc(0)
                emit_sc(1)
                emit_sc(2)
                for ktl in range(32):
                    if ktl + 3 < 32:
                        emit_sc(ktl + 3)
                    sb = SC_BANKS[ktl % 4]
                    ebuf = self.e_bf[sb if sb < 2 else sb - 2]
                    def om(t, ktl=ktl, ebuf=ebuf, w=tac_list[ktl],
                           fbs=tuple(self.bankfree.get(b) for b in O_BANKS) if ktl == 0 else ()):
                        t.wait_ge(self.ac.h, w)
                        for f in fbs:
                            self.wait_bf(t, f)
                        for j in range(4):
                            ins = t.matmul(self.pb[O_BANKS[j]][:, 0:129],
                                           ebuf.ap()[:, j * P:(j + 1) * P],
                                           self.vbuf.ap()[:, ktl * 130: ktl * 130 + 129],
                                           start=(ktl == 0), stop=(ktl == 31))
                        ins.then_inc(self.pe.h, 1)
                    tpe_o = self.pe.inc()
                    self.t(om)
                pre_oe = dv.n
                def oev(v, qc=qc, vh=vh, w=tpe_o, pre=pre_oe):
                    v.wait_ge(self.pe.h, w)
                    v.wait_ge(self.dv.h, pre)
                    for j in range(4):
                        dst = self.oacc.ap()[:, (qc * 4 + j) * 132: (qc * 4 + j) * 132 + 129]
                        if vh == 0:
                            ins = v.tensor_copy(dst, self.pb[O_BANKS[j]][:, 0:129])
                        else:
                            ins = v.tensor_tensor(dst, dst, self.pb[O_BANKS[j]][:, 0:129], ALU.add)
                    ins.then_inc(self.dv.h, 1)
                t_oe = dv.inc()
                self.v(oev)
                for b in O_BANKS:
                    self.bankfree[b] = ("vector", t_oe)
        oacc_dv = self.dv.n
        def emit_epilogue(h=h, oacc_dv=oacc_dv):
            dv, pe = self.dv, self.pe
            for qb in range(NQB):
                off = qb * 132
                pre_ep = dv.n
                prev_tp = getattr(self, "_prev_obf_pe", None)
                def ep12(v, off=off, w=oacc_dv, pre=pre_ep, ptp=prev_tp):
                    if ptp is not None:
                        v.wait_ge(self.pe.h, ptp)
                    v.wait_ge(self.dv.h, max(w, pre))
                    v.reciprocal(self.rd[:, :], self.oacc.ap()[:, off + 128:off + 129]).then_inc(self.dv.h, 1)
                    v.wait_ge(self.dv.h, pre + 1)
                    v.tensor_scalar(self.o_bf.ap()[:, :], self.oacc.ap()[:, off:off + P],
                                    self.rd[:, 0:1], None, ALU.mult).then_inc(self.dv.h, 1)
                dv.inc()
                t2 = dv.inc()
                self.v(ep12)
                def ep3(t, w=t2, fb=self.bankfree.get(1)):
                    t.wait_ge(self.dv.h, w)
                    self.wait_bf(t, fb)
                    t.transpose(self.pb_bf[1][:, 0:P], self.o_bf.ap()[:, :],
                                self.ident_bf.ap()[:, :]).then_inc(self.pe.h, 1)
                t3 = pe.inc()
                self._prev_obf_pe = t3
                self.t(ep3)
                def ep4(v, h=h, qb=qb, w=t3):
                    v.wait_ge(self.pe.h, w)
                    v.tensor_copy(self.oT.ap()[:, h * TOK + qb * P: h * TOK + (qb + 1) * P],
                                  self.pb_bf[1][:, 0:P]).then_inc(self.dv.h, 1)
                t4 = dv.inc()
                self.v(ep4)
                self.bankfree[1] = ("vector", t4)
        self._pend_epilogue = emit_epilogue
        self._head_w_done = self.pe.n

    def emit_wo(self):
        io, pe, dv = self.io, self.pe, self.dv
        attn_pe = self.pe.n
        attn_dv = self.dv.n
        wo_src = io["wo"].rearrange("(kt p) n -> p kt n", p=P)
        xsrc = io["x_own"].rearrange("(qb p) d -> qb p d", p=P)
        half_pe = [None, None]
        prev_hf_dv = attn_dv
        wo_sems = [self.swo, self.sckc[0]]
        for hf in range(2):
            wwaits = [("tensor", attn_pe)] if hf == 0 else [("tensor", half_pe[0])]
            wo_lds = []
            for wc in range(8):
                wo_lds.append(self.dma(
                    "gpsimd", wo_sems[wc % 2],
                    self.wo_b.ap().rearrange("p (kt n) -> p kt n", n=TOK)[:, 2 * wc:2 * (wc + 1), :],
                    wo_src[:, 2 * wc:2 * (wc + 1), hf * TOK:(hf + 1) * TOK],
                    wwaits if wc < 2 else []))
            xld = [None] * NQB
            for qb in range(NQB):
                xwaits = [("vector", prev_hf_dv)] if qb < 2 else [("vector", xld[qb - 2])]
                tsx = self.dma("sync", self.sx[qb % 2], self.xwo[qb % 2].ap()[:, :],
                               xsrc[qb][:, hf * TOK:(hf + 1) * TOK], xwaits)
                for ch in range(2):
                    bank = (qb * 2 + ch) % 4
                    def wo(t, hf=hf, qb=qb, ch=ch, bank=bank, wlds=tuple(wo_lds), fb=self.bankfree.get(bank)):
                        self.wait_bf(t, fb)
                        for kt in range(NKT):
                            if kt % 2 == 0:
                                t.wait_ge(wo_sems[(kt // 2) % 2].h, wlds[kt // 2])
                            ins = t.matmul(self.pb[bank][:, :],
                                           self.oT.ap()[:, kt * TOK + qb * P: kt * TOK + (qb + 1) * P],
                                           self.wo_b.ap()[:, kt * TOK + ch * 512: kt * TOK + ch * 512 + 512],
                                           start=(kt == 0), stop=(kt == NKT - 1))
                        ins.then_inc(self.pe.h, 1)
                    tpe = pe.inc()
                    self.t(wo)
                    def woe(v, hf=hf, qb=qb, ch=ch, bank=bank, w=tpe, wx=tsx):
                        v.wait_ge(self.pe.h, w)
                        v.wait_ge(self.sx[qb % 2].h, wx)
                        off = qb * D + hf * TOK + ch * 512
                        v.tensor_tensor(self.out_acc.ap()[:, off:off + 512], self.pb[bank][:, :],
                                        self.xwo[qb % 2].ap()[:, ch * 512:(ch + 1) * 512],
                                        ALU.add).then_inc(self.dv.h, 1)
                    tdv = dv.inc()
                    self.v(woe)
                    self.bankfree[bank] = ("vector", tdv)
                xld[qb] = dv.n
            half_pe[hf] = self.pe.n
            prev_hf_dv = self.dv.n
        self.p3_dv = self.dv.n
        self.emit_dbg([("d_accp3", self.out_acc.ap()[:, :])], [("vector", self.dv.n)])

    def emit_ffn_pre(self):
        io, pe, dv, ac = self.io, self.pe, self.dv, self.ac
        rw_ld = self.dma("gpsimd", self.swa, self.rwsg_b.ap()[:, :], io["rwsg"][:, :],
                         [("vector", self.p3_dv)])
        xsrc = io["x_own"].rearrange("(qb p) d -> qb p d", p=P)
        xld = [None] * NQB
        base_dv = self.p3_dv
        tpe_r = [None] * NQB
        for qb in range(NQB):
            p_ = qb % 2
            b0, b1 = self.xwo[2 * p_], self.xwo[2 * p_ + 1]
            if qb < 2:
                xwaits = [("vector", self.p3_dv)]
            else:
                xwaits = [("vector", xld[qb - 2])]
            tx0 = self.dma("sync", self.sx[0], b0.ap()[:, :], xsrc[qb][:, 0:TOK], xwaits)
            tx1 = self.dma("sync", self.sx[1], b1.ap()[:, :], xsrc[qb][:, TOK:D], xwaits)
            xnb = self.xn_bf[p_]
            prev_pe = self._xnb_pe[p_]
            negm = self.negm_sv[:, qb:qb + 1]
            rstd = self.rstd_sv[:, qb:qb + 1]
            def apply(v, xnb=xnb, a=tx0, b=tx1, b0=b0, b1=b1, negm=negm, rstd=rstd,
                      w=base_dv, wp=prev_pe):
                v.wait_ge(self.sx[0].h, a)
                v.wait_ge(self.sx[1].h, b)
                v.wait_ge(self.dv.h, w)
                if wp is not None:
                    v.wait_ge(self.pe.h, wp)
                v.tensor_scalar(xnb.ap()[:, 0:TOK], b0.ap()[:, :], negm, rstd, ALU.add, ALU.mult)
                v.tensor_scalar(xnb.ap()[:, TOK:D], b1.ap()[:, :], negm, rstd,
                                ALU.add, ALU.mult).then_inc(self.dv.h, 1)
            t2 = dv.inc()
            self.v(apply)
            dst = [self.xnt.ap()[:, kt * TOK + qb * P: kt * TOK + (qb + 1) * P] for kt in range(NKT)]
            pec, dvc = self.emit_transpose16(xnb.ap(), dst, t2, evac="mixed",
                                             banks=((2, 3) if qb % 2 == 0 else (4, 5)))
            self._xnb_pe[p_] = pec
            xld[qb] = t2
            xnt_dv, xnt_ac = self.dv.n, self.ac.n
            def rmm(t, qb=qb, w=xnt_dv, wa=xnt_ac, wld=rw_ld,
                    fb=(self.bankfree.get(7) if qb == 0 else None)):
                t.wait_ge(self.dv.h, w)
                t.wait_ge(self.ac.h, wa)
                t.wait_ge(self.swa.h, wld)
                self.wait_bf(t, fb)
                for kt in range(NKT - 1):
                    t.matmul(self.pb[7][:, qb * 9: qb * 9 + 9],
                             self.xnt.ap()[:, kt * TOK + qb * P: kt * TOK + (qb + 1) * P],
                             self.rwsg_b.ap()[:, kt * 9: kt * 9 + 9],
                             start=(qb == 0 and kt == 0), stop=False)
                kt = NKT - 1
                t.matmul(self.pb[7][:, qb * 9: qb * 9 + 9],
                         self.xnt.ap()[:, kt * TOK + qb * P: kt * TOK + (qb + 1) * P],
                         self.rwsg_b.ap()[:, kt * 9: kt * 9 + 9],
                         start=False, stop=(qb == NQB - 1)).then_inc(self.pe.h, 1)
            tpe_r[qb] = pe.inc()
            self.t(rmm)
        self.p4a_apply_dv = self.dv.n
        pre_ac2 = ac.n
        def sigs(s, w=tpe_r[NQB - 1], pre=pre_ac2):
            s.wait_ge(self.pe.h, w)
            s.wait_ge(self.ac.h, pre)
            for qb in range(NQB):
                ins = s.activation(self.sig[:, qb:qb + 1],
                                   self.pb[7][:, qb * 9 + 8: qb * 9 + 9], AF.Sigmoid)
            ins.then_inc(self.ac.h, 1)
        t_sig = ac.inc()
        self.s(sigs)
        pre_dv2 = dv.n
        def rmx(v, w=tpe_r[NQB - 1], pre=pre_dv2):
            v.wait_ge(self.pe.h, w)
            v.wait_ge(self.dv.h, pre)
            for qb in range(NQB):
                ins = v.reduce_max(self.m1s[:, qb:qb + 1],
                                   self.pb[7][:, qb * 9: qb * 9 + 8], AX.X, negate=True)
            ins.then_inc(self.dv.h, 1)
        t_rmx = dv.inc()
        self.v(rmx)
        def exps(s, w=t_rmx, pre=t_sig):
            s.wait_ge(self.dv.h, w)
            s.wait_ge(self.ac.h, pre)
            for qb in range(NQB):
                ins = s.activation(self.probs_all[:, qb * 8:(qb + 1) * 8],
                                   self.pb[7][:, qb * 9: qb * 9 + 8],
                                   AF.Exp, bias=self.m1s[:, qb:qb + 1])
            ins.then_inc(self.ac.h, 1)
        t_exp = ac.inc()
        self.s(exps)
        self.bankfree[7] = ("scalar", t_exp)
        b2 = dv.n
        def top2b(v, w=t_exp, b=b2):
            v.wait_ge(self.ac.h, w)
            v.wait_ge(self.dv.h, b)
            for qb in range(NQB):
                ins = v.reduce_max(self.m1s[:, qb:qb + 1],
                                   self.probs_all[:, qb * 8:(qb + 1) * 8], AX.X)
            ins.then_inc(self.dv.h, 1)
            v.wait_ge(self.dv.h, b + 1)
            for qb in range(NQB):
                ins = v.tensor_scalar(self.mask_all[:, qb * 8:(qb + 1) * 8],
                                      self.probs_all[:, qb * 8:(qb + 1) * 8],
                                      self.m1s[:, qb:qb + 1], None, ALU.is_ge)
            ins.then_inc(self.dv.h, 1)
            v.wait_ge(self.dv.h, b + 2)
            v.scalar_tensor_tensor(self.mask_all[:, :], self.mask_all[:, :], -2.0,
                                   self.probs_all[:, :], ALU.mult, ALU.add).then_inc(self.dv.h, 1)
            v.wait_ge(self.dv.h, b + 3)
            for qb in range(NQB):
                ins = v.reduce_max(self.m2s[:, qb:qb + 1],
                                   self.mask_all[:, qb * 8:(qb + 1) * 8], AX.X)
            ins.then_inc(self.dv.h, 1)
            v.wait_ge(self.dv.h, b + 4)
            for qb in range(NQB):
                v.tensor_scalar(self.keep_all[:, qb * 8:(qb + 1) * 8],
                                self.probs_all[:, qb * 8:(qb + 1) * 8],
                                self.m2s[:, qb:qb + 1], None, ALU.is_ge)
            v.tensor_tensor(self.d8[:, :], self.m1s[:, :], self.m2s[:, :],
                            ALU.add).then_inc(self.dv.h, 1)
            v.wait_ge(self.dv.h, b + 5)
            v.reciprocal(self.r8[:, :], self.d8[:, :]).then_inc(self.dv.h, 1)
            v.wait_ge(self.dv.h, b + 6)
            for qb in range(NQB):
                ins = v.scalar_tensor_tensor(self.combine[:, qb * 8:(qb + 1) * 8],
                                             self.probs_all[:, qb * 8:(qb + 1) * 8],
                                             self.r8[:, qb:qb + 1],
                                             self.keep_all[:, qb * 8:(qb + 1) * 8],
                                             ALU.mult, ALU.mult)
            ins.then_inc(self.dv.h, 1)
        dv.inc(6)
        t_comb = dv.inc()
        self.v(top2b)
        self.ffnpre_dv = self.dv.n
        self.emit_dbg([("d_xnt", self.xnt.ap()[:, :]),
                       ("d_comb", self.combine[:, :]),
                       ("d_sig", self.sig[:, :])],
                      [("vector", self.dv.n), ("scalar", self.ac.n)])

    def emit_gu_pair(self, gbuf, ubuf, gsem, usem, gld, uld, dst_cols):
        pe, dv, ac = self.pe, self.dv, self.ac
        for ch in range(2):
            bga, bgb = 2 * ch, 2 * ch + 1
            def gu(t, ch=ch, bga=bga, bgb=bgb, gbuf=gbuf, ubuf=ubuf, gsem=gsem, usem=usem, gld=gld, uld=uld,
                   fa=self.bankfree.get(bga), fbb=self.bankfree.get(bgb)):
                t.wait_ge(gsem.h, gld)
                t.wait_ge(usem.h, uld)
                self.wait_bf(t, fa)
                self.wait_bf(t, fbb)
                for kt in range(NKT):
                    t.matmul(self.pb[bga][:, :], gbuf.ap()[:, kt * P:(kt + 1) * P],
                             self.xnt.ap()[:, kt * TOK + ch * 512: kt * TOK + ch * 512 + 512],
                             start=(kt == 0), stop=(kt == NKT - 1))
                for kt in range(NKT - 1):
                    t.matmul(self.pb[bgb][:, :], ubuf.ap()[:, kt * P:(kt + 1) * P],
                             self.xnt.ap()[:, kt * TOK + ch * 512: kt * TOK + ch * 512 + 512],
                             start=(kt == 0), stop=False)
                kt = NKT - 1
                t.matmul(self.pb[bgb][:, :], ubuf.ap()[:, kt * P:(kt + 1) * P],
                         self.xnt.ap()[:, kt * TOK + ch * 512: kt * TOK + ch * 512 + 512],
                         start=False, stop=True).then_inc(self.pe.h, 1)
            tpe = pe.inc()
            self.t(gu)
            def sil(s, ch=ch, bga=bga, w=tpe):
                s.wait_ge(self.pe.h, w)
                s.activation(self.silu_t[ch].ap()[:, :], self.pb[bga][:, :], AF.Silu).then_inc(self.ac.h, 1)
            tac = ac.inc()
            self.s(sil)
            self.bankfree[bga] = ("scalar", tac)
            def mul(v, ch=ch, bgb=bgb, dst=dst_cols[ch], w=tac):
                v.wait_ge(self.ac.h, w)
                v.scalar_tensor_tensor(dst, self.pb[bgb][:, :], 1.0, self.silu_t[ch].ap()[:, :],
                                       ALU.mult, ALU.mult).then_inc(self.dv.h, 1)
            tdv = dv.inc()
            self.v(mul)
            self.bankfree[bgb] = ("vector", tdv)

    def emit_down(self, act_buf, dsrc4_fn, scal_fn):
        pe, dv = self.pe, self.dv
        act_ready = self.dv.n
        NG = (IMT + 3) // 4
        units = [(dc, g) for dc in range(4) for g in range(NG)]
        base_u = self._dw_u
        lds = {}
        def emit_ld(u):
            dc, g = units[u]
            par = (base_u + u) % 4
            free = self.dwfree.get(par)
            waits = [] if free is None else [("tensor", free)]
            nk = min(4, IMT - 4 * g)
            lds[u] = self.dma("gpsimd", self.sdw[par],
                              self.dwb[par].ap().rearrange("p (k n) -> p k n", n=512)[:, 0:nk, :],
                              dsrc4_fn(g, dc), waits)
        DW_DEPTH = 3
        for u in range(DW_DEPTH):
            emit_ld(u)
        for dc in range(4):
            for half in range(2):
                qbs = range(4) if half == 0 else range(4, NQB)
                for g in range(NG):
                    u = dc * NG + g
                    par = (base_u + u) % 4
                    nk = min(4, IMT - 4 * g)
                    for ki in range(nk):
                        kt = 4 * g + ki
                        def dn(t, kt=kt, ki=ki, dc=dc, par=par, wld=lds[u], act_buf=act_buf,
                               qbs=tuple(qbs),
                               fbs=tuple(self.bankfree.get(b) for b in qbs) if kt == 0 else (),
                               war=act_ready if (kt == 0 and dc == 0 and half == 0) else None):
                            t.wait_ge(self.sdw[par].h, wld)
                            if war is not None:
                                t.wait_ge(self.dv.h, war)
                            for f in fbs:
                                self.wait_bf(t, f)
                            for qb in qbs:
                                ins = t.matmul(self.pb[qb][:, :],
                                               act_buf.ap()[:, kt * TOK + qb * P: kt * TOK + (qb + 1) * P],
                                               self.dwb[par].ap()[:, ki * 512:(ki + 1) * 512],
                                               start=(kt == 0), stop=(kt == IMT - 1))
                            ins.then_inc(self.pe.h, 1)
                        tpe = pe.inc()
                        self.t(dn)
                    if half == 1:
                        self.dwfree[par] = tpe
                        if u + DW_DEPTH < len(units):
                            emit_ld(u + DW_DEPTH)
                last_pe = self.pe.n
                for qb in qbs:
                    def dne(v, qb=qb, dc=dc, w=last_pe, sc=scal_fn(qb)):
                        v.wait_ge(self.pe.h, w)
                        off = qb * D + dc * 512
                        v.scalar_tensor_tensor(self.out_acc.ap()[:, off:off + 512], self.pb[qb][:, :],
                                               sc, self.out_acc.ap()[:, off:off + 512],
                                               ALU.mult, ALU.add).then_inc(self.dv.h, 1)
                    tdv = dv.inc()
                    self.v(dne)
                    self.bankfree[qb] = ("vector", tdv)
                    self.last_dne[qb] = tdv
        self._dw_u += len(units)

    def emit_shared(self):
        io = self.io
        self.dwfree = {}
        self.gwfree = {}
        gsrc = io["shgt"].rearrange("(kt p) n -> p kt n", p=P)
        usrc = io["shupt"].rearrange("(kt p) n -> p kt n", p=P)
        pre = self.p4a_apply_dv
        for q in range(4):
            for mt in range(IMT):
                i_col = q * IM + mt * P
                par = mt % 2
                free = self.gwfree.get(par)
                waits = [("vector", pre)] if free is None else [("tensor", free)]
                gld = self.dma("gpsimd", self.sgw[par],
                               self.gwb[par].ap().rearrange("p (kt n) -> p kt n", n=P),
                               gsrc[:, :, i_col:i_col + P], waits)
                uld = self.dma("gpsimd", self.suw[par],
                               self.uwb[par].ap().rearrange("p (kt n) -> p kt n", n=P),
                               usrc[:, :, i_col:i_col + P], waits)
                dst = [self.shact.ap()[:, mt * TOK + ch * 512: mt * TOK + ch * 512 + 512] for ch in range(2)]
                self.emit_gu_pair(self.gwb[par], self.uwb[par], self.sgw[par], self.suw[par], gld, uld, dst)
                self.gwfree[par] = self.pe.n
            def dsrc4(g, dc, q=q):
                nk = min(4, IMT - 4 * g)
                return io["shdt"][q * IM + 4 * g * P: q * IM + (4 * g + nk) * P,
                                  dc * 512:(dc + 1) * 512].rearrange("(k p) n -> p k n", p=P)
            self.emit_down(self.shact, dsrc4, lambda qb: self.sig[:, qb:qb + 1])
        self.emit_dbg([("d_shact", self.shact.ap()[:, :]),
                       ("d_accp4s", self.out_acc.ap()[:, :])], [("vector", self.dv.n)])

    def emit_moe(self):
        io = self.io
        for e in range(E_):
            gsrc = io[f"gupt{e}"].rearrange("(kt p) n -> p kt n", p=P)
            for mt in range(IMT):
                par = mt % 2
                free = self.gwfree.get(par)
                waits = [] if free is None else [("tensor", free)]
                gld = self.dma("gpsimd", self.sgw[par],
                               self.gwb[par].ap().rearrange("p (kt n) -> p kt n", n=P),
                               gsrc[:, :, mt * P:(mt + 1) * P], waits)
                uld = self.dma("gpsimd", self.suw[par],
                               self.uwb[par].ap().rearrange("p (kt n) -> p kt n", n=P),
                               gsrc[:, :, IM + mt * P: IM + (mt + 1) * P], waits)
                dst = [self.actt.ap()[:, mt * TOK + ch * 512: mt * TOK + ch * 512 + 512] for ch in range(2)]
                self.emit_gu_pair(self.gwb[par], self.uwb[par], self.sgw[par], self.suw[par], gld, uld, dst)
                self.gwfree[par] = self.pe.n
            def dsrc4(g, dc, e=e):
                nk = min(4, IMT - 4 * g)
                return io[f"downt{e}"][4 * g * P:(4 * g + nk) * P,
                                       dc * 512:(dc + 1) * 512].rearrange("(k p) n -> p k n", p=P)
            def scal(qb, e=e):
                return self.combine[:, qb * 8 + e: qb * 8 + e + 1]
            self.emit_down(self.actt, dsrc4, scal)

    def emit_store(self):
        if self.debug >= 8:
            def fin(sy, tv=self.so[3].n):
                sy.wait_ge(self.so[3].h, tv)
            self.y(fin)
            return
        osrc = self.out.rearrange("(qb p) d -> qb p d", p=P)
        for qb in range(NQB):
            sem = self.so[qb % 4]
            pre = sem.n
            sem.inc(16)
            def th(sy, qb=qb, pre=pre, sem=sem, w=self.last_dne[qb]):
                sy.wait_ge(self.dv.h, w)
                sy.wait_ge(sem.h, pre)
                sy.dma_start(osrc[qb], self.out_acc.ap()[:, qb * D:(qb + 1) * D]).then_inc(sem.h, 16)
            self.y(th)
        def fin(sy, targets=tuple((i, self.so[i].n) for i in range(4))):
            for i, tv in targets:
                sy.wait_ge(self.so[i].h, tv)
        self.y(fin)


def rope_host_tables():
    inv_freq = 1.0 / 10000.0 ** (np.arange(0, PD, 2, dtype=np.float32) / PD)
    t = np.arange(S, dtype=np.float32)
    freqs = t[:, None] * inv_freq[None, :]
    emb = np.concatenate([freqs, freqs], axis=-1)
    return (np.ascontiguousarray(np.cos(emb).T.astype(np.float32)),
            np.ascontiguousarray(np.sin(emb).T.astype(np.float32)))


def _pack_rwsg(inputs):
    rw = np.concatenate([np.asarray(inputs["L0_ffn_router_weight"], dtype=np.float32).T,
                         np.asarray(inputs["L0_ffn_shared_gate_w"], dtype=np.float32).T], axis=1)
    return np.ascontiguousarray(rw.reshape(NKT, P, 9).transpose(1, 0, 2).reshape(P, NKT * 9))


def make_in_maps(inputs):
    x = np.ascontiguousarray(np.asarray(inputs["L0_attn_hidden_states"], dtype=np.float32).reshape(S, D))
    cosk, sink = rope_host_tables()
    rtm = np.zeros((PD, PD), dtype=np.float32)
    for i in range(32):
        rtm[i + 32, i] = -1.0
        rtm[i, i + 32] = 1.0
    common = {
        "x_full": x,
        "wdkv": np.ascontiguousarray(inputs["L0_attn_W_DKV"], dtype=np.float32),
        "wdq": np.ascontiguousarray(inputs["L0_attn_W_DQ"], dtype=np.float32),
        "wkr": np.ascontiguousarray(inputs["L0_attn_W_KR"], dtype=np.float32),
        "wuk": np.ascontiguousarray(inputs["L0_attn_W_UK"], dtype=np.float32),
        "wuv": np.ascontiguousarray(inputs["L0_attn_W_UV"], dtype=np.float32),
        "wuq": np.ascontiguousarray(inputs["L0_attn_W_UQ"], dtype=np.float32),
        "wqr": np.ascontiguousarray(inputs["L0_attn_W_QR"], dtype=np.float32),
        "wo": np.ascontiguousarray(inputs["L0_attn_W_O"], dtype=np.float32),
        "cosk": cosk, "sink": sink,
        "rt": rtm,
        "ident": np.eye(P, dtype=np.float32),
        "rwsg": _pack_rwsg(inputs),
        "shgt": np.ascontiguousarray(np.asarray(inputs["L0_ffn_sh_gate_w"], dtype=np.float32).T),
        "shupt": np.ascontiguousarray(np.asarray(inputs["L0_ffn_sh_up_w"], dtype=np.float32).T),
        "shdt": np.ascontiguousarray(np.asarray(inputs["L0_ffn_sh_down_w"], dtype=np.float32).T),
    }
    gup = np.asarray(inputs["L0_ffn_gate_up_proj"], dtype=np.float32)
    dwn = np.asarray(inputs["L0_ffn_down_proj"], dtype=np.float32)
    for e in range(E_):
        common[f"gupt{e}"] = np.ascontiguousarray(gup[e].T)
        common[f"downt{e}"] = np.ascontiguousarray(dwn[e].T)
    in_maps = []
    for c in range(NCORES):
        m = dict(common)
        m["x_own"] = np.ascontiguousarray(x[c * TOK:(c + 1) * TOK])
        m["cosq"] = np.ascontiguousarray(cosk[:, c * TOK:(c + 1) * TOK])
        m["sinq"] = np.ascontiguousarray(sink[:, c * TOK:(c + 1) * TOK])
        in_maps.append(m)
    return in_maps


def kernel(**inputs):
    k = K(debug=0)
    nc = k.build()
    in_maps = make_in_maps(inputs)
    res = run_bass_kernel_spmd(nc, in_maps, core_ids=list(range(NCORES)))
    out = np.concatenate([res.results[c]["out"] for c in range(NCORES)], axis=0)
    return out.reshape(1, S, D).astype(np.float32)
